# version 03
# baseline (speedup 1.0000x reference)
import numpy as np
import ml_dtypes
from contextlib import ExitStack
import concourse.bass as bass
import concourse.mybir as mybir
from concourse.bass_utils import run_bass_kernel_spmd

F32 = mybir.dt.float32
BF16 = mybir.dt.bfloat16
AF = mybir.ActivationFunctionType
ALU = mybir.AluOpType
AX = mybir.AxisListType
BF = ml_dtypes.bfloat16

B = 4
T = 2048
D = 1024
NT = B * T
NCORES = 8


_CONSTS = None

def host_consts():
    global _CONSTS
    if _CONSTS is not None:
        return _CONSTS
    c = {}
    c["onesb"] = np.ones((128, 1), BF)
    c["ones1"] = np.ones((1, 128), BF)
    c["id128"] = np.eye(128, dtype=np.float32).astype(BF)
    idst = np.zeros((128, 32), np.float32)
    for m in range(4):
        idst[32*m:32*m+32, :] = np.eye(32, dtype=np.float32)
    c["idstack"] = idst
    c["maskL"] = np.tril(np.ones((128, 128), np.float32), -1)
    c["id128f"] = np.eye(128, dtype=np.float32)
    bd = np.zeros((128, 128), np.float32)
    for m in range(4):
        bd[32*m:32*m+32, 32*m:32*m+32] = 1.0
    c["bdmaskf"] = bd
    c["maskLd"] = c["maskL"] * bd
    c["maskLoff"] = c["maskL"] * (1 - bd)
    c["maskWuf"] = np.triu(np.ones((128, 128), np.float32))
    c["maskWu"] = np.triu(np.ones((128, 128), np.float32)).astype(BF)
    md = np.zeros((4, 128, 512), np.float32)
    for m in range(4):
        for p in range(128):
            md[m, p, 128*m + p:] = 1.0
    c["maskdiag"] = md.astype(BF)
    half = 16
    invf = 1.0 / 10000.0 ** (np.arange(half, dtype=np.float32) / half)
    ang = np.arange(T, dtype=np.float32)[:, None] * invf[None, :]
    cos1, sin1 = np.cos(ang).astype(np.float32), np.sin(ang).astype(np.float32)
    C1 = np.zeros((T, 512), np.float32)
    S1 = np.zeros((T, 512), np.float32)
    for h in range(16):
        C1[:, h*32:h*32+16] = cos1
        C1[:, h*32+16:h*32+32] = cos1
        S1[:, h*32:h*32+16] = -sin1
        S1[:, h*32+16:h*32+32] = sin1
    c["ropeC"] = C1
    c["ropeS"] = S1
    n_idx = np.arange(D)
    th = -2 * np.pi * np.outer(n_idx, n_idx) / D
    Fr = np.cos(th).astype(np.float32)
    Fi = np.sin(th).astype(np.float32)
    c["dftFr"] = Fr.astype(BF)
    c["dftFi"] = Fi.astype(BF)
    c["dftFrN"] = (Fr / D).astype(BF)
    c["dftFiN"] = (Fi / D).astype(BF)
    c["dftFiNeg"] = (-Fi).astype(BF)
    sel = np.zeros((16, 16 * 128), np.float32)
    for e_ in range(16):
        sel[e_, e_*128:(e_+1)*128] = 1.0
    c["sel16"] = sel.astype(BF)
    _CONSTS = c
    return c


def bc_mid(ap2, reps):
    return bass.AP(tensor=ap2.tensor, offset=ap2.offset,
                   ap=[list(ap2.ap[0]), [0, reps], list(ap2.ap[1])])

def fm_slice(dram_ap, t0, w):
    return dram_ap[:, t0:t0+w].rearrange("(c p) t -> p c t", p=128)


class Eng:
    _OPS = {"tensor_copy", "tensor_tensor", "tensor_add", "tensor_sub", "tensor_mul",
            "tensor_max", "tensor_scalar", "tensor_single_scalar", "tensor_scalar_add",
            "tensor_scalar_sub", "tensor_scalar_mul", "tensor_scalar_max", "tensor_scalar_min",
            "tensor_reduce", "reduce_max", "reduce_sum", "scalar_tensor_tensor", "memset",
            "memzero", "reciprocal", "select", "activation", "copy", "mul", "add", "sqrt",
            "square", "sign", "iota", "transpose", "stream_shuffle", "copy_predicated",
            "partition_broadcast", "partition_all_reduce", "tensor_tensor_scan", "bn_stats",
            "bn_aggr", "affine_select"}

    def __init__(self, bld, eng, sem, name, selfsync=True):
        self.b = bld
        self.e = eng
        self.sem = sem
        self.name = name
        self.n = 0
        self.selfsync = selfsync

    def __getattr__(self, attr):
        fn = getattr(self.e, attr)
        if attr not in Eng._OPS:
            return fn
        def call(*a, **kw):
            if self.selfsync and self.n > 0:
                self.e.wait_ge(self.sem, self.n)
            ins = fn(*a, **kw)
            self.n += 1
            ins.then_inc(self.sem, 1)
            return ins
        return call

    def raw(self, attr, *a, **kw):
        return getattr(self.e, attr)(*a, **kw)

    def inc(self, ins):
        self.n += 1
        ins.then_inc(self.sem, 1)
        return ins

    def selfwait(self):
        if self.n > 0:
            self.e.wait_ge(self.sem, self.n)

    def mark(self, key):
        self.b.record[self.b.key(key)] = (self.name, self.n)

    def wait(self, key):
        nm, n = self.b.marks.get(self.b.key(key), (None, 0))
        if nm is None:
            return
        if n > 0:
            self.e.wait_ge(self.b.sems[nm], n)

    def dma(self, out, in_, stream, **kw):
        if self.selfsync and self.n > 0:
            self.e.wait_ge(self.sem, self.n)
        ins = self.e.dma_start(out=out, in_=in_, **kw)
        sem, cnt = stream.sem, stream.bump()
        ins.then_inc(sem, 16)
        return ins

    def wait_stream(self, stream, k):
        if hasattr(stream, "absn"):
            k = stream.absn(k)
        self.e.wait_ge(stream.sem, 16 * k)


class Stream:
    def __init__(self, sem):
        self.sem = sem
        self.n = 0
    def bump(self):
        self.n += 1
        return self.n


class StreamView:
    def __init__(self, s):
        self.s = s
        self.base = s.n
    @property
    def sem(self):
        return self.s.sem
    @property
    def n(self):
        return self.s.n - self.base
    def bump(self):
        return self.s.bump()
    def absn(self, k):
        return self.base + k


class Builder:
    def __init__(self, marks=None):
        self.marks = marks if marks is not None else {}
        self.record = {}
        self.sems = {}
        self.k = 0
        self.prefix = ""
        self.streampool = {}
        self.gstack = ExitStack()
        self.made = False
        self.nbar = 0

    def key(self, k):
        return (self.prefix,) + (k if isinstance(k, tuple) else (k,))

    def step(self, eng):
        if self.k > 0:
            eng.wait(("ch", self.k))
        return eng

    def end(self, eng):
        self.k += 1
        eng.mark(("ch", self.k))

    def make(self, nc, stk):
        if self.made:
            return self
        e = self.gstack.enter_context
        self.sems = {nm: e(nc.semaphore(nm)) for nm in ("SEV", "SES", "SEG", "SET", "SEQ", "BAR")}
        self.V = Eng(self, nc.vector, self.sems["SEV"], "SEV")
        self.S = Eng(self, nc.scalar, self.sems["SES"], "SES")
        self.G = Eng(self, nc.gpsimd, self.sems["SEG"], "SEG")
        self.Tn = Eng(self, nc.tensor, self.sems["SET"], "SET", selfsync=False)
        self.Q = Eng(self, nc.sync, self.sems["SEQ"], "SEQ", selfsync=False)
        self.made = True
        return self

    def stream(self, nc, stk, name):
        if name not in self.streampool:
            sem = self.gstack.enter_context(nc.semaphore("st_" + name))
            self.streampool[name] = Stream(sem)
        return StreamView(self.streampool[name])


def build_twice(fn):
    b1 = Builder()
    fn(b1)
    rec = b1.record
    prefs = []
    for kk in rec:
        if len(kk) >= 2 and kk[-2] == "ch" and kk[0] not in prefs:
            prefs.append(kk[0])
    last = {}
    for pf in prefs:
        prev_eff = None
        k = 1
        while (pf, "ch", k) in rec:
            nm, n = rec[(pf, "ch", k)]
            if last.get(nm) == n:
                rec[(pf, "ch", k)] = prev_eff if prev_eff is not None else (nm, 0)
            else:
                last[nm] = n
                prev_eff = (nm, n)
            k += 1
    b2 = Builder(marks=rec)
    nc = fn(b2)
    return nc




def _launch_A(b):
    nc = _mknc()
    x = _dram(nc, "x_fm", [D, NT], F32, "ExternalInput")
    lnw = _dram(nc, "lnw", [128, 8], F32, "ExternalInput")
    lnb = _dram(nc, "lnb", [128, 8], F32, "ExternalInput")
    onesb = _dram(nc, "onesb", [128, 1], BF16, "ExternalInput")
    ones1 = _dram(nc, "ones1", [1, 128], BF16, "ExternalInput")
    xn_f = _dram(nc, "xn_f", [D, NT], F32, "ExternalOutput")
    xn_b = _dram(nc, "xn_b", [D, NT], BF16, "ExternalOutput")
    nrm_b = _dram(nc, "nrm_b", [D, NT], BF16, "ExternalOutput")
    W = 512
    NTILE = NT // W
    with ExitStack() as stk:
        e = stk.enter_context
        b.make(nc, stk)
        xt = [e(_sbuf(nc, f"xt{j}", [128, 8, W], F32)) for j in range(2)]
        xb = [e(_sbuf(nc, f"xb{j}", [128, 8, W], BF16)) for j in range(2)]
        xsq = [e(_sbuf(nc, f"xsq{j}", [128, 8, W], BF16)) for j in range(2)]
        xnt = [e(_sbuf(nc, f"xnt{j}", [128, 8, W], F32)) for j in range(2)]
        xnb = [e(_sbuf(nc, f"xnb{j}", [128, 8, W], BF16)) for j in range(2)]
        nrmt = [e(_sbuf(nc, f"nrmt{j}", [128, 8, W], BF16)) for j in range(2)]
        tmp = [e(_sbuf(nc, f"tmp{j}", [128, 8, W], F32)) for j in range(2)]
        rows = [e(_sbuf(nc, f"rows{j}", [1, 4, W], F32)) for j in range(2)]
        rows_b = [e(_sbuf(nc, f"rowsb{j}", [1, 4, W], BF16)) for j in range(2)]
        lnw_sb = e(_sbuf(nc, "lnw_sb", [128, 8], F32))
        lnb_sb = e(_sbuf(nc, "lnb_sb", [128, 8], F32))
        ones_sb = e(_sbuf(nc, "ones_sb", [128, 1], BF16))
        ones1_sb = e(_sbuf(nc, "ones1_sb", [1, 128], BF16))
        ps_mu = e(_psum(nc, "ps_mu", [1, W], F32))
        ps_s2 = e(_psum(nc, "ps_s2", [1, W], F32))
        ps_mu2 = e(_psum(nc, "ps_mu2", [1, W], F32))
        ps_s22 = e(_psum(nc, "ps_s22", [1, W], F32))
        ps_b1 = e(_psum(nc, "ps_b1", [128, W], F32))
        ps_b2 = e(_psum(nc, "ps_b2", [128, W], F32))
        ps_b3 = e(_psum(nc, "ps_b3", [128, W], F32))
        ps_b4 = e(_psum(nc, "ps_b4", [128, W], F32))
        LDs = [b.stream(nc, stk, f"LD{j}") for j in range(2)]
        CAs = [b.stream(nc, stk, f"CA{j}") for j in range(2)]
        SXs = [b.stream(nc, stk, f"SX{j}") for j in range(2)]
        SBs = [b.stream(nc, stk, f"SB{j}") for j in range(2)]
        SNs = [b.stream(nc, stk, f"SN{j}") for j in range(2)]
        L1s = b.stream(nc, stk, "L1")
        blk = e(nc.Block())

        @blk.sync
        def _(q):
            Q = b.Q
            for i in range(NTILE):
                j = i % 2
                if i >= 2:
                    Q.wait(("sq1", i - 2))
                    Q.wait(("apply1", i - 2))
                Q.dma(xt[j][:], fm_slice(x, i*W, W), LDs[j])

        @blk.gpsimd
        def _(g):
            G = b.G
            for i in range(NTILE):
                j = i % 2
                if i >= 2:
                    G.wait(("stats1", i - 2))
                G.dma(xb[j][:], fm_slice(x, i*W, W), CAs[j])
                G.wait(("apply1", i))
                if i >= 2:
                    G.wait(("stats2", i - 2))
                    G.wait_stream(SBs[j], i // 2)
                G.tensor_copy(xnb[j][:], xnt[j][:])
                G.mark(("cast", i))
                G.dma(fm_slice(xn_f, i*W, W), xnt[j][:], SXs[j])
                G.dma(fm_slice(xn_b, i*W, W), xnb[j][:], SBs[j])
                G.wait(("apply2", i))
                G.dma(fm_slice(nrm_b, i*W, W), nrmt[j][:], SNs[j])
            for j in range(2):
                G.wait_stream(SXs[j], NTILE // 2)
                G.wait_stream(SBs[j], NTILE // 2)
                G.wait_stream(SNs[j], NTILE // 2)

        @blk.scalar
        def _(s):
            S = b.S
            S.dma(lnw_sb[:], lnw, L1s)
            S.dma(lnb_sb[:], lnb, L1s)
            S.dma(ones_sb[:], onesb, L1s)
            S.dma(ones1_sb[:], ones1, L1s)
            for i in range(NTILE):
                j = i % 2
                S.wait_stream(LDs[j], i // 2 + 1)
                if i >= 2:
                    S.wait(("stats2", i - 2))
                S.activation(xsq[j][:].rearrange("p c t -> p (c t)"),
                             xt[j][:].rearrange("p c t -> p (c t)"), AF.Square)
                S.mark(("sq1", i))
                S.wait(("rows1a", i))
                S.activation(rows[j][:, 1, :], rows[j][:, 1, :], AF.Ln)
                S.activation(rows_b[j][:, 0, :], rows[j][:, 1, :], AF.Exp, scale=-0.5)
                S.mark(("rstd1", i))
                S.wait(("apply1", i))
                S.activation(xsq[j][:].rearrange("p c t -> p (c t)"),
                             xnt[j][:].rearrange("p c t -> p (c t)"), AF.Square)
                S.mark(("sq2", i))
                S.wait(("rows2a", i))
                S.activation(rows[j][:, 3, :], rows[j][:, 3, :], AF.Ln)
                S.activation(rows_b[j][:, 2, :], rows[j][:, 3, :], AF.Exp, scale=-0.5)
                S.mark(("rstd2", i))

        @blk.tensor
        def _(t):
            Tn = b.Tn
            Tn.wait_stream(L1s, 4)
            for i in range(NTILE):
                j = i % 2
                Tn.wait_stream(CAs[j], i // 2 + 1)
                Tn.wait(("sq1", i))
                if i >= 1:
                    Tn.wait(("rows1a", i - 1))
                for c in range(8):
                    ins = Tn.raw("matmul", ps_mu[:], ones_sb[:], xb[j][:, c, :],
                                 start=(c == 0), stop=(c == 7))
                Tn.inc(ins)
                for c in range(8):
                    ins = Tn.raw("matmul", ps_s2[:], ones_sb[:], xsq[j][:, c, :],
                                 start=(c == 0), stop=(c == 7))
                Tn.inc(ins)
                Tn.mark(("stats1", i))
                Tn.wait(("w1b", i))
                if i >= 1:
                    Tn.wait(("apply1", i - 1))
                Tn.inc(Tn.raw("matmul", ps_b1[:], ones1_sb[:], rows_b[j][:, 0, :], start=True, stop=True))
                Tn.inc(Tn.raw("matmul", ps_b2[:], ones1_sb[:], rows_b[j][:, 1, :], start=True, stop=True))
                Tn.mark(("bcast1", i))
                Tn.wait(("cast", i))
                Tn.wait(("sq2", i))
                if i >= 1:
                    Tn.wait(("rows2a", i - 1))
                for c in range(8):
                    ins = Tn.raw("matmul", ps_mu2[:], ones_sb[:], xnb[j][:, c, :],
                                 start=(c == 0), stop=(c == 7))
                Tn.inc(ins)
                for c in range(8):
                    ins = Tn.raw("matmul", ps_s22[:], ones_sb[:], xsq[j][:, c, :],
                                 start=(c == 0), stop=(c == 7))
                Tn.inc(ins)
                Tn.mark(("stats2", i))
                Tn.wait(("w2b", i))
                if i >= 1:
                    Tn.wait(("apply2", i - 1))
                Tn.inc(Tn.raw("matmul", ps_b3[:], ones1_sb[:], rows_b[j][:, 2, :], start=True, stop=True))
                Tn.inc(Tn.raw("matmul", ps_b4[:], ones1_sb[:], rows_b[j][:, 3, :], start=True, stop=True))
                Tn.mark(("bcast2", i))

        @blk.vector
        def _(v):
            V = b.V
            V.wait_stream(L1s, 4)
            for i in range(NTILE):
                j = i % 2
                V.wait(("stats1", i))
                V.tensor_scalar(rows[j][:, 0, :], ps_mu[:], 1.0 / D, None, op0=ALU.mult)
                V.tensor_tensor(rows[j][:, 1, :], rows[j][:, 0, :], rows[j][:, 0, :], op=ALU.mult)
                V.tensor_scalar(rows[j][:, 1, :], rows[j][:, 1, :], -1.0, 1e-5, op0=ALU.mult, op1=ALU.add)
                V.scalar_tensor_tensor(rows[j][:, 1, :], ps_s2[:], 1.0 / D, rows[j][:, 1, :],
                                       op0=ALU.mult, op1=ALU.add)
                V.mark(("rows1a", i))
                V.wait(("rstd1", i))
                V.tensor_tensor(rows_b[j][:, 1, :], rows[j][:, 0, :], rows_b[j][:, 0, :], op=ALU.mult)
                V.mark(("w1b", i))
                V.wait(("bcast1", i))
                if i >= 2:
                    V.wait_stream(SXs[j], i // 2)
                    V.wait(("cast", i - 2))
                    V.wait(("sq2", i - 2))
                V.tensor_tensor(xnt[j][:], xt[j][:], bc_mid(ps_b1[:], 8), op=ALU.mult)
                V.tensor_tensor(xnt[j][:], xnt[j][:], bc_mid(ps_b2[:], 8), op=ALU.subtract)
                V.mark(("apply1", i))
                V.wait(("stats2", i))
                V.tensor_scalar(rows[j][:, 2, :], ps_mu2[:], 1.0 / D, None, op0=ALU.mult)
                V.tensor_tensor(rows[j][:, 3, :], rows[j][:, 2, :], rows[j][:, 2, :], op=ALU.mult)
                V.tensor_scalar(rows[j][:, 3, :], rows[j][:, 3, :], -1.0, 1e-5, op0=ALU.mult, op1=ALU.add)
                V.scalar_tensor_tensor(rows[j][:, 3, :], ps_s22[:], 1.0 / D, rows[j][:, 3, :],
                                       op0=ALU.mult, op1=ALU.add)
                V.mark(("rows2a", i))
                V.wait(("rstd2", i))
                V.tensor_tensor(rows_b[j][:, 3, :], rows[j][:, 2, :], rows_b[j][:, 2, :], op=ALU.mult)
                V.mark(("w2b", i))
                V.wait(("bcast2", i))
                if i >= 2:
                    V.wait_stream(SNs[j], i // 2)
                V.tensor_tensor(tmp[j][:], xnt[j][:], bc_mid(ps_b3[:], 8), op=ALU.mult)
                V.tensor_tensor(tmp[j][:], tmp[j][:], bc_mid(ps_b4[:], 8), op=ALU.subtract)
                for c in range(8):
                    V.tensor_scalar(nrmt[j][:, c, :], tmp[j][:, c, :],
                                    lnw_sb[:, c:c+1], lnb_sb[:, c:c+1],
                                    op0=ALU.mult, op1=ALU.add)
                V.mark(("apply2", i))
    return nc


def build_launch_A():
    return build_twice(_launch_A)





_FUSED_NC = None
_DRAM_CACHE = {}
_PHASE = [""]

_FUSED_OUTPUTS = {"out_fm"}
_FUSED_INTERNAL = {"xn_f", "xn_b", "nrm_b", "qf", "kf", "kt", "vt", "beta_t", "o_n",
                   "s_fm", "cwT", "moe_fm", "x1_f", "xn1_f", "xn1_b", "q96", "k96",
                   "vaug", "attn", "s1_fm", "ffn1_fm", "mscr"}

def _mknc():
    if _FUSED_NC is not None:
        return _FUSED_NC
    return bass.Bass("TRN2", target_bir_lowering=False)

def _dram(nc, name, shape, dtype, kind):
    if _FUSED_NC is not None:
        if name in _DRAM_CACHE:
            return _DRAM_CACHE[name]
        k = "Internal" if name in _FUSED_INTERNAL else ("ExternalOutput" if name in _FUSED_OUTPUTS else "ExternalInput")
        t = nc.dram_tensor(name, shape, dtype, kind=k).ap()
        _DRAM_CACHE[name] = t
        return t
    return nc.dram_tensor(name, shape, dtype, kind=kind).ap()

def _sbuf(nc, name, shape, dtype):
    return nc.sbuf_tensor(_PHASE[0] + name, shape, dtype)

def _psum(nc, name, shape, dtype=F32):
    return nc.psum_tensor(_PHASE[0] + name, shape, dtype)

def _barrier(nc, bld):
    bld.nbar += 1
    k = bld.nbar
    with nc.Block() as blk:
        @blk.sync
        def _(q_):
            q_.sem_inc(bld.sems["BAR"], 1)
            q_.wait_ge(bld.sems["BAR"], 5 * k)
        @blk.gpsimd
        def _(g):
            g.drain()
            g.sem_inc(bld.sems["BAR"], 1)
            g.wait_ge(bld.sems["BAR"], 5 * k)
        @blk.tensor
        def _(t):
            t.drain()
            t.sem_inc(bld.sems["BAR"], 1)
            t.wait_ge(bld.sems["BAR"], 5 * k)
        @blk.scalar
        def _(s):
            s.drain()
            s.sem_inc(bld.sems["BAR"], 1)
            s.wait_ge(bld.sems["BAR"], 5 * k)
        @blk.vector
        def _(v):
            v.drain()
            v.sem_inc(bld.sems["BAR"], 1)
            v.wait_ge(bld.sems["BAR"], 5 * k)


def run_chain(b, steps, eng):
    for idx, (ename, fn) in enumerate(steps):
        if ename != ENG_OF[eng.name]:
            continue
        k = idx
        while k > 0 and b.marks.get(b.key(("ch", k)), ("x", 0))[1] == 0:
            k -= 1
        if k > 0:
            eng.wait(("ch", k))
        fn(eng)
        eng.mark(("ch", idx + 1))


ENG_OF = {"SEV": "V", "SES": "S", "SEG": "G", "SET": "Tn", "SEQ": "Q"}


import math

def _launch_B(b):
    nc = _mknc()
    nrm = _dram(nc, "nrm_b", [D, NT], BF16, "ExternalInput")
    Wq = _dram(nc, "Wq", [D, 1024], F32, "ExternalInput")
    Wk = _dram(nc, "Wk", [D, 1024], F32, "ExternalInput")
    Wv = _dram(nc, "Wv", [D, 2048], F32, "ExternalInput")
    Wb = _dram(nc, "Wb", [D, 4], F32, "ExternalInput")
    onesb = _dram(nc, "onesb", [128, 1], BF16, "ExternalInput")
    ones1 = _dram(nc, "ones1", [1, 128], BF16, "ExternalInput")
    id128 = _dram(nc, "id128", [128, 128], BF16, "ExternalInput")
    qf = _dram(nc, "qf", [B, 4, 256, T], BF16, "ExternalOutput")
    kf = _dram(nc, "kf", [B, 4, 256, T], BF16, "ExternalOutput")
    kt = _dram(nc, "kt", [B, 4, T, 256], BF16, "ExternalOutput")
    vt = _dram(nc, "vt", [NT, 2048], BF16, "ExternalOutput")
    beta_t = _dram(nc, "beta_t", [NT, 4], F32, "ExternalOutput")
    W = 512
    NTILE = NT // W
    with ExitStack() as stk:
        e = stk.enter_context
        b.make(nc, stk)
        nb = [e(_sbuf(nc, f"nb{j}", [128, 8, W], BF16)) for j in range(2)]
        Wq_sb = e(_sbuf(nc, "Wq_sb", [128, 8, 1024], BF16))
        Wk_sb = e(_sbuf(nc, "Wk_sb", [128, 8, 1024], BF16))
        Wv_sb = e(_sbuf(nc, "Wv_sb", [128, 8, 2048], BF16))
        Wb_sb = e(_sbuf(nc, "Wb_sb", [128, 8, 4], BF16))
        ones_sb = e(_sbuf(nc, "ones_sb", [128, 1], BF16))
        ones1_sb = e(_sbuf(nc, "ones1_sb", [1, 128], BF16))
        id_sb = e(_sbuf(nc, "id_sb", [128, 128], BF16))
        qsil = e(_sbuf(nc, "qsil", [128, 8, W], F32))
        qsq = e(_sbuf(nc, "qsq", [128, 8, W], BF16))
        qout = [e(_sbuf(nc, f"qout{j}", [128, 8, W], BF16)) for j in range(2)]
        kout = [e(_sbuf(nc, f"kout{j}", [128, 8, W], BF16)) for j in range(2)]
        ktt = [e(_sbuf(nc, f"ktt{j}", [128, 8, 4, 128], BF16)) for j in range(2)]
        vout = [e(_sbuf(nc, f"vout{j}", [128, 4, 4, W], BF16)) for j in range(2)]
        bout = [e(_sbuf(nc, f"bout{j}", [128, 4, 4], F32)) for j in range(2)]
        rowf = e(_sbuf(nc, "rowf", [1, 2048], F32))
        row_b = e(_sbuf(nc, "row_b", [1, 2048], BF16))
        psA = e(_psum(nc, "psA", [128, W], F32))
        ps_row = e(_psum(nc, "ps_row", [1, 2048], F32))
        ps_bc = [e(_psum(nc, f"ps_bc{jj}", [128, W], F32)) for jj in range(2)]
        ps_tp = e(_psum(nc, "ps_tp", [128, 4, 128], BF16))
        LDs = [b.stream(nc, stk, f"LD{j}") for j in range(2)]
        WTs = b.stream(nc, stk, "WT")
        SQs = [b.stream(nc, stk, f"SQ{j}") for j in range(2)]
        SKs = [b.stream(nc, stk, f"SK{j}") for j in range(2)]
        STs = [b.stream(nc, stk, f"STT{j}") for j in range(2)]
        SVs = [b.stream(nc, stk, f"SV{j}") for j in range(2)]
        SBs = [b.stream(nc, stk, f"SBB{j}") for j in range(2)]

        steps = []
        def add(ename, fn):
            steps.append((ename, fn))

        for i in range(NTILE):
            j = i % 2
            bb = i // 4
            tl = (i % 4) * W
            for wq, Wsb, outb, biasc, statk in ((1, Wq_sb, qout, -0.5*math.log(256.0), "q"),
                                                (0, Wk_sb, kout, 0.0, "k")):
                for fc in range(8):
                    def mmstep(Tn, j=j, i=i, Wsb=Wsb, fc=fc, wq=wq):
                        if fc == 0 and wq == 1:
                            Tn.wait_stream(WTs, 7)
                            Tn.wait_stream(LDs[j], i // 2 + 1)
                        for dc in range(8):
                            ins = Tn.raw("matmul", psA[:], Wsb[:, dc, fc*128:(fc+1)*128],
                                         nb[j][:, dc, :], start=(dc == 0), stop=(dc == 7))
                        Tn.inc(ins)
                    add("Tn", mmstep)
                    def silustep(S, fc=fc):
                        S.activation(qsil[:, fc, :], psA[:], AF.Silu)
                        S.activation(qsq[:, fc, :], qsil[:, fc, :], AF.Square)
                    add("S", silustep)
                def statstep(Tn):
                    for h in range(4):
                        for dkc in range(2):
                            ins = Tn.raw("matmul", ps_row[:, h*512:(h+1)*512], ones_sb[:],
                                         qsq[:, h*2+dkc, :], start=(dkc == 0), stop=(dkc == 1))
                    Tn.inc(ins)
                add("Tn", statstep)
                def roweps(V):
                    V.tensor_scalar(rowf[:], ps_row[:], 1e-6, None, op0=ALU.add)
                add("V", roweps)
                def rowstep(S):
                    S.activation(rowf[:], rowf[:], AF.Ln)
                    S.activation(row_b[:], rowf[:], AF.Exp, scale=-0.5)
                add("S", rowstep)
                for h in range(4):
                    def bcstep(Tn, h=h):
                        ins = Tn.raw("matmul", ps_bc[h % 2][:], ones1_sb[:],
                                     row_b[:, h*512:(h+1)*512], start=True, stop=True)
                        Tn.inc(ins)
                    add("Tn", bcstep)
                    def appstep(V, h=h, j=j, i=i, outb=outb, statk=statk, wq=wq):
                        if h == 0:
                            V.wait_stream((SQs if statk == "q" else SKs)[j], i // 2)
                        sc = 256.0 ** -0.5 if wq else 1.0
                        V.scalar_tensor_tensor(outb[j][:, 2*h, :], qsil[:, 2*h, :], sc,
                                               ps_bc[h % 2][:], op0=ALU.mult, op1=ALU.mult)
                        V.scalar_tensor_tensor(outb[j][:, 2*h+1, :], qsil[:, 2*h+1, :], sc,
                                               ps_bc[h % 2][:], op0=ALU.mult, op1=ALU.mult)
                    add("V", appstep)
                if wq:
                    def qstore(G, j=j, i=i, bb=bb, tl=tl):
                        G.dma(qf[bb].rearrange("h (c p) t -> p (h c) t", p=128)[:, :, tl:tl+W],
                              qout[j][:], SQs[j])
                    add("G", qstore)
                else:
                    def kstore(G, j=j, i=i, bb=bb, tl=tl):
                        G.dma(kf[bb].rearrange("h (c p) t -> p (h c) t", p=128)[:, :, tl:tl+W],
                              kout[j][:], SKs[j])
                    add("G", kstore)
            for fc in range(8):
                def tpstep(Tn, fc=fc, j=j, i=i):
                    if fc == 0:
                        Tn.wait_stream(STs[j], 8 * (i // 2))
                    for tt in range(4):
                        ins = Tn.raw("transpose", ps_tp[:, tt, :], kout[j][:, fc, tt*128:(tt+1)*128],
                                     id_sb[:])
                    Tn.inc(ins)
                add("Tn", tpstep)
                def tpcopy(V, fc=fc, j=j):
                    V.tensor_copy(ktt[j][:, fc, :, :], ps_tp[:])
                add("V", tpcopy)
            def ktstore(G, j=j, bb=bb, tl=tl):
                for fc in range(8):
                    h, dkc = fc // 2, fc % 2
                    G.dma(kt[bb][h, tl:tl+W, dkc*128:(dkc+1)*128].rearrange("(tt p) dd -> p tt dd", p=128),
                          ktt[j][:, fc, :, :], STs[j])
            add("G", ktstore)
            for gg in range(16):
                def vmm(Tn, gg=gg, j=j, i=i):
                    if gg == 0:
                        Tn.wait_stream(SVs[j], i // 2)
                    tt, vf = gg // 4, gg % 4
                    for dc in range(8):
                        ins = Tn.raw("matmul", psA[:], nb[j][:, dc, tt*128:(tt+1)*128],
                                     Wv_sb[:, dc, vf*512:(vf+1)*512], start=(dc == 0), stop=(dc == 7))
                    Tn.inc(ins)
                add("Tn", vmm)
                def vsil(S, gg=gg, j=j):
                    tt, vf = gg // 4, gg % 4
                    S.activation(vout[j][:, tt, vf, :], psA[:], AF.Silu)
                add("S", vsil)
            def vstore(G, j=j, i=i):
                G.dma(vt[i*W:(i+1)*W, :].rearrange("(tt p) f -> p tt f", p=128),
                      vout[j][:].rearrange("p tt vf t -> p tt (vf t)"), SVs[j])
            add("G", vstore)
            for tt in range(4):
                def bmm(Tn, tt=tt, j=j, i=i):
                    if tt == 0:
                        Tn.wait_stream(SBs[j], i // 2)
                    for dc in range(8):
                        ins = Tn.raw("matmul", ps_bc[tt % 2][:, 0:4], nb[j][:, dc, tt*128:(tt+1)*128],
                                     Wb_sb[:, dc, :], start=(dc == 0), stop=(dc == 7))
                    Tn.inc(ins)
                    if tt == 3:
                        Tn.mark(("nbfree", i))
                add("Tn", bmm)
                def bsig(S, tt=tt, j=j):
                    S.activation(bout[j][:, tt, :], ps_bc[tt % 2][:, 0:4], AF.Sigmoid)
                add("S", bsig)
            def bstore(G, j=j, i=i):
                G.dma(beta_t[i*W:(i+1)*W, :].rearrange("(tt p) h -> p tt h", p=128),
                      bout[j][:], SBs[j])
            add("G", bstore)

        blk = e(nc.Block())

        @blk.sync
        def _(q_):
            Q = b.Q
            for i in range(NTILE):
                j = i % 2
                if i >= 2:
                    Q.wait(("nbfree", i - 2))
                Q.dma(nb[j][:], fm_slice(nrm, i*W, W), LDs[j])

        @blk.gpsimd
        def _(g):
            G = b.G
            G.dma(Wq_sb[:], Wq.rearrange("(c p) f -> p c f", p=128), WTs)
            G.dma(Wk_sb[:], Wk.rearrange("(c p) f -> p c f", p=128), WTs)
            G.dma(Wv_sb[:], Wv.rearrange("(c p) f -> p c f", p=128), WTs)
            G.dma(Wb_sb[:], Wb.rearrange("(c p) f -> p c f", p=128), WTs)
            G.dma(ones_sb[:], onesb, WTs)
            G.dma(ones1_sb[:], ones1, WTs)
            G.dma(id_sb[:], id128, WTs)
            run_chain(b, steps, G)
            for j in range(2):
                for s_ in (SQs[j], SKs[j], STs[j], SVs[j], SBs[j]):
                    if s_.n:
                        G.wait_stream(s_, s_.n)

        @blk.tensor
        def _(t):
            run_chain(b, steps, b.Tn)

        @blk.scalar
        def _(s):
            run_chain(b, steps, b.S)

        @blk.vector
        def _(v):
            run_chain(b, steps, b.V)
    return nc


def build_launch_B():
    return build_twice(_launch_B)



def _launch_C(b):
    nc = _mknc()
    qf = _dram(nc, "qf", [B, 4, 256, T], BF16, "ExternalInput")
    kf = _dram(nc, "kf", [B, 4, 256, T], BF16, "ExternalInput")
    kt = _dram(nc, "kt", [B, 4, T, 256], BF16, "ExternalInput")
    vt = _dram(nc, "vt", [NT, 2048], BF16, "ExternalInput")
    beta_t = _dram(nc, "beta_t", [NT, 4], F32, "ExternalInput")
    onw = _dram(nc, "onw", [128, 4], F32, "ExternalInput")
    onesb = _dram(nc, "onesb", [128, 1], BF16, "ExternalInput")
    ones1 = _dram(nc, "ones1", [1, 128], BF16, "ExternalInput")
    id128 = _dram(nc, "id128", [128, 128], BF16, "ExternalInput")
    id128f = _dram(nc, "id128f", [128, 128], F32, "ExternalInput")
    maskLd = _dram(nc, "maskLd", [128, 128], F32, "ExternalInput")
    maskLoff = _dram(nc, "maskLoff", [128, 128], F32, "ExternalInput")
    maskWuf = _dram(nc, "maskWuf", [128, 128], F32, "ExternalInput")
    bdmaskf = _dram(nc, "bdmaskf", [128, 128], F32, "ExternalInput")
    o_n = _dram(nc, "o_n", [B, 4, 512, T], BF16, "ExternalOutput")

    NCH = T // 128
    with ExitStack() as stk:
        e = stk.enter_context
        b.make(nc, stk)
        qf_t = [e(_sbuf(nc, f"qf_t{j}", [128, 2, 128], BF16)) for j in range(2)]
        kf_t = [e(_sbuf(nc, f"kf_t{j}", [128, 2, 128], BF16)) for j in range(2)]
        kt_t = [e(_sbuf(nc, f"kt_t{j}", [128, 256], BF16)) for j in range(2)]
        v_t = [e(_sbuf(nc, f"v_t{j}", [128, 512], BF16)) for j in range(2)]
        bt_t = [e(_sbuf(nc, f"bt_t{j}", [128, 4], F32)) for j in range(2)]
        S_all = e(_sbuf(nc, "S_all", [128, 16, 2, 512], F32))
        S_bf = e(_sbuf(nc, "S_bf", [128, 2, 512], BF16))
        sol = {nm: e(_sbuf(nc, f"sol_{nm}", [128, 128], BF16))
               for nm in ("Ld", "Loff", "Ud", "U2", "L2", "U4", "L4", "U8", "L8",
                          "IpL2", "IpL4", "IpL8", "IpL16", "Vc", "Xd", "No", "E",
                          "ET", "ET2", "ET3", "Zdn", "ZT")}
        R_sb = e(_sbuf(nc, "R_sb", [128, 512], BF16))
        Rtmp = e(_sbuf(nc, "Rtmp", [128, 512], F32))
        D_sb = e(_sbuf(nc, "D_sb", [128, 512], BF16))
        W_sb = e(_sbuf(nc, "W_sb", [128, 128], BF16))
        o_f = e(_sbuf(nc, "o_f", [128, 512], F32))
        osq = e(_sbuf(nc, "osq", [128, 512], BF16))
        o_out = [e(_sbuf(nc, f"o_out{j}", [128, 4, 128], BF16)) for j in range(2)]
        rowo = e(_sbuf(nc, "rowo", [1, 128], F32))
        rowo_b = e(_sbuf(nc, "rowo_b", [1, 128], BF16))
        onw_sb = e(_sbuf(nc, "onw_sb", [128, 4], F32))
        cst = {}
        for nm, dt_ in (("onesb", BF16), ("ones1", BF16), ("id128", BF16), ("id128f", F32),
                        ("maskLd", F32), ("maskLoff", F32), ("maskWuf", F32), ("bdmaskf", F32)):
            shp = [128, 1] if nm == "onesb" else ([1, 128] if nm == "ones1" else [128, 128])
            cst[nm] = e(_sbuf(nc, f"cst_{nm}", shp, dt_))
        psA = e(_psum(nc, "psA", [128, 128], F32))
        psTP = e(_psum(nc, "psTP", [128, 128], BF16))
        psMM = e(_psum(nc, "psMM", [128, 128], F32))
        psW = e(_psum(nc, "psW", [128, 128], F32))
        psBig = e(_psum(nc, "psBig", [128, 512], F32))
        psS = e(_psum(nc, "psS", [128, 2, 512], F32))
        LDs = [b.stream(nc, stk, f"LD{j}") for j in range(2)]
        BTs = [b.stream(nc, stk, f"BT{j}") for j in range(2)]
        WTs = b.stream(nc, stk, "WT")
        SOs = [b.stream(nc, stk, f"SO{j}") for j in range(2)]

        steps = []
        def add(ename, fn):
            steps.append((ename, fn))

        ldcnt = {}
        nld = 0
        for ci in range(4 * 4 * NCH):
            nld += 4
            ldcnt[ci] = nld

        def inittp(Tn):
            Tn.inc(Tn.raw("transpose", psTP[:], cst["id128"][:], cst["id128"][:]))
        add("Tn", inittp)

        ci = -1
        for ct in range(NCH):
            for bb_ in range(B):
                for h in range(4):
                    ci += 1
                    j = ci % 2
                    jb = (ci // 4) % 2
                    bh = bb_ * 4 + h
                    first = (ct == 0)

                    def sbfcast(S, bh=bh):
                        S.copy(S_bf[:].rearrange("p c t -> p (c t)"),
                               S_all[:, bh, :, :].rearrange("p c t -> p (c t)"))
                    if not first:
                        add("S", sbfcast)

                    def amm(Tn, j=j, ci=ci, jb=jb, h=h):
                        Tn.wait_stream(LDs[j], (ldcnt[ci] + 1) // 2 if False else (ci // 2 + 1) * 4)
                        if h == 0:
                            Tn.wait_stream(BTs[jb], ci // 8 + 1)
                        for c in range(2):
                            ins = Tn.raw("matmul", psA[:], kf_t[j][:, c, :], kf_t[j][:, c, :],
                                         start=(c == 0), stop=(c == 1))
                        Tn.inc(ins)
                    add("Tn", amm)

                    def lbuild(V, j=j, jb=jb, h=h):
                        V.scalar_tensor_tensor(sol["Ld"][:], psA[:], bt_t[jb][:, h:h+1],
                                               cst["maskLd"][:], op0=ALU.mult, op1=ALU.mult)
                        V.scalar_tensor_tensor(sol["Loff"][:], psA[:], bt_t[jb][:, h:h+1],
                                               cst["maskLoff"][:], op0=ALU.mult, op1=ALU.mult)
                    add("V", lbuild)

                    def udtp(Tn):
                        for m in range(4):
                            ins = Tn.raw("transpose", psTP[32*m:32*m+32, 32*m:32*m+32],
                                         sol["Ld"][32*m:32*m+32, 32*m:32*m+32],
                                         cst["id128"][32*m:32*m+32, 32*m:32*m+32],
                                         tile_position=(32*m, 32*m))
                        Tn.inc(ins)
                    add("Tn", udtp)
                    def udev(V):
                        V.tensor_tensor(sol["Ud"][:], psTP[:], cst["bdmaskf"][:], op=ALU.mult)
                    add("V", udev)

                    for (dst_u, dst_l, src_l, src_u) in (("U2", "L2", "Ld", "Ud"),
                                                         ("U4", "L4", "L2", "U2"),
                                                         ("U8", "L8", "L4", "U4")):
                        def pu(Tn, src_l=src_l, src_u=src_u):
                            Tn.inc(Tn.raw("matmul", psMM[:], sol[src_l][:], sol[src_u][:],
                                          start=True, stop=True))
                        add("Tn", pu)
                        def puev(V, dst_u=dst_u):
                            V.tensor_copy(sol[dst_u][:], psMM[:])
                        add("V", puev)
                        def pl(Tn, src_l=src_l, src_u=src_u):
                            Tn.inc(Tn.raw("matmul", psMM[:], sol[src_u][:], sol[src_l][:],
                                          start=True, stop=True))
                        add("Tn", pl)
                        def plev(V, dst_l=dst_l):
                            V.tensor_copy(sol[dst_l][:], psMM[:])
                            V.tensor_tensor(sol["IpL" + dst_l[1:]][:], psMM[:], cst["id128f"][:],
                                            op=ALU.add)
                        add("V", plev)
                    def l16(Tn):
                        Tn.inc(Tn.raw("matmul", psMM[:], sol["U8"][:], sol["L8"][:],
                                      start=True, stop=True))
                    add("Tn", l16)
                    def l16ev(V):
                        V.tensor_tensor(sol["IpL16"][:], psMM[:], cst["id128f"][:], op=ALU.add)
                        V.scalar_tensor_tensor(sol["Vc"][:], sol["Ud"][:], -1.0, cst["id128f"][:],
                                               op0=ALU.mult, op1=ALU.add)
                    add("V", l16ev)
                    for p in ("IpL2", "IpL4", "IpL8", "IpL16"):
                        def zc(Tn, p=p):
                            Tn.inc(Tn.raw("matmul", psMM[:], sol[p][:], sol["Vc"][:],
                                          start=True, stop=True))
                        add("Tn", zc)
                        def zcev(V):
                            V.tensor_copy(sol["Vc"][:], psMM[:])
                        add("V", zcev)
                    def xdtp(Tn):
                        for m in range(4):
                            ins = Tn.raw("transpose", psTP[32*m:32*m+32, 32*m:32*m+32],
                                         sol["Vc"][32*m:32*m+32, 32*m:32*m+32],
                                         cst["id128"][32*m:32*m+32, 32*m:32*m+32],
                                         tile_position=(32*m, 32*m))
                        Tn.inc(ins)
                    add("Tn", xdtp)
                    def xdev(V):
                        V.tensor_tensor(sol["Xd"][:], psTP[:], cst["bdmaskf"][:], op=ALU.mult)
                    add("V", xdev)
                    def notp(Tn):
                        Tn.inc(Tn.raw("transpose", psTP[:], sol["Loff"][:], cst["id128"][:]))
                    add("Tn", notp)
                    def noev(V):
                        V.tensor_copy(sol["No"][:], psTP[:])
                    add("V", noev)
                    def emm(Tn):
                        Tn.inc(Tn.raw("matmul", psMM[:], sol["Xd"][:], sol["No"][:],
                                      start=True, stop=True))
                    add("Tn", emm)
                    def eev(V):
                        V.tensor_copy(sol["E"][:], psMM[:])
                    add("V", eev)
                    def ettp(Tn):
                        Tn.inc(Tn.raw("transpose", psTP[:], sol["E"][:], cst["id128"][:]))
                    add("Tn", ettp)
                    def etev(V):
                        V.tensor_copy(sol["ET"][:], psTP[:])
                    add("V", etev)
                    def et2mm(Tn):
                        Tn.inc(Tn.raw("matmul", psMM[:], sol["E"][:], sol["ET"][:],
                                      start=True, stop=True))
                    add("Tn", et2mm)
                    def et2ev(V):
                        V.tensor_copy(sol["ET2"][:], psMM[:])
                    add("V", et2ev)
                    def et3mm(Tn):
                        Tn.inc(Tn.raw("matmul", psMM[:], sol["E"][:], sol["ET2"][:],
                                      start=True, stop=True))
                    add("Tn", et3mm)
                    def et3ev(V):
                        V.tensor_copy(sol["ET3"][:], psMM[:])
                        V.tensor_scalar(sol["Zdn"][:], sol["Vc"][:], -1.0, None, op0=ALU.mult)
                    add("V", et3ev)
                    def ztmm(Tn):
                        Tn.raw("matmul", psMM[:], cst["id128"][:], sol["Vc"][:], start=True, stop=False)
                        Tn.raw("matmul", psMM[:], sol["ET"][:], sol["Zdn"][:], start=False, stop=False)
                        Tn.raw("matmul", psMM[:], sol["ET2"][:], sol["Vc"][:], start=False, stop=False)
                        Tn.inc(Tn.raw("matmul", psMM[:], sol["ET3"][:], sol["Zdn"][:],
                                      start=False, stop=True))
                    add("Tn", ztmm)
                    def ztev(V):
                        V.tensor_copy(sol["ZT"][:], psMM[:])
                    add("V", ztev)

                    if not first:
                        def ksmm(Tn, j=j):
                            for c in range(2):
                                ins = Tn.raw("matmul", psBig[:], kf_t[j][:, c, :], S_bf[:, c, :],
                                             start=(c == 0), stop=(c == 1))
                            Tn.inc(ins)
                        add("Tn", ksmm)
                        def rv2(V, j=j, jb=jb, h=h):
                            V.tensor_tensor(Rtmp[:], v_t[j][:], psBig[:], op=ALU.subtract)
                            V.tensor_scalar(R_sb[:], Rtmp[:], bt_t[jb][:, h:h+1], None, op0=ALU.mult)
                        add("V", rv2)
                    else:
                        def rv1(V, j=j, jb=jb, h=h):
                            V.tensor_scalar(R_sb[:], v_t[j][:], bt_t[jb][:, h:h+1], None, op0=ALU.mult)
                        add("V", rv1)
                    def dmm(Tn):
                        Tn.inc(Tn.raw("matmul", psBig[:], sol["ZT"][:], R_sb[:],
                                      start=True, stop=True))
                    add("Tn", dmm)
                    def dev(S):
                        S.copy(D_sb[:], psBig[:])
                    add("S", dev)
                    def wmm(Tn, j=j):
                        for c in range(2):
                            ins = Tn.raw("matmul", psW[:], kf_t[j][:, c, :], qf_t[j][:, c, :],
                                         start=(c == 0), stop=(c == 1))
                        Tn.inc(ins)
                    add("Tn", wmm)
                    def wev(V):
                        V.tensor_tensor(W_sb[:], psW[:], cst["maskWuf"][:], op=ALU.mult)
                    add("V", wev)
                    def omm(Tn, j=j, first=first):
                        for dvc in range(4):
                            st_ = True
                            if not first:
                                for c in range(2):
                                    Tn.raw("matmul", psBig[:, dvc*128:(dvc+1)*128],
                                           S_bf[:, c, dvc*128:(dvc+1)*128], qf_t[j][:, c, :],
                                           start=st_, stop=False)
                                    st_ = False
                            ins = Tn.raw("matmul", psBig[:, dvc*128:(dvc+1)*128],
                                         D_sb[:, dvc*128:(dvc+1)*128], W_sb[:],
                                         start=st_, stop=True)
                        Tn.inc(ins)
                    add("Tn", omm)
                    def oev(S):
                        S.copy(o_f[:], psBig[:])
                        S.activation(osq[:], o_f[:], AF.Square)
                    add("S", oev)
                    def onorm_mm(Tn):
                        for dvc in range(4):
                            ins = Tn.raw("matmul", psA[0:1, 0:128], cst["onesb"][:],
                                         osq[:, dvc*128:(dvc+1)*128],
                                         start=(dvc == 0), stop=(dvc == 3))
                        Tn.inc(ins)
                    add("Tn", onorm_mm)
                    def onorm_row(V):
                        V.tensor_scalar(rowo[:], psA[0:1, 0:128], 1.0 / 512, 1e-5,
                                        op0=ALU.mult, op1=ALU.add)
                    add("V", onorm_row)
                    def onorm_rowS(S):
                        S.activation(rowo[:], rowo[:], AF.Ln)
                        S.activation(rowo_b[:], rowo[:], AF.Exp, scale=-0.5)
                    add("S", onorm_rowS)
                    def onorm_bc(Tn):
                        Tn.inc(Tn.raw("matmul", psW[:], cst["ones1"][:], rowo_b[:],
                                      start=True, stop=True))
                    add("Tn", onorm_bc)
                    def oapply(V, ci=ci, j=j):
                        V.wait_stream(SOs[ci % 2], ci // 2)
                        for dvc in range(4):
                            V.scalar_tensor_tensor(o_out[ci % 2][:, dvc, :],
                                                   o_f[:, dvc*128:(dvc+1)*128],
                                                   onw_sb[:, dvc:dvc+1], psW[:],
                                                   op0=ALU.mult, op1=ALU.mult)
                    add("V", oapply)
                    def ostore(G, ci=ci, bb_=bb_, h=h, ct=ct):
                        G.dma(o_n[bb_, h, :, ct*128:(ct+1)*128].rearrange("(c p) t -> p c t", p=128),
                              o_out[ci % 2][:], SOs[ci % 2])
                    add("G", ostore)
                    def smm(Tn, j=j, ci=ci):
                        for dkc in range(2):
                            ins = Tn.raw("matmul", psS[:, dkc, :], kt_t[j][:, dkc*128:(dkc+1)*128],
                                         D_sb[:], start=True, stop=True)
                        Tn.inc(ins)
                        Tn.mark(("chfree", ci))
                    add("Tn", smm)
                    def supd(V, bh=bh, first=first):
                        if first:
                            V.tensor_copy(S_all[:, bh, :, :].rearrange("p c t -> p (c t)"),
                                          psS[:].rearrange("p c t -> p (c t)"))
                        else:
                            for c in range(2):
                                V.tensor_tensor(S_all[:, bh, c, :], S_all[:, bh, c, :],
                                                psS[:, c, :], op=ALU.add)
                    add("V", supd)

        blk = e(nc.Block())

        @blk.sync
        def _(q_):
            Q = b.Q
            ci = -1
            for ct in range(NCH):
                for bb_ in range(B):
                    for h in range(4):
                        ci += 1
                        j = ci % 2
                        if ci >= 2:
                            Q.wait(("chfree", ci - 2))
                        if h == 0:
                            jb = (ci // 4) % 2
                            if ci >= 5:
                                Q.wait(("chfree", ci - 5))
                            Q.dma(bt_t[jb][:], beta_t[bb_*T+ct*128: bb_*T+ct*128+128, :], BTs[jb])
                        Q.dma(qf_t[j][:], qf[bb_, h, :, ct*128:(ct+1)*128].rearrange("(c p) t -> p c t", p=128), LDs[j])
                        Q.dma(kf_t[j][:], kf[bb_, h, :, ct*128:(ct+1)*128].rearrange("(c p) t -> p c t", p=128), LDs[j])
                        Q.dma(kt_t[j][:], kt[bb_, h, ct*128:(ct+1)*128, :], LDs[j])
                        Q.dma(v_t[j][:], vt[bb_*T+ct*128: bb_*T+ct*128+128, h*512:(h+1)*512], LDs[j])

        @blk.gpsimd
        def _(g):
            G = b.G
            G.dma(onw_sb[:], onw, WTs)
            for nm, dr in (("onesb", onesb), ("ones1", ones1), ("id128", id128),
                           ("id128f", id128f), ("maskLd", maskLd), ("maskLoff", maskLoff),
                           ("maskWuf", maskWuf), ("bdmaskf", bdmaskf)):
                G.dma(cst[nm][:], dr, WTs)
            run_chain(b, steps, G)
            for j in range(2):
                if SOs[j].n:
                    G.wait_stream(SOs[j], SOs[j].n)

        @blk.tensor
        def _(t):
            b.Tn.wait_stream(WTs, 9)
            run_chain(b, steps, b.Tn)

        @blk.scalar
        def _(s):
            run_chain(b, steps, b.S)

        @blk.vector
        def _(v):
            b.V.wait_stream(WTs, 9)
            run_chain(b, steps, b.V)
    return nc


def build_launch_C():
    return build_twice(_launch_C)



def _launch_D(b):
    nc = _mknc()
    o_n = _dram(nc, "o_n", [B, 4, 512, T], BF16, "ExternalInput")
    Wo = _dram(nc, "Wo", [2048, 1024], F32, "ExternalInput")
    xn_f = _dram(nc, "xn_f", [D, NT], F32, "ExternalInput")
    s_fm = _dram(nc, "s_fm", [D, NT], F32, "ExternalOutput")
    W = 512
    NTILE = NT // W
    with ExitStack() as stk:
        e = stk.enter_context
        b.make(nc, stk)
        ot = [e(_sbuf(nc, f"ot{j}", [128, 16, W], BF16)) for j in range(2)]
        xnt = [e(_sbuf(nc, f"xnt{j}", [128, 8, W], F32)) for j in range(2)]
        st_ = [e(_sbuf(nc, f"st{j}", [128, 8, W], F32)) for j in range(2)]
        Wo_sb = e(_sbuf(nc, "Wo_sb", [128, 16, 1024], BF16))
        psA = e(_psum(nc, "psA", [128, W], F32))
        psB = e(_psum(nc, "psB", [128, W], F32))
        LDs = [b.stream(nc, stk, f"LD{j}") for j in range(2)]
        XNs = [b.stream(nc, stk, f"XN{j}") for j in range(2)]
        WTs = b.stream(nc, stk, "WT")
        SSs = [b.stream(nc, stk, f"SS{j}") for j in range(2)]

        steps = []
        def add(en, fn):
            steps.append((en, fn))
        for i in range(NTILE):
            j = i % 2
            for fc in range(8):
                def mmst(Tn, i=i, j=j, fc=fc):
                    if fc == 0:
                        Tn.wait_stream(WTs, 1)
                        Tn.wait_stream(LDs[j], i // 2 + 1)
                        Tn.wait_stream(XNs[j], i // 2 + 1)
                    ps = psA if fc % 2 == 0 else psB
                    for hv in range(16):
                        ins = Tn.raw("matmul", ps[:], Wo_sb[:, hv, fc*128:(fc+1)*128],
                                     ot[j][:, hv, :], start=(hv == 0), stop=(hv == 15))
                    Tn.inc(ins)
                    if fc == 7:
                        Tn.mark(("otfree", i))
                add("Tn", mmst)
                def addst(V, i=i, j=j, fc=fc):
                    if fc == 0:
                        V.wait_stream(SSs[j], i // 2)
                    ps = psA if fc % 2 == 0 else psB
                    V.tensor_tensor(st_[j][:, fc, :], xnt[j][:, fc, :], ps[:], op=ALU.add)
                    if fc == 7:
                        V.mark(("sdone", i))
                add("V", addst)
            def sstore(G, i=i, j=j):
                G.dma(fm_slice(s_fm, i*W, W), st_[j][:], SSs[j])
            add("G", sstore)

        blk = e(nc.Block())
        @blk.sync
        def _(q_):
            Q = b.Q
            for i in range(NTILE):
                j = i % 2
                bb = i // 4
                tl = (i % 4) * W
                if i >= 2:
                    Q.wait(("otfree", i - 2))
                Q.dma(ot[j][:], o_n[bb][:, :, tl:tl+W].rearrange("h (c p) t -> p (h c) t", p=128), LDs[j])
        @blk.scalar
        def _(s):
            S = b.S
            for i in range(NTILE):
                j = i % 2
                if i >= 2:
                    S.wait(("sdone", i - 2))
                S.dma(xnt[j][:], fm_slice(xn_f, i*W, W), XNs[j])
        @blk.gpsimd
        def _(g):
            G = b.G
            G.dma(Wo_sb[:], Wo.rearrange("(c p) f -> p c f", p=128), WTs)
            run_chain(b, steps, G)
            for j in range(2):
                if SSs[j].n:
                    G.wait_stream(SSs[j], SSs[j].n)
        @blk.tensor
        def _(t):
            run_chain(b, steps, b.Tn)
        @blk.vector
        def _(v):
            run_chain(b, steps, b.V)
    return nc


def build_launch_D():
    return build_twice(_launch_D)



def _launch_E(b):
    nc = _mknc()
    xn_b = _dram(nc, "xn_b", [D, NT], BF16, "ExternalInput")
    Wr = _dram(nc, "Wr", [D, 16], F32, "ExternalInput")
    id128 = _dram(nc, "id128", [128, 128], BF16, "ExternalInput")
    cwT = _dram(nc, "cwT", [16, NT], BF16, "ExternalOutput")
    W = 512
    NTILE = NT // W
    with ExitStack() as stk:
        e = stk.enter_context
        b.make(nc, stk)
        xnb = [e(_sbuf(nc, f"xnb{j}", [128, 8, W], BF16)) for j in range(2)]
        Wr_sb = e(_sbuf(nc, "Wr_sb", [128, 8, 16], BF16))
        id_sb = e(_sbuf(nc, "id_sb", [128, 128], BF16))
        probs = e(_sbuf(nc, "probs", [128, 16], F32))
        lo = e(_sbuf(nc, "lo", [128, 16], F32))
        ge = e(_sbuf(nc, "ge", [128, 16], F32))
        cw_b = e(_sbuf(nc, "cw_b", [128, 16], BF16))
        mrow = e(_sbuf(nc, "mrow", [128, 4], F32))
        cwt_sb = [e(_sbuf(nc, f"cwt{j}", [16, W], BF16)) for j in range(2)]
        psR = e(_psum(nc, "psR", [128, 16], F32))
        psT = e(_psum(nc, "psT", [16, 128], BF16))
        LDs = [b.stream(nc, stk, f"LD{j}") for j in range(2)]
        WTs = b.stream(nc, stk, "WT")
        SCs = [b.stream(nc, stk, f"SC{j}") for j in range(2)]

        steps = []
        def add(en, fn):
            steps.append((en, fn))
        for i in range(NTILE):
            j = i % 2
            for tt in range(4):
                def rmm(Tn, i=i, j=j, tt=tt):
                    if tt == 0:
                        Tn.wait_stream(WTs, 2)
                        Tn.wait_stream(LDs[j], i // 2 + 1)
                    for dc in range(8):
                        ins = Tn.raw("matmul", psR[:], xnb[j][:, dc, tt*128:(tt+1)*128],
                                     Wr_sb[:, dc, :], start=(dc == 0), stop=(dc == 7))
                    Tn.inc(ins)
                    if tt == 3:
                        Tn.mark(("xfree", i))
                add("Tn", rmm)
                def soft1(V):
                    V.reduce_max(mrow[:, 0:1], psR[:], axis=AX.X, negate=True)
                add("V", soft1)
                def soft2(S):
                    S.activation(probs[:], psR[:], AF.Exp, bias=mrow[:, 0:1],
                                 accum_out=mrow[:, 1:2])
                add("S", soft2)
                def soft3(V, i=i, j=j, tt=tt):
                    V.reciprocal(mrow[:, 1:2], mrow[:, 1:2])
                    V.tensor_scalar(probs[:], probs[:], mrow[:, 1:2], None, op0=ALU.mult)
                    V.reduce_max(mrow[:, 2:3], probs[:], axis=AX.X)
                    V.tensor_scalar(ge[:], probs[:], mrow[:, 2:3], None, op0=ALU.is_ge)
                    V.tensor_tensor(lo[:], probs[:], ge[:], op=ALU.mult)
                    V.tensor_tensor(lo[:], probs[:], lo[:], op=ALU.subtract)
                    V.reduce_max(mrow[:, 3:4], lo[:], axis=AX.X)
                    V.tensor_scalar(ge[:], probs[:], mrow[:, 3:4], None, op0=ALU.is_ge)
                    V.tensor_tensor(cw_b[:], probs[:], ge[:], op=ALU.mult)
                add("V", soft3)
                def tpcw(Tn, tt=tt):
                    Tn.inc(Tn.raw("transpose", psT[:], cw_b[:], id_sb[:]))
                add("Tn", tpcw)
                def cpcw(V, i=i, j=j, tt=tt):
                    if tt == 0:
                        V.wait_stream(SCs[j], i // 2)
                    V.tensor_copy(cwt_sb[j][:, tt*128:(tt+1)*128], psT[:])
                add("V", cpcw)
            def cstore(G, i=i, j=j):
                G.dma(cwT[:, i*W:(i+1)*W], cwt_sb[j][:], SCs[j])
            add("G", cstore)

        blk = e(nc.Block())
        @blk.sync
        def _(q_):
            Q = b.Q
            for i in range(NTILE):
                j = i % 2
                if i >= 2:
                    Q.wait(("xfree", i - 2))
                Q.dma(xnb[j][:], fm_slice(xn_b, i*W, W), LDs[j])
        @blk.gpsimd
        def _(g):
            G = b.G
            G.dma(Wr_sb[:], Wr.rearrange("(c p) f -> p c f", p=128), WTs)
            G.dma(id_sb[:], id128, WTs)
            run_chain(b, steps, G)
            for j in range(2):
                if SCs[j].n:
                    G.wait_stream(SCs[j], SCs[j].n)
        @blk.tensor
        def _(t):
            run_chain(b, steps, b.Tn)
        @blk.scalar
        def _(s):
            run_chain(b, steps, b.S)
        @blk.vector
        def _(v):
            run_chain(b, steps, b.V)
    return nc


def build_launch_E():
    return build_twice(_launch_E)



def _launch_G(b):
    nc = _mknc()
    x = _dram(nc, "x_fm", [D, NT], F32, "ExternalInput")
    s_fm = _dram(nc, "s_fm", [D, NT], F32, "ExternalInput")
    moe = _dram(nc, "moe_fm", [D, NT], F32, "ExternalInput")
    onesb = _dram(nc, "onesb", [128, 1], BF16, "ExternalInput")
    ones1 = _dram(nc, "ones1", [1, 128], BF16, "ExternalInput")
    x1_f = _dram(nc, "x1_f", [D, NT], F32, "ExternalOutput")
    xn1_f = _dram(nc, "xn1_f", [D, NT], F32, "ExternalOutput")
    xn1_b = _dram(nc, "xn1_b", [D, NT], BF16, "ExternalOutput")
    W = 512
    NTILE = NT // W
    with ExitStack() as stk:
        e = stk.enter_context
        b.make(nc, stk)
        xt = [e(_sbuf(nc, f"xt{j}", [128, 8, W], F32)) for j in range(2)]
        st_ = [e(_sbuf(nc, f"st{j}", [128, 8, W], F32)) for j in range(2)]
        mt = [e(_sbuf(nc, f"mt{j}", [128, 8, W], F32)) for j in range(2)]
        x1b = [e(_sbuf(nc, f"x1b{j}", [128, 8, W], BF16)) for j in range(2)]
        xsq = [e(_sbuf(nc, f"xsq{j}", [128, 8, W], BF16)) for j in range(2)]
        xn1bt = [e(_sbuf(nc, f"xn1b{j}", [128, 8, W], BF16)) for j in range(2)]
        x1t = xt
        xn1t = mt
        rows = [e(_sbuf(nc, f"rows{j}", [1, 2, W], F32)) for j in range(2)]
        rows_b = [e(_sbuf(nc, f"rowsb{j}", [1, 2, W], BF16)) for j in range(2)]
        ones_sb = e(_sbuf(nc, "ones_sb", [128, 1], BF16))
        ones1_sb = e(_sbuf(nc, "ones1_sb", [1, 128], BF16))
        ps_mu = e(_psum(nc, "ps_mu", [1, W], F32))
        ps_s2 = e(_psum(nc, "ps_s2", [1, W], F32))
        ps_b1 = e(_psum(nc, "ps_b1", [128, W], F32))
        ps_b2 = e(_psum(nc, "ps_b2", [128, W], F32))
        LXs = [b.stream(nc, stk, f"LX{j}") for j in range(2)]
        LSs = [b.stream(nc, stk, f"LS{j}") for j in range(2)]
        LMs = [b.stream(nc, stk, f"LM{j}") for j in range(2)]
        WTs = b.stream(nc, stk, "WT")
        S1s = [b.stream(nc, stk, f"S1{j}") for j in range(2)]
        S2s = [b.stream(nc, stk, f"S2{j}") for j in range(2)]
        S3s = [b.stream(nc, stk, f"S3{j}") for j in range(2)]

        steps = []
        def add(en, fn):
            steps.append((en, fn))
        for i in range(NTILE):
            j = i % 2
            def gsilu(S, i=i, j=j):
                S.wait_stream(LMs[j], i // 2 + 1)
                S.activation(mt[j][:].rearrange("p c t -> p (c t)"),
                             mt[j][:].rearrange("p c t -> p (c t)"), AF.Silu)
            add("S", gsilu)
            def comb(V, i=i, j=j):
                V.wait_stream(LXs[j], i // 2 + 1)
                V.wait_stream(LSs[j], i // 2 + 1)
                if i >= 2:
                    V.wait_stream(S1s[j], i // 2)
                V.tensor_tensor(mt[j][:], st_[j][:], mt[j][:], op=ALU.mult)
                V.tensor_tensor(x1t[j][:], xt[j][:], mt[j][:], op=ALU.add)
                V.mark(("x1done", i))
            add("V", comb)
            def x1cast(G, i=i, j=j):
                G.tensor_copy(x1b[j][:], x1t[j][:])
                G.dma(fm_slice(x1_f, i*W, W), x1t[j][:], S1s[j])
            add("G", x1cast)
            def sq(S, i=i, j=j):
                S.activation(xsq[j][:].rearrange("p c t -> p (c t)"),
                             x1t[j][:].rearrange("p c t -> p (c t)"), AF.Square)
            add("S", sq)
            def stats(Tn, i=i, j=j):
                if i == 0:
                    Tn.wait_stream(WTs, 2)
                for c in range(8):
                    ins = Tn.raw("matmul", ps_mu[:], ones_sb[:], x1b[j][:, c, :],
                                 start=(c == 0), stop=(c == 7))
                Tn.inc(ins)
                for c in range(8):
                    ins = Tn.raw("matmul", ps_s2[:], ones_sb[:], xsq[j][:, c, :],
                                 start=(c == 0), stop=(c == 7))
                Tn.inc(ins)
            add("Tn", stats)
            def rowsa(V, j=j):
                V.tensor_scalar(rows[j][:, 0, :], ps_mu[:], 1.0 / D, None, op0=ALU.mult)
                V.tensor_tensor(rows[j][:, 1, :], rows[j][:, 0, :], rows[j][:, 0, :], op=ALU.mult)
                V.tensor_scalar(rows[j][:, 1, :], rows[j][:, 1, :], -1.0, 1e-5, op0=ALU.mult, op1=ALU.add)
                V.scalar_tensor_tensor(rows[j][:, 1, :], ps_s2[:], 1.0 / D, rows[j][:, 1, :],
                                       op0=ALU.mult, op1=ALU.add)
            add("V", rowsa)
            def rowsb(S, j=j):
                S.activation(rows[j][:, 1, :], rows[j][:, 1, :], AF.Ln)
                S.activation(rows_b[j][:, 0, :], rows[j][:, 1, :], AF.Exp, scale=-0.5)
            add("S", rowsb)
            def roww(V, j=j):
                V.tensor_tensor(rows_b[j][:, 1, :], rows[j][:, 0, :], rows_b[j][:, 0, :], op=ALU.mult)
            add("V", roww)
            def bcast(Tn, j=j):
                Tn.inc(Tn.raw("matmul", ps_b1[:], ones1_sb[:], rows_b[j][:, 0, :], start=True, stop=True))
                Tn.inc(Tn.raw("matmul", ps_b2[:], ones1_sb[:], rows_b[j][:, 1, :], start=True, stop=True))
            add("Tn", bcast)
            def apply(V, i=i, j=j):
                pass_ = None
                V.tensor_tensor(xn1t[j][:], x1t[j][:], bc_mid(ps_b1[:], 8), op=ALU.mult)
                V.tensor_tensor(xn1t[j][:], xn1t[j][:], bc_mid(ps_b2[:], 8), op=ALU.subtract)
            add("V", apply)
            def outs(G, i=i, j=j):
                if i >= 2:
                    G.wait_stream(S3s[j], i // 2)
                G.tensor_copy(xn1bt[j][:], xn1t[j][:])
                G.dma(fm_slice(xn1_f, i*W, W), xn1t[j][:], S2s[j])
                G.dma(fm_slice(xn1_b, i*W, W), xn1bt[j][:], S3s[j])
            add("G", outs)

        blk = e(nc.Block())
        @blk.sync
        def _(q_):
            Q = b.Q
            for i in range(NTILE):
                j = i % 2
                if i >= 2:
                    Q.wait(("x1done", i - 2))
                    Q.wait_stream(S1s[j], i // 2)
                    Q.wait_stream(S2s[j], i // 2)
                    Q.wait_stream(S3s[j], i // 2)
                Q.dma(xt[j][:], fm_slice(x, i*W, W), LXs[j])
                Q.dma(st_[j][:], fm_slice(s_fm, i*W, W), LSs[j])
                Q.dma(mt[j][:], fm_slice(moe, i*W, W), LMs[j])
        @blk.scalar
        def _(s):
            run_chain(b, steps, b.S)
        @blk.gpsimd
        def _(g):
            G = b.G
            G.dma(ones_sb[:], onesb, WTs)
            G.dma(ones1_sb[:], ones1, WTs)
            run_chain(b, steps, G)
            for j in range(2):
                for s_ in (S1s[j], S2s[j], S3s[j]):
                    if s_.n:
                        G.wait_stream(s_, s_.n)
        @blk.tensor
        def _(t):
            run_chain(b, steps, b.Tn)
        @blk.vector
        def _(v):
            run_chain(b, steps, b.V)
    return nc


def build_launch_G():
    return build_twice(_launch_G)



def _launch_F(b):
    nc = _mknc()
    xn_b = _dram(nc, "xn_b", [D, NT], BF16, "ExternalInput")
    cwT = _dram(nc, "cwT", [16, NT], BF16, "ExternalInput")
    Wg = _dram(nc, "Wg", [16, D, 512], F32, "ExternalInput")
    Wu = _dram(nc, "Wu", [16, D, 512], F32, "ExternalInput")
    Wd = _dram(nc, "Wd", [16, 512, D], F32, "ExternalInput")
    Sg = _dram(nc, "Sg", [D, 1024], F32, "ExternalInput")
    Su = _dram(nc, "Su", [D, 1024], F32, "ExternalInput")
    Sd = _dram(nc, "Sd", [1024, D], F32, "ExternalInput")
    sel16 = _dram(nc, "sel16", [16, 2048], BF16, "ExternalInput")
    moe = _dram(nc, "moe_fm", [D, NT], F32, "ExternalOutput")
    WB = 2048
    NBLK = NT // WB
    NE = 18
    with ExitStack() as stk:
        e = stk.enter_context
        b.make(nc, stk)
        xnb = e(_sbuf(nc, "xnb", [128, 8, WB], BF16))
        cw_sb = e(_sbuf(nc, "cw_sb", [16, WB], BF16))
        sel_sb = e(_sbuf(nc, "sel_sb", [16, 2048], BF16))
        Wg_sb = [e(_sbuf(nc, f"Wg{j}", [128, 8, 512], BF16)) for j in range(2)]
        Wu_sb = [e(_sbuf(nc, f"Wu{j}", [128, 8, 512], BF16)) for j in range(2)]
        Wd_sb = [e(_sbuf(nc, f"Wd{j}", [128, 4, 1024], BF16)) for j in range(2)]
        acc = e(_sbuf(nc, "acc", [128, 8, WB], F32))
        hg = e(_sbuf(nc, "hg", [128, 4, 512], F32))
        h1 = e(_sbuf(nc, "h1", [128, 4, 512], F32))
        hc = e(_sbuf(nc, "hc", [128, 4, 512], BF16))
        psG = e(_psum(nc, "psG", [128, 4, 512], F32))
        psU = e(_psum(nc, "psU", [128, 4, 512], F32))
        LXs = b.stream(nc, stk, "LX")
        LCs = b.stream(nc, stk, "LC")
        WTs = b.stream(nc, stk, "WT")
        Wgs = [b.stream(nc, stk, f"WG{j}") for j in range(2)]
        SMs = b.stream(nc, stk, "SM")

        steps = []
        def add(en, fn):
            steps.append((en, fn))

        def emit_wload(G, eidx, j):
            if eidx < 16:
                G.dma(Wg_sb[j][:], Wg[eidx].rearrange("(c p) f -> p c f", p=128), Wgs[j])
                G.dma(Wu_sb[j][:], Wu[eidx].rearrange("(c p) f -> p c f", p=128), Wgs[j])
                G.dma(Wd_sb[j][:], Wd[eidx].rearrange("(c p) f -> p c f", p=128), Wgs[j])
            else:
                hh = eidx - 16
                G.dma(Wg_sb[j][:], Sg[:, hh*512:(hh+1)*512].rearrange("(c p) f -> p c f", p=128), Wgs[j])
                G.dma(Wu_sb[j][:], Su[:, hh*512:(hh+1)*512].rearrange("(c p) f -> p c f", p=128), Wgs[j])
                G.dma(Wd_sb[j][:], Sd[hh*512:(hh+1)*512, :].rearrange("(c p) f -> p c f", p=128), Wgs[j])

        ne = 0
        for blk_i in range(NBLK):
            def bload(G, blk_i=blk_i):
                if blk_i > 0:
                    G.wait_stream(SMs, blk_i)
                G.dma(xnb[:], fm_slice(xn_b, blk_i*WB, WB), LXs)
                G.dma(cw_sb[:], cwT[:, blk_i*WB:(blk_i+1)*WB], LCs)
            add("G", bload)
            for eidx in list(range(16)) + [16, 17]:
                j = ne % 2
                if ne == 0:
                    def wl0(G, eidx=eidx, j=j):
                        emit_wload(G, eidx, j)
                    add("G", wl0)
                nxt = eidx + 1 if eidx < 17 else (0 if blk_i + 1 < NBLK else None)
                if nxt is not None:
                    def wln(G, nxt=nxt, j2=(ne + 1) % 2):
                        emit_wload(G, nxt, j2)
                    add("G", wln)
                for q in range(4):
                    def gmm(Tn, j=j, q=q, ne=ne, blk_i=blk_i):
                        if q == 0:
                            Tn.wait_stream(Wgs[j], 3 * (ne // 2 + 1))
                            Tn.wait_stream(LXs, blk_i + 1)
                            Tn.wait_stream(LCs, blk_i + 1)
                            if ne == 0:
                                Tn.wait_stream(WTs, 1)
                        for fc in range(4):
                            for dc in range(8):
                                ins = Tn.raw("matmul", psG[:, fc, :],
                                             Wg_sb[j][:, dc, fc*128:(fc+1)*128],
                                             xnb[:, dc, q*512:(q+1)*512],
                                             start=(dc == 0), stop=(dc == 7))
                        Tn.inc(ins)
                    add("Tn", gmm)
                    def gsil(S):
                        S.activation(hg[:].rearrange("p c t -> p (c t)"),
                                     psG[:].rearrange("p c t -> p (c t)"), AF.Silu)
                    add("S", gsil)
                    def umm(Tn, j=j, eidx=eidx, q=q):
                        for fc in range(4):
                            for dc in range(8):
                                ins = Tn.raw("matmul", psU[:, fc, :],
                                             Wu_sb[j][:, dc, fc*128:(fc+1)*128],
                                             xnb[:, dc, q*512:(q+1)*512],
                                             start=(dc == 0), stop=(dc == 7))
                        Tn.inc(ins)
                        if eidx < 16:
                            Tn.inc(Tn.raw("matmul", psG[:, 0, :], sel_sb[:, eidx*128:(eidx+1)*128],
                                          cw_sb[:, q*512:(q+1)*512], start=True, stop=True))
                    add("Tn", umm)
                    def hbuild(V, eidx=eidx):
                        V.tensor_tensor(h1[:].rearrange("p c t -> p (c t)"),
                                        hg[:].rearrange("p c t -> p (c t)"),
                                        psU[:].rearrange("p c t -> p (c t)"), op=ALU.mult)
                        if eidx < 16:
                            V.tensor_tensor(hc[:], h1[:], bc_mid(psG[:, 0, :], 4), op=ALU.mult)
                        else:
                            V.tensor_copy(hc[:].rearrange("p c t -> p (c t)"),
                                          h1[:].rearrange("p c t -> p (c t)"))
                    add("V", hbuild)
                    for half in range(2):
                        ps = psG if half == 0 else psU
                        def wdmm(Tn, j=j, half=half, ps=ps):
                            for dchh in range(4):
                                dch = half*4 + dchh
                                for fsc in range(4):
                                    ins = Tn.raw("matmul", ps[:, dchh, :],
                                                 Wd_sb[j][:, fsc, dch*128:(dch+1)*128],
                                                 hc[:, fsc, :],
                                                 start=(fsc == 0), stop=(fsc == 3))
                            Tn.inc(ins)
                        add("Tn", wdmm)
                        def wdacc(V, half=half, q=q, eidx=eidx, ps=ps):
                            a = acc[:, half*4:half*4+4, q*512:(q+1)*512]
                            pv = ps[:]
                            if eidx == 0:
                                V.tensor_copy(a, pv)
                            else:
                                V.tensor_tensor(a, a, pv, op=ALU.add)
                        add("V", wdacc)
                ne += 1
            def mstore(G, blk_i=blk_i):
                G.dma(fm_slice(moe, blk_i*WB, WB), acc[:], SMs)
            add("G", mstore)

        blk = e(nc.Block())
        @blk.gpsimd
        def _(g):
            G = b.G
            G.dma(sel_sb[:], sel16, WTs)
            run_chain(b, steps, G)
            if SMs.n:
                G.wait_stream(SMs, SMs.n)
        @blk.tensor
        def _(t):
            run_chain(b, steps, b.Tn)
        @blk.scalar
        def _(s):
            run_chain(b, steps, b.S)
        @blk.vector
        def _(v):
            run_chain(b, steps, b.V)
        @blk.sync
        def _(q_):
            pass
    return nc


def build_launch_F():
    return build_twice(_launch_F)



def _launch_H(b):
    nc = _mknc()
    xn1 = _dram(nc, "xn1_b", [D, NT], BF16, "ExternalInput")
    Wq1 = _dram(nc, "Wq1", [D, 1024], F32, "ExternalInput")
    Wqr = _dram(nc, "Wqr", [D, 512], F32, "ExternalInput")
    Wkr = _dram(nc, "Wkr", [D, 32], F32, "ExternalInput")
    Wkvd = _dram(nc, "Wkvd", [D, 256], F32, "ExternalInput")
    Wku = _dram(nc, "Wku", [256, 1024], F32, "ExternalInput")
    Wvu = _dram(nc, "Wvu", [256, 1024], F32, "ExternalInput")
    ropeC = _dram(nc, "ropeC", [T, 512], F32, "ExternalInput")
    ropeS = _dram(nc, "ropeS", [T, 512], F32, "ExternalInput")
    id128 = _dram(nc, "id128", [128, 128], BF16, "ExternalInput")
    q96 = _dram(nc, "q96", [B, 16, 96, T], BF16, "ExternalOutput")
    k96 = _dram(nc, "k96", [B, 16, 96, T], BF16, "ExternalOutput")
    vaug = _dram(nc, "vaug", [B, T, 1040], BF16, "ExternalOutput")
    W = 512
    NTILE = NT // W
    with ExitStack() as stk:
        e = stk.enter_context
        b.make(nc, stk)
        xnb = [e(_sbuf(nc, f"xnb{j}", [128, 8, W], BF16)) for j in range(2)]
        Wq_sb = e(_sbuf(nc, "Wq_sb", [128, 8, 1024], BF16))
        Wqr_sb = e(_sbuf(nc, "Wqr_sb", [128, 8, 512], BF16))
        Wkr_sb = e(_sbuf(nc, "Wkr_sb", [128, 8, 32], BF16))
        Wkvd_sb = e(_sbuf(nc, "Wkvd_sb", [128, 8, 256], BF16))
        Wku_sb = e(_sbuf(nc, "Wku_sb", [128, 2, 1024], BF16))
        Wvu_sb = e(_sbuf(nc, "Wvu_sb", [128, 2, 1024], BF16))
        id_sb = e(_sbuf(nc, "id_sb", [128, 128], BF16))
        ropC = [e(_sbuf(nc, f"ropC{j}", [128, 4, 512], F32)) for j in range(2)]
        ropS = [e(_sbuf(nc, f"ropS{j}", [128, 4, 512], F32)) for j in range(2)]
        qc_t = [e(_sbuf(nc, f"qc{j}", [128, 8, W], BF16)) for j in range(2)]
        lat_b = [e(_sbuf(nc, f"lat{j}", [128, 2, W], BF16)) for j in range(2)]
        kc_t = [e(_sbuf(nc, f"kc{j}", [128, 8, W], BF16)) for j in range(2)]
        qr_tm = e(_sbuf(nc, "qr_tm", [128, 4, 512], F32))
        qr_b = e(_sbuf(nc, "qr_b", [128, 4, 512], BF16))
        kr_tm = e(_sbuf(nc, "kr_tm", [128, 4, 32], F32))
        kr_b = e(_sbuf(nc, "kr_b", [128, 4, 32], BF16))
        qr_f = [e(_sbuf(nc, f"qrf{j}", [128, 4, 128], BF16)) for j in range(2)]
        kr_f = [e(_sbuf(nc, f"krf{j}", [32, 4, 128], BF16)) for j in range(2)]
        vau_t = [e(_sbuf(nc, f"vau{j}", [128, 4, 1040], BF16)) for j in range(2)]
        tmp1 = e(_sbuf(nc, "tmp1", [128, 512], F32))
        tmp2 = e(_sbuf(nc, "tmp2", [128, 512], F32))
        psA = e(_psum(nc, "psA", [128, W], F32))
        psB = e(_psum(nc, "psB", [128, W], F32))
        psT = e(_psum(nc, "psT", [128, 128], BF16))
        psK = e(_psum(nc, "psK", [32, 128], BF16))
        LDs = [b.stream(nc, stk, f"LD{j}") for j in range(2)]
        RPs = [b.stream(nc, stk, f"RP{j}") for j in range(2)]
        WTs = b.stream(nc, stk, "WT")
        SQ = [b.stream(nc, stk, f"SQ{j}") for j in range(2)]
        SK = [b.stream(nc, stk, f"SK{j}") for j in range(2)]
        SV = [b.stream(nc, stk, f"SV{j}") for j in range(2)]

        steps = []
        def add(en, fn):
            steps.append((en, fn))
        for i in range(NTILE):
            j = i % 2
            bb = i // 4
            tl = (i % 4) * W
            for fc in range(8):
                def qcmm(Tn, i=i, j=j, fc=fc):
                    if fc == 0:
                        Tn.wait_stream(WTs, 7)
                        Tn.wait_stream(LDs[j], i // 2 + 1)
                        Tn.wait_stream(RPs[j], 2 * (i // 2 + 1))
                    ps = psA if fc % 2 == 0 else psB
                    for dc in range(8):
                        ins = Tn.raw("matmul", ps[:], Wq_sb[:, dc, fc*128:(fc+1)*128],
                                     xnb[j][:, dc, :], start=(dc == 0), stop=(dc == 7))
                    Tn.inc(ins)
                add("Tn", qcmm)
                def qcev(S, i=i, j=j, fc=fc):
                    if fc == 0:
                        S.wait_stream(SQ[j], 80 * (i // 2))
                    ps = psA if fc % 2 == 0 else psB
                    S.copy(qc_t[j][:, fc, :], ps[:])
                add("S", qcev)
            def qcstore(G, i=i, j=j, bb=bb, tl=tl):
                for fc in range(8):
                    for h2 in range(2):
                        G.dma(q96[bb, fc*2+h2, 0:64, tl:tl+W], qc_t[j][64*h2:64*h2+64, fc, :], SQ[j])
            add("G", qcstore)
            for tt in range(4):
                def qrmm(Tn, j=j, tt=tt):
                    for dc in range(8):
                        ins = Tn.raw("matmul", psA[:], xnb[j][:, dc, tt*128:(tt+1)*128],
                                     Wqr_sb[:, dc, :], start=(dc == 0), stop=(dc == 7))
                    Tn.inc(ins)
                add("Tn", qrmm)
                def qrrope(V, j=j, tt=tt):
                    def v32(ap, off):
                        return bass.AP(tensor=ap.tensor, offset=ap.offset + off,
                                       ap=[list(ap.ap[0]), [32, 16], [1, 16]])
                    V.tensor_tensor(tmp1[:], psA[:], ropC[j][:, tt, :], op=ALU.mult)
                    V.tensor_tensor(v32(tmp2[:], 0), v32(psA[:], 16),
                                    v32(ropS[j][:, tt, :], 0), op=ALU.mult)
                    V.tensor_tensor(v32(tmp2[:], 16), v32(psA[:], 0),
                                    v32(ropS[j][:, tt, :], 16), op=ALU.mult)
                    V.tensor_tensor(qr_b[:, tt, :], tmp1[:], tmp2[:], op=ALU.add)
                add("V", qrrope)
            for tt in range(4):
                for g in range(4):
                    def qrtp(Tn, j=j, tt=tt, g=g):
                        Tn.inc(Tn.raw("transpose", psT[:], qr_b[:, tt, g*128:(g+1)*128], id_sb[:]))
                    add("Tn", qrtp)
                    def qrtpc(V, j=j, tt=tt, g=g, i=i):
                        if g == 0:
                            V.wait_stream(SQ[j], 80 * (i // 2) + 16 + 16 * tt)
                        V.tensor_copy(qr_f[j][:, g, :], psT[:])
                    add("V", qrtpc)
                def qrstore(G, j=j, bb=bb, tl=tl, tt=tt):
                    for g in range(4):
                        for hh in range(4):
                            G.dma(q96[bb, 4*g+hh, 64:96, tl+tt*128:tl+(tt+1)*128],
                                  qr_f[j][32*hh:32*hh+32, g, :], SQ[j])
                add("G", qrstore)
            for tt in range(4):
                def krmm(Tn, j=j, tt=tt):
                    for dc in range(8):
                        ins = Tn.raw("matmul", psA[:, 0:32], xnb[j][:, dc, tt*128:(tt+1)*128],
                                     Wkr_sb[:, dc, :], start=(dc == 0), stop=(dc == 7))
                    Tn.inc(ins)
                add("Tn", krmm)
                def krrope(V, j=j, tt=tt):
                    V.tensor_tensor(tmp1[:, 0:32], psA[:, 0:32], ropC[j][:, tt, 0:32], op=ALU.mult)
                    V.tensor_tensor(tmp2[:, 0:16], psA[:, 16:32], ropS[j][:, tt, 0:16], op=ALU.mult)
                    V.tensor_tensor(tmp2[:, 16:32], psA[:, 0:16], ropS[j][:, tt, 16:32], op=ALU.mult)
                    V.tensor_tensor(kr_b[:, tt, :], tmp1[:, 0:32], tmp2[:, 0:32], op=ALU.add)
                add("V", krrope)
                def krtp(Tn, j=j, tt=tt):
                    Tn.inc(Tn.raw("transpose", psK[:], kr_b[:, tt, :], id_sb[:]))
                add("Tn", krtp)
                def krtpc(V, j=j, tt=tt, i=i):
                    if tt == 0:
                        V.wait_stream(SK[j], 80 * (i // 2))
                    V.tensor_copy(kr_f[j][0:32, tt, :], psK[:])
                add("V", krtpc)
            def krstore(G, j=j, bb=bb, tl=tl):
                for tt in range(4):
                    for h in range(16):
                        G.dma(k96[bb, h, 64:96, tl+tt*128:tl+(tt+1)*128],
                              kr_f[j][0:32, tt, :], SK[j])
            add("G", krstore)
            for fc in range(2):
                def latmm(Tn, j=j, fc=fc):
                    for dc in range(8):
                        ins = Tn.raw("matmul", psA[:], Wkvd_sb[:, dc, fc*128:(fc+1)*128],
                                     xnb[j][:, dc, :], start=(dc == 0), stop=(dc == 7))
                    Tn.inc(ins)
                add("Tn", latmm)
                def latev(S, j=j, fc=fc):
                    S.copy(lat_b[j][:, fc, :], psA[:])
                add("S", latev)
            for fc in range(8):
                def kcmm(Tn, j=j, fc=fc):
                    ps = psA if fc % 2 == 0 else psB
                    for dc in range(2):
                        ins = Tn.raw("matmul", ps[:], Wku_sb[:, dc, fc*128:(fc+1)*128],
                                     lat_b[j][:, dc, :], start=(dc == 0), stop=(dc == 1))
                    Tn.inc(ins)
                add("Tn", kcmm)
                def kcev(S, j=j, fc=fc, i=i):
                    if fc == 0:
                        S.wait_stream(SK[j], 80 * (i // 2))
                    ps = psA if fc % 2 == 0 else psB
                    S.copy(kc_t[j][:, fc, :], ps[:])
                add("S", kcev)
            def kcstore(G, i=i, j=j, bb=bb, tl=tl):
                for fc in range(8):
                    for h2 in range(2):
                        G.dma(k96[bb, fc*2+h2, 0:64, tl:tl+W], kc_t[j][64*h2:64*h2+64, fc, :], SK[j])
            add("G", kcstore)
            def vinit(G, j=j, i=i):
                G.wait_stream(SV[j], i // 2)
                ap = bass.AP(tensor=vau_t[j][:].tensor, offset=vau_t[j][:].offset + 64,
                             ap=[list(vau_t[j][:].ap[0]), [1040, 4], [65, 16], [1, 1]])
                G.memset(ap, 1.0)
            add("G", vinit)
            for tt in range(4):
                for vh in range(2):
                    def vmm(Tn, j=j, tt=tt, vh=vh):
                        ps = psA if vh == 0 else psB
                        for dc in range(2):
                            ins = Tn.raw("matmul", ps[:], lat_b[j][:, dc, tt*128:(tt+1)*128],
                                         Wvu_sb[:, dc, vh*512:(vh+1)*512], start=(dc == 0), stop=(dc == 1))
                        Tn.inc(ins)
                    add("Tn", vmm)
                    def vev(V, j=j, tt=tt, vh=vh):
                        ps = psA if vh == 0 else psB
                        dst = vau_t[j][:, tt, :]
                        dstap = bass.AP(tensor=dst.tensor, offset=dst.offset + vh*8*65,
                                        ap=[list(dst.ap[0]), [65, 8], [1, 64]])
                        V.tensor_copy(dstap, ps[:].rearrange("p (h x) -> p h x", h=8))
                    add("V", vev)
            def vstore(G, i=i, j=j, bb=bb, tl=tl):
                G.dma(vaug[bb, tl:tl+W, :].rearrange("(tt p) f -> p tt f", p=128),
                      vau_t[j][:], SV[j])
                G.inc(G.raw("engine_nop"))
                G.mark(("tiledone", i))
            add("G", vstore)

        blk = e(nc.Block())
        @blk.sync
        def _(q_):
            Q = b.Q
            for i in range(NTILE):
                j = i % 2
                bb = i // 4
                tl = (i % 4) * W
                if i >= 2:
                    Q.wait(("tiledone", i - 2))
                Q.dma(xnb[j][:], fm_slice(xn1, i*W, W), LDs[j])
                Q.dma(ropC[j][:], ropeC[tl:tl+W, :].rearrange("(tt p) f -> p tt f", p=128), RPs[j])
                Q.dma(ropS[j][:], ropeS[tl:tl+W, :].rearrange("(tt p) f -> p tt f", p=128), RPs[j])
        @blk.gpsimd
        def _(g):
            G = b.G
            G.dma(Wq_sb[:], Wq1.rearrange("(c p) f -> p c f", p=128), WTs)
            G.dma(Wqr_sb[:], Wqr.rearrange("(c p) f -> p c f", p=128), WTs)
            G.dma(Wkr_sb[:], Wkr.rearrange("(c p) f -> p c f", p=128), WTs)
            G.dma(Wkvd_sb[:], Wkvd.rearrange("(c p) f -> p c f", p=128), WTs)
            G.dma(Wku_sb[:], Wku.rearrange("(c p) f -> p c f", p=128), WTs)
            G.dma(Wvu_sb[:], Wvu.rearrange("(c p) f -> p c f", p=128), WTs)
            G.dma(id_sb[:], id128, WTs)
            run_chain(b, steps, G)
            for j in range(2):
                for s_ in (SQ[j], SK[j], SV[j]):
                    if s_.n:
                        G.wait_stream(s_, s_.n)
        @blk.tensor
        def _(t):
            run_chain(b, steps, b.Tn)
        @blk.scalar
        def _(s):
            run_chain(b, steps, b.S)
        @blk.vector
        def _(v):
            run_chain(b, steps, b.V)
    return nc


def build_launch_H():
    return build_twice(_launch_H)



def _launch_I(b):
    nc = _mknc()
    q96 = _dram(nc, "q96", [B, 16, 96, T], BF16, "ExternalInput")
    k96 = _dram(nc, "k96", [B, 16, 96, T], BF16, "ExternalInput")
    vaug = _dram(nc, "vaug", [B, T, 1040], BF16, "ExternalInput")
    lobo = _dram(nc, "lobo", [1, 16], F32, "ExternalInput")
    ones1 = _dram(nc, "ones1", [1, 128], BF16, "ExternalInput")
    id128 = _dram(nc, "id128", [128, 128], BF16, "ExternalInput")
    maskdiag = _dram(nc, "maskdiag", [4, 128, 512], BF16, "ExternalInput")
    attn = _dram(nc, "attn", [B, 1024, T], BF16, "ExternalOutput")
    mscr = _dram(nc, "mscr", [64, T], F32, "Internal")
    SCALE = float(1.0 / np.sqrt(96.0))
    with ExitStack() as stk:
        e = stk.enter_context
        b.make(nc, stk)
        qt = [e(_sbuf(nc, f"qt{j}", [96, T], BF16)) for j in range(2)]
        kt_ = [e(_sbuf(nc, f"kt{j}", [96, T], BF16)) for j in range(2)]
        vt_ = [e(_sbuf(nc, f"vt{j}", [128, 16, 65], BF16)) for j in range(2)]
        emax = e(_sbuf(nc, "emax", [128, 16, 128], BF16))
        e_sb = e(_sbuf(nc, "e_sb", [128, 512], BF16))
        emc = e(_sbuf(nc, "emc", [128, 16], F32))
        mrow = e(_sbuf(nc, "mrow", [1, 2048], F32))
        drow = e(_sbuf(nc, "drow", [1, 2048], F32))
        rrow = e(_sbuf(nc, "rrow", [1, 2048], BF16))
        u_sb = e(_sbuf(nc, "u_sb", [64, 512], F32))
        ao = [e(_sbuf(nc, f"ao{j}", [64, 4, 512], BF16)) for j in range(2)]
        Ch = e(_sbuf(nc, "Ch", [1, 16], F32))
        ones1_sb = e(_sbuf(nc, "ones1_sb", [1, 128], BF16))
        id_sb = e(_sbuf(nc, "id_sb", [128, 128], BF16))
        md_sb = e(_sbuf(nc, "md_sb", [128, 4, 512], BF16))
        psSc = e(_psum(nc, "psSc", [128, 512], F32))
        psO = [e(_psum(nc, f"psO{qb}", [65, 512], F32)) for qb in range(4)]
        psT = e(_psum(nc, "psT", [128, 128], BF16))
        psBc = e(_psum(nc, "psBc", [128, 512], F32))
        LDs = [b.stream(nc, stk, f"LD{j}") for j in range(2)]
        WTs = b.stream(nc, stk, "WT")
        MWs = b.stream(nc, stk, "MW")
        MRs = b.stream(nc, stk, "MR")
        AOs = [b.stream(nc, stk, f"AO{j}") for j in range(2)]

        steps = []
        def add(en, fn):
            steps.append((en, fn))
        def chstart(S):
            S.activation(Ch[:], Ch[:], AF.Exp)
        def chld(G):
            G.dma(Ch[:], lobo, WTs)
            G.dma(ones1_sb[:], ones1, WTs)
            G.dma(id_sb[:], id128, WTs)
            G.dma(md_sb[:], maskdiag.rearrange("m p t -> p m t"), WTs)
        add("G", chld)
        def chexp(S):
            S.wait_stream(WTs, 4)
            S.activation(Ch[:], Ch[:], AF.Exp)
        add("S", chexp)

        bh = -1
        for bb_ in range(B):
            for h in range(16):
                bh += 1
                j = bh % 2
                for tkc in range(16):
                    qb0 = tkc // 4
                    for qb in range(qb0, 4):
                        def scmm(Tn, j=j, tkc=tkc, qb=qb, bh=bh):
                            if tkc == 0 and qb == 0:
                                Tn.wait_stream(LDs[j], 3 * (bh // 2 + 1))
                            Tn.inc(Tn.raw("matmul", psSc[:], kt_[j][:, tkc*128:(tkc+1)*128],
                                          qt[j][:, qb*512:(qb+1)*512], start=True, stop=True))
                        add("Tn", scmm)
                        def scexp(S, tkc=tkc, qb=qb):
                            S.activation(e_sb[:], psSc[:], AF.Exp, scale=SCALE)
                        add("S", scexp)
                        if qb == qb0:
                            def scmask(V, tkc=tkc):
                                V.tensor_tensor(e_sb[:], e_sb[:], md_sb[:, tkc % 4, :], op=ALU.mult)
                            add("V", scmask)
                        def emaxup(V, tkc=tkc, qb=qb):
                            dst = emax[:, qb*4:(qb+1)*4, :]
                            if tkc == 0:
                                V.tensor_copy(dst, e_sb[:].rearrange("p (c t) -> p c t", c=4))
                            else:
                                V.tensor_max(dst, dst, e_sb[:].rearrange("p (c t) -> p c t", c=4))
                        add("V", emaxup)
                        def pvmm(Tn, j=j, tkc=tkc, qb=qb):
                            Tn.inc(Tn.raw("matmul", psO[qb][:], vt_[j][:, tkc, :], e_sb[:],
                                          start=(tkc == 0), stop=(tkc == qb*4 + 3)))
                        add("Tn", pvmm)
                for tc in range(16):
                    def mtp(Tn, tc=tc):
                        Tn.inc(Tn.raw("transpose", psT[:], emax[:, tc, :], id_sb[:]))
                    add("Tn", mtp)
                    def mred(V, tc=tc):
                        V.reduce_max(emc[:, tc:tc+1], psT[:], axis=AX.X)
                    add("V", mred)
                def mwrite(G, bh=bh):
                    for tc in range(16):
                        G.dma(mscr[bh, tc*128:(tc+1)*128], emc[:, tc:tc+1], MWs)
                add("G", mwrite)
                def mread(G, bh=bh):
                    G.wait_stream(MWs, 16 * (bh + 1))
                    G.dma(mrow[:], mscr[bh:bh+1, :], MRs)
                add("G", mread)
                def dcol(V, bh=bh, h=h):
                    V.wait_stream(MRs, bh + 1)
                    for qb in range(4):
                        V.tensor_copy(drow[:, qb*512:(qb+1)*512], psO[qb][64:65, :])
                    V.tensor_scalar(mrow[:], mrow[:], Ch[:, h:h+1], None, op0=ALU.mult)
                    V.tensor_tensor(drow[:], drow[:], mrow[:], op=ALU.add)
                    V.reciprocal(drow[:], drow[:])
                    V.tensor_copy(rrow[:], drow[:])
                add("V", dcol)
                for qb in range(4):
                    def rbc(Tn, qb=qb):
                        Tn.inc(Tn.raw("matmul", psBc[:], ones1_sb[:], rrow[:, qb*512:(qb+1)*512],
                                      start=True, stop=True))
                    add("Tn", rbc)
                    def uev(S, qb=qb):
                        S.copy(u_sb[:], psO[qb][0:64, :])
                    add("S", uev)
                    def oak(V, qb=qb, bh=bh, j=j):
                        if qb == 0:
                            V.wait_stream(AOs[j], 4 * (bh // 2))
                        V.tensor_tensor(ao[j][:, qb, :], u_sb[:], psBc[0:64, :], op=ALU.mult)
                    add("V", oak)
                def astore(G, bh=bh, j=j, bb_=bb_, h=h):
                    for qb in range(4):
                        G.dma(attn[bb_, h*64:(h+1)*64, qb*512:(qb+1)*512], ao[j][:, qb, :], AOs[j])
                    G.inc(G.raw("engine_nop"))
                    G.mark(("bhdone", bh))
                add("G", astore)

        blk = e(nc.Block())
        @blk.sync
        def _(q_):
            Q = b.Q
            bh = -1
            for bb_ in range(B):
                for h in range(16):
                    bh += 1
                    j = bh % 2
                    if bh >= 2:
                        Q.wait(("bhdone", bh - 2))
                    Q.dma(qt[j][:], q96[bb_, h], LDs[j])
                    Q.dma(kt_[j][:], k96[bb_, h], LDs[j])
                    Q.dma(vt_[j][:], vaug[bb_][:, h*65:(h+1)*65].rearrange("(c p) f -> p c f", p=128), LDs[j])
        @blk.gpsimd
        def _(g):
            run_chain(b, steps, b.G)
            for j in range(2):
                if AOs[j].n:
                    b.G.wait_stream(AOs[j], AOs[j].n)
        @blk.tensor
        def _(t):
            run_chain(b, steps, b.Tn)
        @blk.scalar
        def _(s):
            run_chain(b, steps, b.S)
        @blk.vector
        def _(v):
            run_chain(b, steps, b.V)
    return nc


def build_launch_I():
    return build_twice(_launch_I)



def _launch_J(b):
    nc = _mknc()
    attn = _dram(nc, "attn", [B, 1024, T], BF16, "ExternalInput")
    Wo1 = _dram(nc, "Wo1", [1024, 1024], F32, "ExternalInput")
    s1 = _dram(nc, "s1_fm", [D, NT], F32, "ExternalOutput")
    W = 512
    NTILE = NT // W
    with ExitStack() as stk:
        e = stk.enter_context
        b.make(nc, stk)
        at = [e(_sbuf(nc, f"at{j}", [128, 8, W], BF16)) for j in range(2)]
        st_ = [e(_sbuf(nc, f"st{j}", [128, 8, W], F32)) for j in range(2)]
        Wo_sb = e(_sbuf(nc, "Wo_sb", [128, 8, 1024], BF16))
        psA = e(_psum(nc, "psA", [128, W], F32))
        psB = e(_psum(nc, "psB", [128, W], F32))
        LDs = [b.stream(nc, stk, f"LD{j}") for j in range(2)]
        WTs = b.stream(nc, stk, "WT")
        SSs = [b.stream(nc, stk, f"SS{j}") for j in range(2)]
        steps = []
        def add(en, fn):
            steps.append((en, fn))
        for i in range(NTILE):
            j = i % 2
            for fc in range(8):
                def mmst(Tn, i=i, j=j, fc=fc):
                    if fc == 0:
                        Tn.wait_stream(WTs, 1)
                        Tn.wait_stream(LDs[j], i // 2 + 1)
                    ps = psA if fc % 2 == 0 else psB
                    for hv in range(8):
                        ins = Tn.raw("matmul", ps[:], Wo_sb[:, hv, fc*128:(fc+1)*128],
                                     at[j][:, hv, :], start=(hv == 0), stop=(hv == 7))
                    Tn.inc(ins)
                    if fc == 7:
                        Tn.mark(("atfree", i))
                add("Tn", mmst)
                def evst(S, i=i, j=j, fc=fc):
                    if fc == 0:
                        S.wait_stream(SSs[j], i // 2)
                    ps = psA if fc % 2 == 0 else psB
                    S.copy(st_[j][:, fc, :], ps[:])
                add("S", evst)
            def sstore(G, i=i, j=j):
                G.dma(fm_slice(s1, i*W, W), st_[j][:], SSs[j])
            add("G", sstore)
        blk = e(nc.Block())
        @blk.sync
        def _(q_):
            Q = b.Q
            for i in range(NTILE):
                j = i % 2
                bb = i // 4
                tl = (i % 4) * W
                if i >= 2:
                    Q.wait(("atfree", i - 2))
                Q.dma(at[j][:], attn[bb][:, tl:tl+W].rearrange("(c p) t -> p c t", p=128), LDs[j])
        @blk.gpsimd
        def _(g):
            G = b.G
            G.dma(Wo_sb[:], Wo1.rearrange("(c p) f -> p c f", p=128), WTs)
            run_chain(b, steps, G)
            for j in range(2):
                if SSs[j].n:
                    G.wait_stream(SSs[j], SSs[j].n)
        @blk.tensor
        def _(t):
            run_chain(b, steps, b.Tn)
        @blk.scalar
        def _(s):
            run_chain(b, steps, b.S)
        @blk.vector
        def _(v):
            pass
    return nc


def build_launch_J():
    return build_twice(_launch_J)



def _launch_K(b):
    nc = _mknc()
    xn1 = _dram(nc, "xn1_b", [D, NT], BF16, "ExternalInput")
    Wfr = _dram(nc, "Wfr", [D, 1024], F32, "ExternalInput")
    Wfi = _dram(nc, "Wfi", [D, 1024], F32, "ExternalInput")
    bias = _dram(nc, "bias", [128, 8], F32, "ExternalInput")
    dFr = _dram(nc, "dftFr", [D, 1024], BF16, "ExternalInput")
    dFi = _dram(nc, "dftFi", [D, 1024], BF16, "ExternalInput")
    dFiN = _dram(nc, "dftFiNeg", [D, 1024], BF16, "ExternalInput")
    dGr = _dram(nc, "dftFrN", [D, 1024], BF16, "ExternalInput")
    dGi = _dram(nc, "dftFiN", [D, 1024], BF16, "ExternalInput")
    ffn1 = _dram(nc, "ffn1_fm", [D, NT], F32, "ExternalOutput")
    W = 512
    NTILE = NT // W
    with ExitStack() as stk:
        e = stk.enter_context
        b.make(nc, stk)
        Wr_sb = e(_sbuf(nc, "Wr_sb", [128, 8, 1024], BF16))
        Wi_sb = e(_sbuf(nc, "Wi_sb", [128, 8, 1024], BF16))
        Fr_sb = e(_sbuf(nc, "Fr_sb", [128, 8, 1024], BF16))
        Fi_sb = e(_sbuf(nc, "Fi_sb", [128, 8, 1024], BF16))
        FiN_sb = e(_sbuf(nc, "FiN_sb", [128, 8, 1024], BF16))
        T1r = e(_sbuf(nc, "T1r", [128, 8, 1024], BF16))
        T1i = e(_sbuf(nc, "T1i", [128, 8, 1024], BF16))
        M_sb = e(_sbuf(nc, "M_sb", [128, 8, 1024], BF16))
        xnb = [e(_sbuf(nc, f"xnb{j}", [128, 8, W], BF16)) for j in range(2)]
        ft = [e(_sbuf(nc, f"ft{j}", [128, 8, W], F32)) for j in range(2)]
        bias_sb = e(_sbuf(nc, "bias_sb", [128, 8], F32))
        psA = e(_psum(nc, "psA", [128, W], F32))
        psB = e(_psum(nc, "psB", [128, W], F32))
        LDs = [b.stream(nc, stk, f"LD{j}") for j in range(2)]
        WTs = b.stream(nc, stk, "WT")
        SFs = [b.stream(nc, stk, f"SF{j}") for j in range(2)]
        steps = []
        def add(en, fn):
            steps.append((en, fn))
        for (dst, t1, t2) in ((T1r, (Wr_sb, Fr_sb), (Wi_sb, FiN_sb)),
                              (T1i, (Wr_sb, Fi_sb), (Wi_sb, Fr_sb))):
            for oc in range(8):
                for nb2 in range(2):
                    def t1mm(Tn, dst=dst, t1=t1, t2=t2, oc=oc, nb2=nb2):
                        if oc == 0 and nb2 == 0 and dst is T1r:
                            Tn.wait_stream(WTs, 6)
                        ps = psA if nb2 == 0 else psB
                        for dc in range(8):
                            Tn.raw("matmul", ps[:], t1[0][:, dc, oc*128:(oc+1)*128],
                                   t1[1][:, dc, nb2*512:(nb2+1)*512], start=(dc == 0), stop=False)
                        for dc in range(8):
                            ins = Tn.raw("matmul", ps[:], t2[0][:, dc, oc*128:(oc+1)*128],
                                         t2[1][:, dc, nb2*512:(nb2+1)*512], start=False, stop=(dc == 7))
                        Tn.inc(ins)
                    add("Tn", t1mm)
                    def t1ev(S, dst=dst, oc=oc, nb2=nb2):
                        ps = psA if nb2 == 0 else psB
                        S.copy(dst[:, oc, nb2*512:(nb2+1)*512], ps[:])
                    add("S", t1ev)
        def reloadG(G):
            G.dma(Fr_sb[:], dGr.rearrange("(c p) f -> p c f", p=128), WTs)
            G.dma(Fi_sb[:], dGi.rearrange("(c p) f -> p c f", p=128), WTs)
        add("G", reloadG)
        for ncc in range(8):
            for nb2 in range(2):
                def mmm(Tn, ncc=ncc, nb2=nb2):
                    if ncc == 0 and nb2 == 0:
                        Tn.wait_stream(WTs, 8)
                    ps = psA if nb2 == 0 else psB
                    for oc in range(8):
                        Tn.raw("matmul", ps[:], T1r[:, oc, ncc*128:(ncc+1)*128],
                               dGr_sb[:, oc, nb2*512:(nb2+1)*512], start=(oc == 0), stop=False)
                    for oc in range(8):
                        ins = Tn.raw("matmul", ps[:], T1i[:, oc, ncc*128:(ncc+1)*128],
                                     dGi_sb[:, oc, nb2*512:(nb2+1)*512], start=False, stop=(oc == 7))
                    Tn.inc(ins)
                add("Tn", mmm)
                def mev(S, ncc=ncc, nb2=nb2):
                    S.copy(M_sb[:, ncc, nb2*512:(nb2+1)*512], ps_sel(nb2))
                add("S", mev)
        for i in range(NTILE):
            j = i % 2
            for fc in range(8):
                def fmm(Tn, i=i, j=j, fc=fc):
                    if fc == 0:
                        Tn.wait_stream(LDs[j], i // 2 + 1)
                    ps = psA if fc % 2 == 0 else psB
                    for dc in range(8):
                        ins = Tn.raw("matmul", ps[:], M_sb[:, dc, fc*128:(fc+1)*128],
                                     xnb[j][:, dc, :], start=(dc == 0), stop=(dc == 7))
                    Tn.inc(ins)
                    if fc == 7:
                        Tn.mark(("xnfree", i))
                add("Tn", fmm)
                def fev(V, i=i, j=j, fc=fc):
                    if fc == 0:
                        V.wait_stream(SFs[j], i // 2)
                    ps = psA if fc % 2 == 0 else psB
                    V.tensor_scalar(ft[j][:, fc, :], ps[:], bias_sb[:, fc:fc+1], None, op0=ALU.add)
                add("V", fev)
            def fstore(G, i=i, j=j):
                G.dma(fm_slice(ffn1, i*W, W), ft[j][:], SFs[j])
            add("G", fstore)

        dGr_sb = Fr_sb
        dGi_sb = Fi_sb
        def ps_sel(nb2):
            return (psA if nb2 == 0 else psB)[:]

        blk = e(nc.Block())
        @blk.sync
        def _(q_):
            Q = b.Q
            for i in range(NTILE):
                j = i % 2
                if i >= 2:
                    Q.wait(("xnfree", i - 2))
                Q.dma(xnb[j][:], fm_slice(xn1, i*W, W), LDs[j])
        @blk.gpsimd
        def _(g):
            G = b.G
            G.dma(Wr_sb[:], Wfr.rearrange("(c p) f -> p c f", p=128), WTs)
            G.dma(Wi_sb[:], Wfi.rearrange("(c p) f -> p c f", p=128), WTs)
            G.dma(Fr_sb[:], dFr.rearrange("(c p) f -> p c f", p=128), WTs)
            G.dma(Fi_sb[:], dFi.rearrange("(c p) f -> p c f", p=128), WTs)
            G.dma(FiN_sb[:], dFiN.rearrange("(c p) f -> p c f", p=128), WTs)
            G.dma(bias_sb[:], bias, WTs)
            run_chain(b, steps, G)
            for j in range(2):
                if SFs[j].n:
                    G.wait_stream(SFs[j], SFs[j].n)
        @blk.tensor
        def _(t):
            run_chain(b, steps, b.Tn)
        @blk.scalar
        def _(s):
            run_chain(b, steps, b.S)
        @blk.vector
        def _(v):
            run_chain(b, steps, b.V)
    return nc


def build_launch_K():
    return build_twice(_launch_K)



def _launch_L(b):
    nc = _mknc()
    x1 = _dram(nc, "x1_f", [D, NT], F32, "ExternalInput")
    s1 = _dram(nc, "s1_fm", [D, NT], F32, "ExternalInput")
    ffn1 = _dram(nc, "ffn1_fm", [D, NT], F32, "ExternalInput")
    out = _dram(nc, "out_fm", [D, NT], F32, "ExternalOutput")
    W = 512
    NTILE = NT // W
    with ExitStack() as stk:
        e = stk.enter_context
        b.make(nc, stk)
        xt = [e(_sbuf(nc, f"xt{j}", [128, 8, W], F32)) for j in range(2)]
        st_ = [e(_sbuf(nc, f"st{j}", [128, 8, W], F32)) for j in range(2)]
        mt = [e(_sbuf(nc, f"mt{j}", [128, 8, W], F32)) for j in range(2)]
        LXs = [b.stream(nc, stk, f"LX{j}") for j in range(2)]
        LSs = [b.stream(nc, stk, f"LS{j}") for j in range(2)]
        LMs = [b.stream(nc, stk, f"LM{j}") for j in range(2)]
        SOs = [b.stream(nc, stk, f"SO{j}") for j in range(2)]
        steps = []
        def add(en, fn):
            steps.append((en, fn))
        for i in range(NTILE):
            j = i % 2
            def gsilu(S, i=i, j=j):
                S.wait_stream(LMs[j], i // 2 + 1)
                S.activation(mt[j][:].rearrange("p c t -> p (c t)"),
                             mt[j][:].rearrange("p c t -> p (c t)"), AF.Silu)
            add("S", gsilu)
            def comb(V, i=i, j=j):
                V.wait_stream(LXs[j], i // 2 + 1)
                V.wait_stream(LSs[j], i // 2 + 1)
                if i >= 2:
                    V.wait_stream(SOs[j], i // 2)
                V.tensor_tensor(mt[j][:], st_[j][:], mt[j][:], op=ALU.mult)
                V.tensor_tensor(xt[j][:], xt[j][:], mt[j][:], op=ALU.add)
                V.mark(("odone", i))
            add("V", comb)
            def ostore(G, i=i, j=j):
                G.dma(fm_slice(out, i*W, W), xt[j][:], SOs[j])
            add("G", ostore)
        blk = e(nc.Block())
        @blk.sync
        def _(q_):
            Q = b.Q
            for i in range(NTILE):
                j = i % 2
                if i >= 2:
                    Q.wait(("odone", i - 2))
                    Q.wait_stream(SOs[j], i // 2)
                Q.dma(xt[j][:], fm_slice(x1, i*W, W), LXs[j])
                Q.dma(st_[j][:], fm_slice(s1, i*W, W), LSs[j])
                Q.dma(mt[j][:], fm_slice(ffn1, i*W, W), LMs[j])
        @blk.gpsimd
        def _(g):
            G = b.G
            run_chain(b, steps, G)
            for j in range(2):
                if SOs[j].n:
                    G.wait_stream(SOs[j], SOs[j].n)
        @blk.scalar
        def _(s):
            run_chain(b, steps, b.S)
        @blk.vector
        def _(v):
            run_chain(b, steps, b.V)
        @blk.tensor
        def _(t):
            pass
    return nc


def build_launch_L():
    return build_twice(_launch_L)



def _launch_ALL(b):
    global _FUSED_NC, _DRAM_CACHE
    nc = bass.Bass("TRN2", target_bir_lowering=False)
    _FUSED_NC = nc
    _DRAM_CACHE = {}
    try:
        for nm, fn in (("A", _launch_A), ("B", _launch_B), ("C", _launch_C),
                       ("D", _launch_D), ("E", _launch_E), ("F", _launch_F),
                       ("G", _launch_G), ("H", _launch_H), ("I", _launch_I),
                       ("J", _launch_J), ("K", _launch_K), ("L", _launch_L)):
            _PHASE[0] = nm + "_"
            b.prefix = nm
            fn(b)
            _barrier(nc, b)
    finally:
        _FUSED_NC = None
        _PHASE[0] = ""
        b.prefix = ""
    return nc


def build_launch_ALL():
    return build_twice(_launch_ALL)



def shard_inputs(inputs):
    c = host_consts()
    per_core = []
    x = np.asarray(inputs["L0_attn_x"])
    for core in range(NCORES):
        xs = x[4*core:4*core+4]
        x_fm = np.ascontiguousarray(xs.reshape(NT, D).T)
        per_core.append({"x_fm": x_fm})
    return per_core


def chunk_vec(v):
    return np.ascontiguousarray(np.asarray(v, np.float32).reshape(-1, 128).T)


def _bf(a):
    return np.asarray(a).astype(BF) if a.dtype != BF else a


def kernel(**inputs):
    c = host_consts()
    f32 = lambda k: np.ascontiguousarray(np.asarray(inputs[k], np.float32))
    shared = {
        "A": {"lnw": chunk_vec(inputs["L0_attn_ln_w"]), "lnb": chunk_vec(inputs["L0_attn_ln_b"]),
              "onesb": c["onesb"], "ones1": c["ones1"]},
        "B": {"Wq": f32("L0_attn_Wq"), "Wk": f32("L0_attn_Wk"), "Wv": f32("L0_attn_Wv"),
              "Wb": f32("L0_attn_Wb"), "onesb": c["onesb"], "ones1": c["ones1"], "id128": c["id128"]},
        "C": {"onw": chunk_vec(inputs["L0_attn_o_norm_w"]), "onesb": c["onesb"], "ones1": c["ones1"],
              "id128": c["id128"], "id128f": c["id128f"], "maskLd": c["maskLd"],
              "maskLoff": c["maskLoff"], "maskWuf": c["maskWuf"], "bdmaskf": c["bdmaskf"]},
        "D": {"Wo": f32("L0_attn_Wo")},
        "E": {"Wr": f32("L0_ffn_Wr"), "id128": c["id128"]},
        "F": {"Wg": f32("L0_ffn_Wg"), "Wu": f32("L0_ffn_Wu"), "Wd": f32("L0_ffn_Wd"),
              "Sg": f32("L0_ffn_Sg"), "Su": f32("L0_ffn_Su"), "Sd": f32("L0_ffn_Sd"),
              "sel16": c["sel16"]},
        "G": {"onesb": c["onesb"], "ones1": c["ones1"]},
        "H": {"Wq1": f32("L1_attn_Wq"), "Wqr": f32("L1_attn_Wqr"), "Wkr": f32("L1_attn_Wkr"),
              "Wkvd": f32("L1_attn_Wkvd"), "Wku": f32("L1_attn_Wku"), "Wvu": f32("L1_attn_Wvu"),
              "ropeC": c["ropeC"], "ropeS": c["ropeS"], "id128": c["id128"]},
        "I": {"lobo": np.ascontiguousarray(np.asarray(inputs["L1_attn_lobo_log"], np.float32).reshape(1, 16)),
              "ones1": c["ones1"], "id128": c["id128"], "maskdiag": c["maskdiag"]},
        "J": {"Wo1": f32("L1_attn_Wo")},
        "K": {"Wfr": f32("L1_ffn_w_freq_real"), "Wfi": f32("L1_ffn_w_freq_imag"),
              "bias": chunk_vec(inputs["L1_ffn_bias"]),
              "dftFr": c["dftFr"], "dftFi": c["dftFi"], "dftFiNeg": c["dftFiNeg"],
              "dftFrN": c["dftFrN"], "dftFiN": c["dftFiN"]},
        "L": {},
    }
    x = np.asarray(inputs["L0_attn_x"], np.float32)
    state = []
    for core in range(NCORES):
        xs = x[4*core:4*core+4]
        state.append({"x_fm": np.ascontiguousarray(xs.reshape(NT, D).T)})

    merged = {}
    for d_ in shared.values():
        merged.update(d_)
    nc = build_launch_ALL()
    in_maps = [dict(merged, x_fm=state[core]["x_fm"]) for core in range(NCORES)]
    res = run_bass_kernel_spmd(nc, in_maps, core_ids=list(range(NCORES)))
    for core in range(NCORES):
        state[core]["out_fm"] = np.asarray(res.results[core]["out_fm"])

    out = np.empty((32, T, D), np.float32)
    for core in range(NCORES):
        out[4*core:4*core+4] = state[core]["out_fm"].T.reshape(4, T, D)
    return out


# version 04
# speedup vs baseline: 1.0635x; 1.0635x over previous
import numpy as np
import ml_dtypes
from contextlib import ExitStack
import concourse.bass as bass
import concourse.mybir as mybir
from concourse.bass_utils import run_bass_kernel_spmd

F32 = mybir.dt.float32
BF16 = mybir.dt.bfloat16
AF = mybir.ActivationFunctionType
ALU = mybir.AluOpType
AX = mybir.AxisListType
BF = ml_dtypes.bfloat16

B = 4
T = 2048
D = 1024
NT = B * T
NCORES = 8


_CONSTS = None

def host_consts():
    global _CONSTS
    if _CONSTS is not None:
        return _CONSTS
    c = {}
    c["onesb"] = np.ones((128, 1), BF)
    c["ones1"] = np.ones((1, 128), BF)
    c["id128"] = np.eye(128, dtype=np.float32).astype(BF)
    idst = np.zeros((128, 32), np.float32)
    for m in range(4):
        idst[32*m:32*m+32, :] = np.eye(32, dtype=np.float32)
    c["idstack"] = idst
    c["maskL"] = np.tril(np.ones((128, 128), np.float32), -1)
    c["id128f"] = np.eye(128, dtype=np.float32)
    bd = np.zeros((128, 128), np.float32)
    for m in range(4):
        bd[32*m:32*m+32, 32*m:32*m+32] = 1.0
    c["bdmaskf"] = bd
    c["maskLd"] = c["maskL"] * bd
    c["maskLoff"] = c["maskL"] * (1 - bd)
    c["maskWuf"] = np.triu(np.ones((128, 128), np.float32))
    c["maskWu"] = np.triu(np.ones((128, 128), np.float32)).astype(BF)
    md = np.zeros((4, 128, 512), np.float32)
    for m in range(4):
        for p in range(128):
            md[m, p, 128*m + p:] = 1.0
    c["maskdiag"] = md.astype(BF)
    half = 16
    invf = 1.0 / 10000.0 ** (np.arange(half, dtype=np.float32) / half)
    ang = np.arange(T, dtype=np.float32)[:, None] * invf[None, :]
    cos1, sin1 = np.cos(ang).astype(np.float32), np.sin(ang).astype(np.float32)
    C1 = np.zeros((T, 512), np.float32)
    S1 = np.zeros((T, 512), np.float32)
    for h in range(16):
        C1[:, h*32:h*32+16] = cos1
        C1[:, h*32+16:h*32+32] = cos1
        S1[:, h*32:h*32+16] = -sin1
        S1[:, h*32+16:h*32+32] = sin1
    c["ropeC"] = C1
    c["ropeS"] = S1
    n_idx = np.arange(D)
    th = -2 * np.pi * np.outer(n_idx, n_idx) / D
    Fr = np.cos(th).astype(np.float32)
    Fi = np.sin(th).astype(np.float32)
    c["dftFr"] = Fr.astype(BF)
    c["dftFi"] = Fi.astype(BF)
    c["dftFrN"] = (Fr / D).astype(BF)
    c["dftFiN"] = (Fi / D).astype(BF)
    c["dftFiNeg"] = (-Fi).astype(BF)
    sel = np.zeros((16, 16 * 128), np.float32)
    for e_ in range(16):
        sel[e_, e_*128:(e_+1)*128] = 1.0
    c["sel16"] = sel.astype(BF)
    _CONSTS = c
    return c


def bc_mid(ap2, reps):
    return bass.AP(tensor=ap2.tensor, offset=ap2.offset,
                   ap=[list(ap2.ap[0]), [0, reps], list(ap2.ap[1])])

def fm_slice(dram_ap, t0, w):
    return dram_ap[:, t0:t0+w].rearrange("(c p) t -> p c t", p=128)


class Eng:
    _OPS = {"tensor_copy", "tensor_tensor", "tensor_add", "tensor_sub", "tensor_mul",
            "tensor_max", "tensor_scalar", "tensor_single_scalar", "tensor_scalar_add",
            "tensor_scalar_sub", "tensor_scalar_mul", "tensor_scalar_max", "tensor_scalar_min",
            "tensor_reduce", "reduce_max", "reduce_sum", "scalar_tensor_tensor", "memset",
            "memzero", "reciprocal", "select", "activation", "copy", "mul", "add", "sqrt",
            "square", "sign", "iota", "transpose", "stream_shuffle", "copy_predicated",
            "partition_broadcast", "partition_all_reduce", "tensor_tensor_scan", "bn_stats",
            "bn_aggr", "affine_select"}

    def __init__(self, bld, eng, sem, name, selfsync=True):
        self.b = bld
        self.e = eng
        self.sem = sem
        self.name = name
        self.n = 0
        self.selfsync = selfsync

    def __getattr__(self, attr):
        fn = getattr(self.e, attr)
        if attr not in Eng._OPS:
            return fn
        def call(*a, **kw):
            if self.selfsync and self.n > 0:
                self.e.wait_ge(self.sem, self.n)
            ins = fn(*a, **kw)
            self.n += 1
            ins.then_inc(self.sem, 1)
            return ins
        return call

    def raw(self, attr, *a, **kw):
        return getattr(self.e, attr)(*a, **kw)

    def inc(self, ins):
        self.n += 1
        ins.then_inc(self.sem, 1)
        return ins

    def selfwait(self):
        if self.n > 0:
            self.e.wait_ge(self.sem, self.n)

    def mark(self, key):
        self.b.record[self.b.key(key)] = (self.name, self.n)

    def wait(self, key):
        nm, n = self.b.marks.get(self.b.key(key), (None, 0))
        if nm is None:
            return
        if n > 0:
            self.e.wait_ge(self.b.sems[nm], n)

    def dma(self, out, in_, stream, **kw):
        if self.selfsync and self.n > 0:
            self.e.wait_ge(self.sem, self.n)
        ins = self.e.dma_start(out=out, in_=in_, **kw)
        sem, cnt = stream.sem, stream.bump()
        ins.then_inc(sem, 16)
        return ins

    def wait_stream(self, stream, k):
        if hasattr(stream, "absn"):
            k = stream.absn(k)
        self.e.wait_ge(stream.sem, 16 * k)


class Stream:
    def __init__(self, sem):
        self.sem = sem
        self.n = 0
    def bump(self):
        self.n += 1
        return self.n


class StreamView:
    def __init__(self, s):
        self.s = s
        self.base = s.n
    @property
    def sem(self):
        return self.s.sem
    @property
    def n(self):
        return self.s.n - self.base
    def bump(self):
        return self.s.bump()
    def absn(self, k):
        return self.base + k


class Builder:
    def __init__(self, marks=None):
        self.marks = marks if marks is not None else {}
        self.record = {}
        self.sems = {}
        self.k = 0
        self.prefix = ""
        self.streampool = {}
        self.gstack = ExitStack()
        self.made = False
        self.nbar = 0

    def key(self, k):
        return (self.prefix,) + (k if isinstance(k, tuple) else (k,))

    def step(self, eng):
        if self.k > 0:
            eng.wait(("ch", self.k))
        return eng

    def end(self, eng):
        self.k += 1
        eng.mark(("ch", self.k))

    def make(self, nc, stk):
        if self.made:
            return self
        e = self.gstack.enter_context
        self.sems = {nm: e(nc.semaphore(nm)) for nm in ("SEV", "SES", "SEG", "SET", "SEQ", "BAR")}
        self.V = Eng(self, nc.vector, self.sems["SEV"], "SEV")
        self.S = Eng(self, nc.scalar, self.sems["SES"], "SES")
        self.G = Eng(self, nc.gpsimd, self.sems["SEG"], "SEG")
        self.Tn = Eng(self, nc.tensor, self.sems["SET"], "SET", selfsync=False)
        self.Q = Eng(self, nc.sync, self.sems["SEQ"], "SEQ", selfsync=False)
        self.made = True
        return self

    def stream(self, nc, stk, name):
        if name not in self.streampool:
            sem = self.gstack.enter_context(nc.semaphore("st_" + name))
            self.streampool[name] = Stream(sem)
        return StreamView(self.streampool[name])


def build_twice(fn):
    b1 = Builder()
    fn(b1)
    rec = b1.record
    prefs = []
    for kk in rec:
        if len(kk) >= 2 and kk[-2] == "ch" and kk[0] not in prefs:
            prefs.append(kk[0])
    last = {}
    for pf in prefs:
        prev_eff = None
        k = 1
        while (pf, "ch", k) in rec:
            nm, n = rec[(pf, "ch", k)]
            if last.get(nm) == n:
                rec[(pf, "ch", k)] = prev_eff if prev_eff is not None else (nm, 0)
            else:
                last[nm] = n
                prev_eff = (nm, n)
            k += 1
    b2 = Builder(marks=rec)
    nc = fn(b2)
    return nc




def _launch_A(b):
    nc = _mknc()
    x = _dram(nc, "x_fm", [D, NT], F32, "ExternalInput")
    lnw = _dram(nc, "lnw", [128, 8], F32, "ExternalInput")
    lnb = _dram(nc, "lnb", [128, 8], F32, "ExternalInput")
    onesb = _dram(nc, "onesb", [128, 1], BF16, "ExternalInput")
    ones1 = _dram(nc, "ones1", [1, 128], BF16, "ExternalInput")
    xn_f = _dram(nc, "xn_f", [D, NT], F32, "ExternalOutput")
    xn_b = _dram(nc, "xn_b", [D, NT], BF16, "ExternalOutput")
    nrm_b = _dram(nc, "nrm_b", [D, NT], BF16, "ExternalOutput")
    W = 512
    NTILE = NT // W
    with ExitStack() as stk:
        e = stk.enter_context
        b.make(nc, stk)
        xt = [e(_sbuf(nc, f"xt{j}", [128, 8, W], F32)) for j in range(2)]
        xb = [e(_sbuf(nc, f"xb{j}", [128, 8, W], BF16)) for j in range(2)]
        xsq = [e(_sbuf(nc, f"xsq{j}", [128, 8, W], BF16)) for j in range(2)]
        xnt = [e(_sbuf(nc, f"xnt{j}", [128, 8, W], F32)) for j in range(2)]
        xnb = [e(_sbuf(nc, f"xnb{j}", [128, 8, W], BF16)) for j in range(2)]
        nrmt = [e(_sbuf(nc, f"nrmt{j}", [128, 8, W], BF16)) for j in range(2)]
        tmp = [e(_sbuf(nc, f"tmp{j}", [128, 8, W], F32)) for j in range(2)]
        rows = [e(_sbuf(nc, f"rows{j}", [1, 4, W], F32)) for j in range(2)]
        rows_b = [e(_sbuf(nc, f"rowsb{j}", [1, 4, W], BF16)) for j in range(2)]
        lnw_sb = e(_sbuf(nc, "lnw_sb", [128, 8], F32))
        lnb_sb = e(_sbuf(nc, "lnb_sb", [128, 8], F32))
        ones_sb = e(_sbuf(nc, "ones_sb", [128, 1], BF16))
        ones1_sb = e(_sbuf(nc, "ones1_sb", [1, 128], BF16))
        ps_mu = e(_psum(nc, "ps_mu", [1, W], F32))
        ps_s2 = e(_psum(nc, "ps_s2", [1, W], F32))
        ps_mu2 = e(_psum(nc, "ps_mu2", [1, W], F32))
        ps_s22 = e(_psum(nc, "ps_s22", [1, W], F32))
        ps_b1 = e(_psum(nc, "ps_b1", [128, W], F32))
        ps_b2 = e(_psum(nc, "ps_b2", [128, W], F32))
        ps_b3 = e(_psum(nc, "ps_b3", [128, W], F32))
        ps_b4 = e(_psum(nc, "ps_b4", [128, W], F32))
        LDs = [b.stream(nc, stk, f"LD{j}") for j in range(2)]
        CAs = [b.stream(nc, stk, f"CA{j}") for j in range(2)]
        SXs = [b.stream(nc, stk, f"SX{j}") for j in range(2)]
        SBs = [b.stream(nc, stk, f"SB{j}") for j in range(2)]
        SNs = [b.stream(nc, stk, f"SN{j}") for j in range(2)]
        L1s = b.stream(nc, stk, "L1")
        blk = e(nc.Block())

        @blk.sync
        def _(q):
            Q = b.Q
            for i in range(NTILE):
                j = i % 2
                if i >= 2:
                    Q.wait(("sq1", i - 2))
                    Q.wait(("apply1", i - 2))
                Q.dma(xt[j][:], fm_slice(x, i*W, W), LDs[j])

        @blk.gpsimd
        def _(g):
            G = b.G
            for i in range(NTILE):
                j = i % 2
                if i >= 2:
                    G.wait(("stats1", i - 2))
                G.dma(xb[j][:], fm_slice(x, i*W, W), CAs[j])
                G.wait(("apply1", i))
                if i >= 2:
                    G.wait(("stats2", i - 2))
                    G.wait_stream(SBs[j], i // 2)
                G.tensor_copy(xnb[j][:], xnt[j][:])
                G.mark(("cast", i))
                G.dma(fm_slice(xn_f, i*W, W), xnt[j][:], SXs[j])
                G.dma(fm_slice(xn_b, i*W, W), xnb[j][:], SBs[j])
                G.wait(("apply2", i))
                G.dma(fm_slice(nrm_b, i*W, W), nrmt[j][:], SNs[j])
            for j in range(2):
                G.wait_stream(SXs[j], NTILE // 2)
                G.wait_stream(SBs[j], NTILE // 2)
                G.wait_stream(SNs[j], NTILE // 2)

        @blk.scalar
        def _(s):
            S = b.S
            S.dma(lnw_sb[:], lnw, L1s)
            S.dma(lnb_sb[:], lnb, L1s)
            S.dma(ones_sb[:], onesb, L1s)
            S.dma(ones1_sb[:], ones1, L1s)
            for i in range(NTILE):
                j = i % 2
                S.wait_stream(LDs[j], i // 2 + 1)
                if i >= 2:
                    S.wait(("stats2", i - 2))
                S.activation(xsq[j][:].rearrange("p c t -> p (c t)"),
                             xt[j][:].rearrange("p c t -> p (c t)"), AF.Square)
                S.mark(("sq1", i))
                S.wait(("rows1a", i))
                S.activation(rows[j][:, 1, :], rows[j][:, 1, :], AF.Ln)
                S.activation(rows_b[j][:, 0, :], rows[j][:, 1, :], AF.Exp, scale=-0.5)
                S.mark(("rstd1", i))
                S.wait(("apply1", i))
                S.activation(xsq[j][:].rearrange("p c t -> p (c t)"),
                             xnt[j][:].rearrange("p c t -> p (c t)"), AF.Square)
                S.mark(("sq2", i))
                S.wait(("rows2a", i))
                S.activation(rows[j][:, 3, :], rows[j][:, 3, :], AF.Ln)
                S.activation(rows_b[j][:, 2, :], rows[j][:, 3, :], AF.Exp, scale=-0.5)
                S.mark(("rstd2", i))

        @blk.tensor
        def _(t):
            Tn = b.Tn
            Tn.wait_stream(L1s, 4)
            for i in range(NTILE):
                j = i % 2
                Tn.wait_stream(CAs[j], i // 2 + 1)
                Tn.wait(("sq1", i))
                if i >= 1:
                    Tn.wait(("rows1a", i - 1))
                for c in range(8):
                    ins = Tn.raw("matmul", ps_mu[:], ones_sb[:], xb[j][:, c, :],
                                 start=(c == 0), stop=(c == 7))
                Tn.inc(ins)
                for c in range(8):
                    ins = Tn.raw("matmul", ps_s2[:], ones_sb[:], xsq[j][:, c, :],
                                 start=(c == 0), stop=(c == 7))
                Tn.inc(ins)
                Tn.mark(("stats1", i))
                Tn.wait(("w1b", i))
                if i >= 1:
                    Tn.wait(("apply1", i - 1))
                Tn.inc(Tn.raw("matmul", ps_b1[:], ones1_sb[:], rows_b[j][:, 0, :], start=True, stop=True))
                Tn.inc(Tn.raw("matmul", ps_b2[:], ones1_sb[:], rows_b[j][:, 1, :], start=True, stop=True))
                Tn.mark(("bcast1", i))
                Tn.wait(("cast", i))
                Tn.wait(("sq2", i))
                if i >= 1:
                    Tn.wait(("rows2a", i - 1))
                for c in range(8):
                    ins = Tn.raw("matmul", ps_mu2[:], ones_sb[:], xnb[j][:, c, :],
                                 start=(c == 0), stop=(c == 7))
                Tn.inc(ins)
                for c in range(8):
                    ins = Tn.raw("matmul", ps_s22[:], ones_sb[:], xsq[j][:, c, :],
                                 start=(c == 0), stop=(c == 7))
                Tn.inc(ins)
                Tn.mark(("stats2", i))
                Tn.wait(("w2b", i))
                if i >= 1:
                    Tn.wait(("apply2", i - 1))
                Tn.inc(Tn.raw("matmul", ps_b3[:], ones1_sb[:], rows_b[j][:, 2, :], start=True, stop=True))
                Tn.inc(Tn.raw("matmul", ps_b4[:], ones1_sb[:], rows_b[j][:, 3, :], start=True, stop=True))
                Tn.mark(("bcast2", i))

        @blk.vector
        def _(v):
            V = b.V
            V.wait_stream(L1s, 4)
            for i in range(NTILE):
                j = i % 2
                V.wait(("stats1", i))
                V.tensor_scalar(rows[j][:, 0, :], ps_mu[:], 1.0 / D, None, op0=ALU.mult)
                V.tensor_tensor(rows[j][:, 1, :], rows[j][:, 0, :], rows[j][:, 0, :], op=ALU.mult)
                V.tensor_scalar(rows[j][:, 1, :], rows[j][:, 1, :], -1.0, 1e-5, op0=ALU.mult, op1=ALU.add)
                V.scalar_tensor_tensor(rows[j][:, 1, :], ps_s2[:], 1.0 / D, rows[j][:, 1, :],
                                       op0=ALU.mult, op1=ALU.add)
                V.mark(("rows1a", i))
                V.wait(("rstd1", i))
                V.tensor_tensor(rows_b[j][:, 1, :], rows[j][:, 0, :], rows_b[j][:, 0, :], op=ALU.mult)
                V.mark(("w1b", i))
                V.wait(("bcast1", i))
                if i >= 2:
                    V.wait_stream(SXs[j], i // 2)
                    V.wait(("cast", i - 2))
                    V.wait(("sq2", i - 2))
                V.tensor_tensor(xnt[j][:], xt[j][:], bc_mid(ps_b1[:], 8), op=ALU.mult)
                V.tensor_tensor(xnt[j][:], xnt[j][:], bc_mid(ps_b2[:], 8), op=ALU.subtract)
                V.mark(("apply1", i))
                V.wait(("stats2", i))
                V.tensor_scalar(rows[j][:, 2, :], ps_mu2[:], 1.0 / D, None, op0=ALU.mult)
                V.tensor_tensor(rows[j][:, 3, :], rows[j][:, 2, :], rows[j][:, 2, :], op=ALU.mult)
                V.tensor_scalar(rows[j][:, 3, :], rows[j][:, 3, :], -1.0, 1e-5, op0=ALU.mult, op1=ALU.add)
                V.scalar_tensor_tensor(rows[j][:, 3, :], ps_s22[:], 1.0 / D, rows[j][:, 3, :],
                                       op0=ALU.mult, op1=ALU.add)
                V.mark(("rows2a", i))
                V.wait(("rstd2", i))
                V.tensor_tensor(rows_b[j][:, 3, :], rows[j][:, 2, :], rows_b[j][:, 2, :], op=ALU.mult)
                V.mark(("w2b", i))
                V.wait(("bcast2", i))
                if i >= 2:
                    V.wait_stream(SNs[j], i // 2)
                V.tensor_tensor(tmp[j][:], xnt[j][:], bc_mid(ps_b3[:], 8), op=ALU.mult)
                V.tensor_tensor(tmp[j][:], tmp[j][:], bc_mid(ps_b4[:], 8), op=ALU.subtract)
                for c in range(8):
                    V.tensor_scalar(nrmt[j][:, c, :], tmp[j][:, c, :],
                                    lnw_sb[:, c:c+1], lnb_sb[:, c:c+1],
                                    op0=ALU.mult, op1=ALU.add)
                V.mark(("apply2", i))
    return nc


def build_launch_A():
    return build_twice(_launch_A)





_FUSED_NC = None
_DRAM_CACHE = {}
_PHASE = [""]

_FUSED_OUTPUTS = {"out_fm"}
_FUSED_INTERNAL = {"xn_f", "xn_b", "nrm_b", "qf", "kf", "kt", "vt", "beta_t", "o_n",
                   "s_fm", "cwT", "moe_fm", "x1_f", "xn1_f", "xn1_b", "q96", "k96",
                   "vaug", "attn", "s1_fm", "ffn1_fm", "mscr"}

def _mknc():
    if _FUSED_NC is not None:
        return _FUSED_NC
    return bass.Bass("TRN2", target_bir_lowering=False)

def _dram(nc, name, shape, dtype, kind):
    if _FUSED_NC is not None:
        if name in _DRAM_CACHE:
            return _DRAM_CACHE[name]
        k = "Internal" if name in _FUSED_INTERNAL else ("ExternalOutput" if name in _FUSED_OUTPUTS else "ExternalInput")
        t = nc.dram_tensor(name, shape, dtype, kind=k).ap()
        _DRAM_CACHE[name] = t
        return t
    return nc.dram_tensor(name, shape, dtype, kind=kind).ap()

def _sbuf(nc, name, shape, dtype):
    return nc.sbuf_tensor(_PHASE[0] + name, shape, dtype)

def _psum(nc, name, shape, dtype=F32):
    return nc.psum_tensor(_PHASE[0] + name, shape, dtype)

def _barrier(nc, bld):
    bld.nbar += 1
    k = bld.nbar
    with nc.Block() as blk:
        @blk.sync
        def _(q_):
            q_.sem_inc(bld.sems["BAR"], 1)
            q_.wait_ge(bld.sems["BAR"], 5 * k)
        @blk.gpsimd
        def _(g):
            g.drain()
            g.sem_inc(bld.sems["BAR"], 1)
            g.wait_ge(bld.sems["BAR"], 5 * k)
        @blk.tensor
        def _(t):
            t.drain()
            t.sem_inc(bld.sems["BAR"], 1)
            t.wait_ge(bld.sems["BAR"], 5 * k)
        @blk.scalar
        def _(s):
            s.drain()
            s.sem_inc(bld.sems["BAR"], 1)
            s.wait_ge(bld.sems["BAR"], 5 * k)
        @blk.vector
        def _(v):
            v.drain()
            v.sem_inc(bld.sems["BAR"], 1)
            v.wait_ge(bld.sems["BAR"], 5 * k)


def run_chain(b, steps, eng):
    for idx, step in enumerate(steps):
        ename, fn = step[0], step[1]
        deps = step[2] if len(step) > 2 else (idx,)
        if ename != ENG_OF[eng.name]:
            continue
        for dep in deps:
            k = dep
            while k > 0 and b.marks.get(b.key(("ch", k)), ("x", 0))[1] == 0:
                k -= 1
            if k > 0:
                eng.wait(("ch", k))
        fn(eng)
        eng.mark(("ch", idx + 1))


ENG_OF = {"SEV": "V", "SES": "S", "SEG": "G", "SET": "Tn", "SEQ": "Q"}


import math

def _launch_B(b):
    nc = _mknc()
    nrm = _dram(nc, "nrm_b", [D, NT], BF16, "ExternalInput")
    Wq = _dram(nc, "Wq", [D, 1024], F32, "ExternalInput")
    Wk = _dram(nc, "Wk", [D, 1024], F32, "ExternalInput")
    Wv = _dram(nc, "Wv", [D, 2048], F32, "ExternalInput")
    Wb = _dram(nc, "Wb", [D, 4], F32, "ExternalInput")
    onesb = _dram(nc, "onesb", [128, 1], BF16, "ExternalInput")
    ones1 = _dram(nc, "ones1", [1, 128], BF16, "ExternalInput")
    id128 = _dram(nc, "id128", [128, 128], BF16, "ExternalInput")
    qf = _dram(nc, "qf", [B, 4, 256, T], BF16, "ExternalOutput")
    kf = _dram(nc, "kf", [B, 4, 256, T], BF16, "ExternalOutput")
    kt = _dram(nc, "kt", [B, 4, T, 256], BF16, "ExternalOutput")
    vt = _dram(nc, "vt", [NT, 2048], BF16, "ExternalOutput")
    beta_t = _dram(nc, "beta_t", [NT, 4], F32, "ExternalOutput")
    W = 512
    NTILE = NT // W
    with ExitStack() as stk:
        e = stk.enter_context
        b.make(nc, stk)
        nb = [e(_sbuf(nc, f"nb{j}", [128, 8, W], BF16)) for j in range(2)]
        Wq_sb = e(_sbuf(nc, "Wq_sb", [128, 8, 1024], BF16))
        Wk_sb = e(_sbuf(nc, "Wk_sb", [128, 8, 1024], BF16))
        Wv_sb = e(_sbuf(nc, "Wv_sb", [128, 8, 2048], BF16))
        Wb_sb = e(_sbuf(nc, "Wb_sb", [128, 8, 4], BF16))
        ones_sb = e(_sbuf(nc, "ones_sb", [128, 1], BF16))
        ones1_sb = e(_sbuf(nc, "ones1_sb", [1, 128], BF16))
        id_sb = e(_sbuf(nc, "id_sb", [128, 128], BF16))
        qsil = e(_sbuf(nc, "qsil", [128, 8, W], F32))
        qsq = e(_sbuf(nc, "qsq", [128, 8, W], BF16))
        qout = [e(_sbuf(nc, f"qout{j}", [128, 8, W], BF16)) for j in range(2)]
        kout = [e(_sbuf(nc, f"kout{j}", [128, 8, W], BF16)) for j in range(2)]
        ktt = [e(_sbuf(nc, f"ktt{j}", [128, 8, 4, 128], BF16)) for j in range(2)]
        vout = [e(_sbuf(nc, f"vout{j}", [128, 4, 4, W], BF16)) for j in range(2)]
        bout = [e(_sbuf(nc, f"bout{j}", [128, 4, 4], F32)) for j in range(2)]
        rowf = e(_sbuf(nc, "rowf", [1, 2048], F32))
        row_b = e(_sbuf(nc, "row_b", [1, 2048], BF16))
        psA = e(_psum(nc, "psA", [128, W], F32))
        ps_row = e(_psum(nc, "ps_row", [1, 2048], F32))
        ps_bc = [e(_psum(nc, f"ps_bc{jj}", [128, W], F32)) for jj in range(2)]
        ps_tp = e(_psum(nc, "ps_tp", [128, 4, 128], BF16))
        LDs = [b.stream(nc, stk, f"LD{j}") for j in range(2)]
        WTs = b.stream(nc, stk, "WT")
        SQs = [b.stream(nc, stk, f"SQ{j}") for j in range(2)]
        SKs = [b.stream(nc, stk, f"SK{j}") for j in range(2)]
        STs = [b.stream(nc, stk, f"STT{j}") for j in range(2)]
        SVs = [b.stream(nc, stk, f"SV{j}") for j in range(2)]
        SBs = [b.stream(nc, stk, f"SBB{j}") for j in range(2)]

        steps = []
        def add(ename, fn):
            steps.append((ename, fn))

        for i in range(NTILE):
            j = i % 2
            bb = i // 4
            tl = (i % 4) * W
            for wq, Wsb, outb, biasc, statk in ((1, Wq_sb, qout, -0.5*math.log(256.0), "q"),
                                                (0, Wk_sb, kout, 0.0, "k")):
                for fc in range(8):
                    def mmstep(Tn, j=j, i=i, Wsb=Wsb, fc=fc, wq=wq):
                        if fc == 0 and wq == 1:
                            Tn.wait_stream(WTs, 7)
                            Tn.wait_stream(LDs[j], i // 2 + 1)
                        for dc in range(8):
                            ins = Tn.raw("matmul", psA[:], Wsb[:, dc, fc*128:(fc+1)*128],
                                         nb[j][:, dc, :], start=(dc == 0), stop=(dc == 7))
                        Tn.inc(ins)
                    add("Tn", mmstep)
                    def silustep(S, fc=fc):
                        S.activation(qsil[:, fc, :], psA[:], AF.Silu)
                        S.activation(qsq[:, fc, :], qsil[:, fc, :], AF.Square)
                    add("S", silustep)
                def statstep(Tn):
                    for h in range(4):
                        for dkc in range(2):
                            ins = Tn.raw("matmul", ps_row[:, h*512:(h+1)*512], ones_sb[:],
                                         qsq[:, h*2+dkc, :], start=(dkc == 0), stop=(dkc == 1))
                    Tn.inc(ins)
                add("Tn", statstep)
                def roweps(V):
                    V.tensor_scalar(rowf[:], ps_row[:], 1e-6, None, op0=ALU.add)
                add("V", roweps)
                def rowstep(S):
                    S.activation(rowf[:], rowf[:], AF.Ln)
                    S.activation(row_b[:], rowf[:], AF.Exp, scale=-0.5)
                add("S", rowstep)
                for h in range(4):
                    def bcstep(Tn, h=h):
                        ins = Tn.raw("matmul", ps_bc[h % 2][:], ones1_sb[:],
                                     row_b[:, h*512:(h+1)*512], start=True, stop=True)
                        Tn.inc(ins)
                    add("Tn", bcstep)
                    def appstep(V, h=h, j=j, i=i, outb=outb, statk=statk, wq=wq):
                        if h == 0:
                            V.wait_stream((SQs if statk == "q" else SKs)[j], i // 2)
                        sc = 256.0 ** -0.5 if wq else 1.0
                        V.scalar_tensor_tensor(outb[j][:, 2*h, :], qsil[:, 2*h, :], sc,
                                               ps_bc[h % 2][:], op0=ALU.mult, op1=ALU.mult)
                        V.scalar_tensor_tensor(outb[j][:, 2*h+1, :], qsil[:, 2*h+1, :], sc,
                                               ps_bc[h % 2][:], op0=ALU.mult, op1=ALU.mult)
                    add("V", appstep)
                if wq:
                    def qstore(G, j=j, i=i, bb=bb, tl=tl):
                        G.dma(qf[bb].rearrange("h (c p) t -> p (h c) t", p=128)[:, :, tl:tl+W],
                              qout[j][:], SQs[j])
                    add("G", qstore)
                else:
                    def kstore(G, j=j, i=i, bb=bb, tl=tl):
                        G.dma(kf[bb].rearrange("h (c p) t -> p (h c) t", p=128)[:, :, tl:tl+W],
                              kout[j][:], SKs[j])
                    add("G", kstore)
            for fc in range(8):
                def tpstep(Tn, fc=fc, j=j, i=i):
                    if fc == 0:
                        Tn.wait_stream(STs[j], 8 * (i // 2))
                    for tt in range(4):
                        ins = Tn.raw("transpose", ps_tp[:, tt, :], kout[j][:, fc, tt*128:(tt+1)*128],
                                     id_sb[:])
                    Tn.inc(ins)
                add("Tn", tpstep)
                def tpcopy(V, fc=fc, j=j):
                    V.tensor_copy(ktt[j][:, fc, :, :], ps_tp[:])
                add("V", tpcopy)
            def ktstore(G, j=j, bb=bb, tl=tl):
                for fc in range(8):
                    h, dkc = fc // 2, fc % 2
                    G.dma(kt[bb][h, tl:tl+W, dkc*128:(dkc+1)*128].rearrange("(tt p) dd -> p tt dd", p=128),
                          ktt[j][:, fc, :, :], STs[j])
            add("G", ktstore)
            for gg in range(16):
                def vmm(Tn, gg=gg, j=j, i=i):
                    if gg == 0:
                        Tn.wait_stream(SVs[j], i // 2)
                    tt, vf = gg // 4, gg % 4
                    for dc in range(8):
                        ins = Tn.raw("matmul", psA[:], nb[j][:, dc, tt*128:(tt+1)*128],
                                     Wv_sb[:, dc, vf*512:(vf+1)*512], start=(dc == 0), stop=(dc == 7))
                    Tn.inc(ins)
                add("Tn", vmm)
                def vsil(S, gg=gg, j=j):
                    tt, vf = gg // 4, gg % 4
                    S.activation(vout[j][:, tt, vf, :], psA[:], AF.Silu)
                add("S", vsil)
            def vstore(G, j=j, i=i):
                G.dma(vt[i*W:(i+1)*W, :].rearrange("(tt p) f -> p tt f", p=128),
                      vout[j][:].rearrange("p tt vf t -> p tt (vf t)"), SVs[j])
            add("G", vstore)
            for tt in range(4):
                def bmm(Tn, tt=tt, j=j, i=i):
                    if tt == 0:
                        Tn.wait_stream(SBs[j], i // 2)
                    for dc in range(8):
                        ins = Tn.raw("matmul", ps_bc[tt % 2][:, 0:4], nb[j][:, dc, tt*128:(tt+1)*128],
                                     Wb_sb[:, dc, :], start=(dc == 0), stop=(dc == 7))
                    Tn.inc(ins)
                    if tt == 3:
                        Tn.mark(("nbfree", i))
                add("Tn", bmm)
                def bsig(S, tt=tt, j=j):
                    S.activation(bout[j][:, tt, :], ps_bc[tt % 2][:, 0:4], AF.Sigmoid)
                add("S", bsig)
            def bstore(G, j=j, i=i):
                G.dma(beta_t[i*W:(i+1)*W, :].rearrange("(tt p) h -> p tt h", p=128),
                      bout[j][:], SBs[j])
            add("G", bstore)

        blk = e(nc.Block())

        @blk.sync
        def _(q_):
            Q = b.Q
            for i in range(NTILE):
                j = i % 2
                if i >= 2:
                    Q.wait(("nbfree", i - 2))
                Q.dma(nb[j][:], fm_slice(nrm, i*W, W), LDs[j])

        @blk.gpsimd
        def _(g):
            G = b.G
            G.dma(Wq_sb[:], Wq.rearrange("(c p) f -> p c f", p=128), WTs)
            G.dma(Wk_sb[:], Wk.rearrange("(c p) f -> p c f", p=128), WTs)
            G.dma(Wv_sb[:], Wv.rearrange("(c p) f -> p c f", p=128), WTs)
            G.dma(Wb_sb[:], Wb.rearrange("(c p) f -> p c f", p=128), WTs)
            G.dma(ones_sb[:], onesb, WTs)
            G.dma(ones1_sb[:], ones1, WTs)
            G.dma(id_sb[:], id128, WTs)
            run_chain(b, steps, G)
            for j in range(2):
                for s_ in (SQs[j], SKs[j], STs[j], SVs[j], SBs[j]):
                    if s_.n:
                        G.wait_stream(s_, s_.n)

        @blk.tensor
        def _(t):
            run_chain(b, steps, b.Tn)

        @blk.scalar
        def _(s):
            run_chain(b, steps, b.S)

        @blk.vector
        def _(v):
            run_chain(b, steps, b.V)
    return nc


def build_launch_B():
    return build_twice(_launch_B)



def _launch_C(b):
    nc = _mknc()
    qf = _dram(nc, "qf", [B, 4, 256, T], BF16, "ExternalInput")
    kf = _dram(nc, "kf", [B, 4, 256, T], BF16, "ExternalInput")
    kt = _dram(nc, "kt", [B, 4, T, 256], BF16, "ExternalInput")
    vt = _dram(nc, "vt", [NT, 2048], BF16, "ExternalInput")
    beta_t = _dram(nc, "beta_t", [NT, 4], F32, "ExternalInput")
    onw = _dram(nc, "onw", [128, 4], F32, "ExternalInput")
    onesb = _dram(nc, "onesb", [128, 1], BF16, "ExternalInput")
    ones1 = _dram(nc, "ones1", [1, 128], BF16, "ExternalInput")
    id128 = _dram(nc, "id128", [128, 128], BF16, "ExternalInput")
    id128f = _dram(nc, "id128f", [128, 128], F32, "ExternalInput")
    maskLd = _dram(nc, "maskLd", [128, 128], F32, "ExternalInput")
    maskLoff = _dram(nc, "maskLoff", [128, 128], F32, "ExternalInput")
    maskWuf = _dram(nc, "maskWuf", [128, 128], F32, "ExternalInput")
    bdmaskf = _dram(nc, "bdmaskf", [128, 128], F32, "ExternalInput")
    o_n = _dram(nc, "o_n", [B, 4, 512, T], BF16, "ExternalOutput")

    NCH = T // 128
    with ExitStack() as stk:
        e = stk.enter_context
        b.make(nc, stk)
        qf_t = [e(_sbuf(nc, f"qf_t{j}", [128, 2, 128], BF16)) for j in range(2)]
        kf_t = [e(_sbuf(nc, f"kf_t{j}", [128, 2, 128], BF16)) for j in range(2)]
        kt_t = [e(_sbuf(nc, f"kt_t{j}", [128, 256], BF16)) for j in range(2)]
        v_t = [e(_sbuf(nc, f"v_t{j}", [128, 512], BF16)) for j in range(2)]
        bt_t = [e(_sbuf(nc, f"bt_t{j}", [128, 4], F32)) for j in range(2)]
        S_all = e(_sbuf(nc, "S_all", [128, 16, 2, 512], F32))
        S_bf = e(_sbuf(nc, "S_bf", [128, 2, 512], BF16))
        sol = {nm: e(_sbuf(nc, f"sol_{nm}", [128, 128], BF16))
               for nm in ("Ld", "Loff", "Ud", "U2", "L2", "U4", "L4", "U8", "L8",
                          "IpL2", "IpL4", "IpL8", "IpL16", "Vc", "Xd", "No", "E",
                          "ET", "ET2", "ET3", "Zdn", "ZT")}
        R_sb = e(_sbuf(nc, "R_sb", [128, 512], BF16))
        Rtmp = e(_sbuf(nc, "Rtmp", [128, 512], F32))
        D_sb = e(_sbuf(nc, "D_sb", [128, 512], BF16))
        W_sb = e(_sbuf(nc, "W_sb", [128, 128], BF16))
        o_f = e(_sbuf(nc, "o_f", [128, 512], F32))
        osq = e(_sbuf(nc, "osq", [128, 512], BF16))
        o_out = [e(_sbuf(nc, f"o_out{j}", [128, 4, 128], BF16)) for j in range(2)]
        rowo = e(_sbuf(nc, "rowo", [1, 128], F32))
        rowo_b = e(_sbuf(nc, "rowo_b", [1, 128], BF16))
        onw_sb = e(_sbuf(nc, "onw_sb", [128, 4], F32))
        cst = {}
        for nm, dt_ in (("onesb", BF16), ("ones1", BF16), ("id128", BF16), ("id128f", F32),
                        ("maskLd", F32), ("maskLoff", F32), ("maskWuf", F32), ("bdmaskf", F32)):
            shp = [128, 1] if nm == "onesb" else ([1, 128] if nm == "ones1" else [128, 128])
            cst[nm] = e(_sbuf(nc, f"cst_{nm}", shp, dt_))
        psA = e(_psum(nc, "psA", [128, 128], F32))
        psTP = e(_psum(nc, "psTP", [128, 128], BF16))
        psMM = e(_psum(nc, "psMM", [128, 128], F32))
        psW = e(_psum(nc, "psW", [128, 128], F32))
        psBig = e(_psum(nc, "psBig", [128, 512], F32))
        psS = e(_psum(nc, "psS", [128, 2, 512], F32))
        LDs = [b.stream(nc, stk, f"LD{j}") for j in range(2)]
        BTs = [b.stream(nc, stk, f"BT{j}") for j in range(2)]
        WTs = b.stream(nc, stk, "WT")
        SOs = [b.stream(nc, stk, f"SO{j}") for j in range(2)]

        steps = []
        def add(ename, fn):
            steps.append((ename, fn))

        ldcnt = {}
        nld = 0
        for ci in range(4 * 4 * NCH):
            nld += 4
            ldcnt[ci] = nld

        def inittp(Tn):
            Tn.inc(Tn.raw("transpose", psTP[:], cst["id128"][:], cst["id128"][:]))
        add("Tn", inittp)

        ci = -1
        for ct in range(NCH):
            for bb_ in range(B):
                for h in range(4):
                    ci += 1
                    j = ci % 2
                    jb = (ci // 4) % 2
                    bh = bb_ * 4 + h
                    first = (ct == 0)

                    def sbfcast(S, bh=bh):
                        S.copy(S_bf[:].rearrange("p c t -> p (c t)"),
                               S_all[:, bh, :, :].rearrange("p c t -> p (c t)"))
                    if not first:
                        add("S", sbfcast)

                    def amm(Tn, j=j, ci=ci, jb=jb, h=h):
                        Tn.wait_stream(LDs[j], (ldcnt[ci] + 1) // 2 if False else (ci // 2 + 1) * 4)
                        if h == 0:
                            Tn.wait_stream(BTs[jb], ci // 8 + 1)
                        for c in range(2):
                            ins = Tn.raw("matmul", psA[:], kf_t[j][:, c, :], kf_t[j][:, c, :],
                                         start=(c == 0), stop=(c == 1))
                        Tn.inc(ins)
                    add("Tn", amm)

                    def lbuild(V, j=j, jb=jb, h=h):
                        V.scalar_tensor_tensor(sol["Ld"][:], psA[:], bt_t[jb][:, h:h+1],
                                               cst["maskLd"][:], op0=ALU.mult, op1=ALU.mult)
                        V.scalar_tensor_tensor(sol["Loff"][:], psA[:], bt_t[jb][:, h:h+1],
                                               cst["maskLoff"][:], op0=ALU.mult, op1=ALU.mult)
                    add("V", lbuild)

                    def udtp(Tn):
                        for m in range(4):
                            ins = Tn.raw("transpose", psTP[32*m:32*m+32, 32*m:32*m+32],
                                         sol["Ld"][32*m:32*m+32, 32*m:32*m+32],
                                         cst["id128"][32*m:32*m+32, 32*m:32*m+32],
                                         tile_position=(32*m, 32*m))
                        Tn.inc(ins)
                    add("Tn", udtp)
                    def udev(V):
                        V.tensor_tensor(sol["Ud"][:], psTP[:], cst["bdmaskf"][:], op=ALU.mult)
                    add("V", udev)

                    for (dst_u, dst_l, src_l, src_u) in (("U2", "L2", "Ld", "Ud"),
                                                         ("U4", "L4", "L2", "U2"),
                                                         ("U8", "L8", "L4", "U4")):
                        def pu(Tn, src_l=src_l, src_u=src_u):
                            Tn.inc(Tn.raw("matmul", psMM[:], sol[src_l][:], sol[src_u][:],
                                          start=True, stop=True))
                        add("Tn", pu)
                        def puev(V, dst_u=dst_u):
                            V.tensor_copy(sol[dst_u][:], psMM[:])
                        add("V", puev)
                        def pl(Tn, src_l=src_l, src_u=src_u):
                            Tn.inc(Tn.raw("matmul", psMM[:], sol[src_u][:], sol[src_l][:],
                                          start=True, stop=True))
                        add("Tn", pl)
                        def plev(V, dst_l=dst_l):
                            V.tensor_copy(sol[dst_l][:], psMM[:])
                            V.tensor_tensor(sol["IpL" + dst_l[1:]][:], psMM[:], cst["id128f"][:],
                                            op=ALU.add)
                        add("V", plev)
                    def l16(Tn):
                        Tn.inc(Tn.raw("matmul", psMM[:], sol["U8"][:], sol["L8"][:],
                                      start=True, stop=True))
                    add("Tn", l16)
                    def l16ev(V):
                        V.tensor_tensor(sol["IpL16"][:], psMM[:], cst["id128f"][:], op=ALU.add)
                        V.scalar_tensor_tensor(sol["Vc"][:], sol["Ud"][:], -1.0, cst["id128f"][:],
                                               op0=ALU.mult, op1=ALU.add)
                    add("V", l16ev)
                    for p in ("IpL2", "IpL4", "IpL8", "IpL16"):
                        def zc(Tn, p=p):
                            Tn.inc(Tn.raw("matmul", psMM[:], sol[p][:], sol["Vc"][:],
                                          start=True, stop=True))
                        add("Tn", zc)
                        def zcev(V):
                            V.tensor_copy(sol["Vc"][:], psMM[:])
                        add("V", zcev)
                    def xdtp(Tn):
                        for m in range(4):
                            ins = Tn.raw("transpose", psTP[32*m:32*m+32, 32*m:32*m+32],
                                         sol["Vc"][32*m:32*m+32, 32*m:32*m+32],
                                         cst["id128"][32*m:32*m+32, 32*m:32*m+32],
                                         tile_position=(32*m, 32*m))
                        Tn.inc(ins)
                    add("Tn", xdtp)
                    def xdev(V):
                        V.tensor_tensor(sol["Xd"][:], psTP[:], cst["bdmaskf"][:], op=ALU.mult)
                    add("V", xdev)
                    def notp(Tn):
                        Tn.inc(Tn.raw("transpose", psTP[:], sol["Loff"][:], cst["id128"][:]))
                    add("Tn", notp)
                    def noev(V):
                        V.tensor_copy(sol["No"][:], psTP[:])
                    add("V", noev)
                    def emm(Tn):
                        Tn.inc(Tn.raw("matmul", psMM[:], sol["Xd"][:], sol["No"][:],
                                      start=True, stop=True))
                    add("Tn", emm)
                    def eev(V):
                        V.tensor_copy(sol["E"][:], psMM[:])
                    add("V", eev)
                    def ettp(Tn):
                        Tn.inc(Tn.raw("transpose", psTP[:], sol["E"][:], cst["id128"][:]))
                    add("Tn", ettp)
                    def etev(V):
                        V.tensor_copy(sol["ET"][:], psTP[:])
                    add("V", etev)
                    def et2mm(Tn):
                        Tn.inc(Tn.raw("matmul", psMM[:], sol["E"][:], sol["ET"][:],
                                      start=True, stop=True))
                    add("Tn", et2mm)
                    def et2ev(V):
                        V.tensor_copy(sol["ET2"][:], psMM[:])
                    add("V", et2ev)
                    def et3mm(Tn):
                        Tn.inc(Tn.raw("matmul", psMM[:], sol["E"][:], sol["ET2"][:],
                                      start=True, stop=True))
                    add("Tn", et3mm)
                    def et3ev(V):
                        V.tensor_copy(sol["ET3"][:], psMM[:])
                        V.tensor_scalar(sol["Zdn"][:], sol["Vc"][:], -1.0, None, op0=ALU.mult)
                    add("V", et3ev)
                    def ztmm(Tn):
                        Tn.raw("matmul", psMM[:], cst["id128"][:], sol["Vc"][:], start=True, stop=False)
                        Tn.raw("matmul", psMM[:], sol["ET"][:], sol["Zdn"][:], start=False, stop=False)
                        Tn.raw("matmul", psMM[:], sol["ET2"][:], sol["Vc"][:], start=False, stop=False)
                        Tn.inc(Tn.raw("matmul", psMM[:], sol["ET3"][:], sol["Zdn"][:],
                                      start=False, stop=True))
                    add("Tn", ztmm)
                    def ztev(V):
                        V.tensor_copy(sol["ZT"][:], psMM[:])
                    add("V", ztev)

                    if not first:
                        def ksmm(Tn, j=j):
                            for c in range(2):
                                ins = Tn.raw("matmul", psBig[:], kf_t[j][:, c, :], S_bf[:, c, :],
                                             start=(c == 0), stop=(c == 1))
                            Tn.inc(ins)
                        add("Tn", ksmm)
                        def rv2(V, j=j, jb=jb, h=h):
                            V.tensor_tensor(Rtmp[:], v_t[j][:], psBig[:], op=ALU.subtract)
                            V.tensor_scalar(R_sb[:], Rtmp[:], bt_t[jb][:, h:h+1], None, op0=ALU.mult)
                        add("V", rv2)
                    else:
                        def rv1(V, j=j, jb=jb, h=h):
                            V.tensor_scalar(R_sb[:], v_t[j][:], bt_t[jb][:, h:h+1], None, op0=ALU.mult)
                        add("V", rv1)
                    def dmm(Tn):
                        Tn.inc(Tn.raw("matmul", psBig[:], sol["ZT"][:], R_sb[:],
                                      start=True, stop=True))
                    add("Tn", dmm)
                    def dev(S):
                        S.copy(D_sb[:], psBig[:])
                    add("S", dev)
                    def wmm(Tn, j=j):
                        for c in range(2):
                            ins = Tn.raw("matmul", psW[:], kf_t[j][:, c, :], qf_t[j][:, c, :],
                                         start=(c == 0), stop=(c == 1))
                        Tn.inc(ins)
                    add("Tn", wmm)
                    def wev(V):
                        V.tensor_tensor(W_sb[:], psW[:], cst["maskWuf"][:], op=ALU.mult)
                    add("V", wev)
                    def omm(Tn, j=j, first=first):
                        for dvc in range(4):
                            st_ = True
                            if not first:
                                for c in range(2):
                                    Tn.raw("matmul", psBig[:, dvc*128:(dvc+1)*128],
                                           S_bf[:, c, dvc*128:(dvc+1)*128], qf_t[j][:, c, :],
                                           start=st_, stop=False)
                                    st_ = False
                            ins = Tn.raw("matmul", psBig[:, dvc*128:(dvc+1)*128],
                                         D_sb[:, dvc*128:(dvc+1)*128], W_sb[:],
                                         start=st_, stop=True)
                        Tn.inc(ins)
                    add("Tn", omm)
                    def oev(S):
                        S.copy(o_f[:], psBig[:])
                        S.activation(osq[:], o_f[:], AF.Square)
                    add("S", oev)
                    def onorm_mm(Tn):
                        for dvc in range(4):
                            ins = Tn.raw("matmul", psA[0:1, 0:128], cst["onesb"][:],
                                         osq[:, dvc*128:(dvc+1)*128],
                                         start=(dvc == 0), stop=(dvc == 3))
                        Tn.inc(ins)
                    add("Tn", onorm_mm)
                    def onorm_row(V):
                        V.tensor_scalar(rowo[:], psA[0:1, 0:128], 1.0 / 512, 1e-5,
                                        op0=ALU.mult, op1=ALU.add)
                    add("V", onorm_row)
                    def onorm_rowS(S):
                        S.activation(rowo[:], rowo[:], AF.Ln)
                        S.activation(rowo_b[:], rowo[:], AF.Exp, scale=-0.5)
                    add("S", onorm_rowS)
                    def onorm_bc(Tn):
                        Tn.inc(Tn.raw("matmul", psW[:], cst["ones1"][:], rowo_b[:],
                                      start=True, stop=True))
                    add("Tn", onorm_bc)
                    def oapply(V, ci=ci, j=j):
                        V.wait_stream(SOs[ci % 2], ci // 2)
                        for dvc in range(4):
                            V.scalar_tensor_tensor(o_out[ci % 2][:, dvc, :],
                                                   o_f[:, dvc*128:(dvc+1)*128],
                                                   onw_sb[:, dvc:dvc+1], psW[:],
                                                   op0=ALU.mult, op1=ALU.mult)
                    add("V", oapply)
                    def ostore(G, ci=ci, bb_=bb_, h=h, ct=ct):
                        G.dma(o_n[bb_, h, :, ct*128:(ct+1)*128].rearrange("(c p) t -> p c t", p=128),
                              o_out[ci % 2][:], SOs[ci % 2])
                    add("G", ostore)
                    def smm(Tn, j=j, ci=ci):
                        for dkc in range(2):
                            ins = Tn.raw("matmul", psS[:, dkc, :], kt_t[j][:, dkc*128:(dkc+1)*128],
                                         D_sb[:], start=True, stop=True)
                        Tn.inc(ins)
                        Tn.mark(("chfree", ci))
                    add("Tn", smm)
                    def supd(V, bh=bh, first=first):
                        if first:
                            V.tensor_copy(S_all[:, bh, :, :].rearrange("p c t -> p (c t)"),
                                          psS[:].rearrange("p c t -> p (c t)"))
                        else:
                            for c in range(2):
                                V.tensor_tensor(S_all[:, bh, c, :], S_all[:, bh, c, :],
                                                psS[:, c, :], op=ALU.add)
                    add("V", supd)

        blk = e(nc.Block())

        @blk.sync
        def _(q_):
            Q = b.Q
            ci = -1
            for ct in range(NCH):
                for bb_ in range(B):
                    for h in range(4):
                        ci += 1
                        j = ci % 2
                        if ci >= 2:
                            Q.wait(("chfree", ci - 2))
                        if h == 0:
                            jb = (ci // 4) % 2
                            if ci >= 5:
                                Q.wait(("chfree", ci - 5))
                            Q.dma(bt_t[jb][:], beta_t[bb_*T+ct*128: bb_*T+ct*128+128, :], BTs[jb])
                        Q.dma(qf_t[j][:], qf[bb_, h, :, ct*128:(ct+1)*128].rearrange("(c p) t -> p c t", p=128), LDs[j])
                        Q.dma(kf_t[j][:], kf[bb_, h, :, ct*128:(ct+1)*128].rearrange("(c p) t -> p c t", p=128), LDs[j])
                        Q.dma(kt_t[j][:], kt[bb_, h, ct*128:(ct+1)*128, :], LDs[j])
                        Q.dma(v_t[j][:], vt[bb_*T+ct*128: bb_*T+ct*128+128, h*512:(h+1)*512], LDs[j])

        @blk.gpsimd
        def _(g):
            G = b.G
            G.dma(onw_sb[:], onw, WTs)
            for nm, dr in (("onesb", onesb), ("ones1", ones1), ("id128", id128),
                           ("id128f", id128f), ("maskLd", maskLd), ("maskLoff", maskLoff),
                           ("maskWuf", maskWuf), ("bdmaskf", bdmaskf)):
                G.dma(cst[nm][:], dr, WTs)
            run_chain(b, steps, G)
            for j in range(2):
                if SOs[j].n:
                    G.wait_stream(SOs[j], SOs[j].n)

        @blk.tensor
        def _(t):
            b.Tn.wait_stream(WTs, 9)
            run_chain(b, steps, b.Tn)

        @blk.scalar
        def _(s):
            run_chain(b, steps, b.S)

        @blk.vector
        def _(v):
            b.V.wait_stream(WTs, 9)
            run_chain(b, steps, b.V)
    return nc


def build_launch_C():
    return build_twice(_launch_C)



def _launch_D(b):
    nc = _mknc()
    o_n = _dram(nc, "o_n", [B, 4, 512, T], BF16, "ExternalInput")
    Wo = _dram(nc, "Wo", [2048, 1024], F32, "ExternalInput")
    xn_f = _dram(nc, "xn_f", [D, NT], F32, "ExternalInput")
    s_fm = _dram(nc, "s_fm", [D, NT], F32, "ExternalOutput")
    W = 512
    NTILE = NT // W
    with ExitStack() as stk:
        e = stk.enter_context
        b.make(nc, stk)
        ot = [e(_sbuf(nc, f"ot{j}", [128, 16, W], BF16)) for j in range(2)]
        xnt = [e(_sbuf(nc, f"xnt{j}", [128, 8, W], F32)) for j in range(2)]
        st_ = [e(_sbuf(nc, f"st{j}", [128, 8, W], F32)) for j in range(2)]
        Wo_sb = e(_sbuf(nc, "Wo_sb", [128, 16, 1024], BF16))
        psA = e(_psum(nc, "psA", [128, W], F32))
        psB = e(_psum(nc, "psB", [128, W], F32))
        LDs = [b.stream(nc, stk, f"LD{j}") for j in range(2)]
        XNs = [b.stream(nc, stk, f"XN{j}") for j in range(2)]
        WTs = b.stream(nc, stk, "WT")
        SSs = [b.stream(nc, stk, f"SS{j}") for j in range(2)]

        steps = []
        def add(en, fn):
            steps.append((en, fn))
        for i in range(NTILE):
            j = i % 2
            for fc in range(8):
                def mmst(Tn, i=i, j=j, fc=fc):
                    if fc == 0:
                        Tn.wait_stream(WTs, 1)
                        Tn.wait_stream(LDs[j], i // 2 + 1)
                        Tn.wait_stream(XNs[j], i // 2 + 1)
                    ps = psA if fc % 2 == 0 else psB
                    for hv in range(16):
                        ins = Tn.raw("matmul", ps[:], Wo_sb[:, hv, fc*128:(fc+1)*128],
                                     ot[j][:, hv, :], start=(hv == 0), stop=(hv == 15))
                    Tn.inc(ins)
                    if fc == 7:
                        Tn.mark(("otfree", i))
                add("Tn", mmst)
                def addst(V, i=i, j=j, fc=fc):
                    if fc == 0:
                        V.wait_stream(SSs[j], i // 2)
                    ps = psA if fc % 2 == 0 else psB
                    V.tensor_tensor(st_[j][:, fc, :], xnt[j][:, fc, :], ps[:], op=ALU.add)
                    if fc == 7:
                        V.mark(("sdone", i))
                add("V", addst)
            def sstore(G, i=i, j=j):
                G.dma(fm_slice(s_fm, i*W, W), st_[j][:], SSs[j])
            add("G", sstore)

        blk = e(nc.Block())
        @blk.sync
        def _(q_):
            Q = b.Q
            for i in range(NTILE):
                j = i % 2
                bb = i // 4
                tl = (i % 4) * W
                if i >= 2:
                    Q.wait(("otfree", i - 2))
                Q.dma(ot[j][:], o_n[bb][:, :, tl:tl+W].rearrange("h (c p) t -> p (h c) t", p=128), LDs[j])
        @blk.scalar
        def _(s):
            S = b.S
            for i in range(NTILE):
                j = i % 2
                if i >= 2:
                    S.wait(("sdone", i - 2))
                S.dma(xnt[j][:], fm_slice(xn_f, i*W, W), XNs[j])
        @blk.gpsimd
        def _(g):
            G = b.G
            G.dma(Wo_sb[:], Wo.rearrange("(c p) f -> p c f", p=128), WTs)
            run_chain(b, steps, G)
            for j in range(2):
                if SSs[j].n:
                    G.wait_stream(SSs[j], SSs[j].n)
        @blk.tensor
        def _(t):
            run_chain(b, steps, b.Tn)
        @blk.vector
        def _(v):
            run_chain(b, steps, b.V)
    return nc


def build_launch_D():
    return build_twice(_launch_D)



def _launch_E(b):
    nc = _mknc()
    xn_b = _dram(nc, "xn_b", [D, NT], BF16, "ExternalInput")
    Wr = _dram(nc, "Wr", [D, 16], F32, "ExternalInput")
    id128 = _dram(nc, "id128", [128, 128], BF16, "ExternalInput")
    cwT = _dram(nc, "cwT", [16, NT], BF16, "ExternalOutput")
    W = 512
    NTILE = NT // W
    with ExitStack() as stk:
        e = stk.enter_context
        b.make(nc, stk)
        xnb = [e(_sbuf(nc, f"xnb{j}", [128, 8, W], BF16)) for j in range(2)]
        Wr_sb = e(_sbuf(nc, "Wr_sb", [128, 8, 16], BF16))
        id_sb = e(_sbuf(nc, "id_sb", [128, 128], BF16))
        probs = e(_sbuf(nc, "probs", [128, 16], F32))
        lo = e(_sbuf(nc, "lo", [128, 16], F32))
        ge = e(_sbuf(nc, "ge", [128, 16], F32))
        cw_b = e(_sbuf(nc, "cw_b", [128, 16], BF16))
        mrow = e(_sbuf(nc, "mrow", [128, 4], F32))
        cwt_sb = [e(_sbuf(nc, f"cwt{j}", [16, W], BF16)) for j in range(2)]
        psR = e(_psum(nc, "psR", [128, 16], F32))
        psT = e(_psum(nc, "psT", [16, 128], BF16))
        LDs = [b.stream(nc, stk, f"LD{j}") for j in range(2)]
        WTs = b.stream(nc, stk, "WT")
        SCs = [b.stream(nc, stk, f"SC{j}") for j in range(2)]

        steps = []
        def add(en, fn):
            steps.append((en, fn))
        for i in range(NTILE):
            j = i % 2
            for tt in range(4):
                def rmm(Tn, i=i, j=j, tt=tt):
                    if tt == 0:
                        Tn.wait_stream(WTs, 2)
                        Tn.wait_stream(LDs[j], i // 2 + 1)
                    for dc in range(8):
                        ins = Tn.raw("matmul", psR[:], xnb[j][:, dc, tt*128:(tt+1)*128],
                                     Wr_sb[:, dc, :], start=(dc == 0), stop=(dc == 7))
                    Tn.inc(ins)
                    if tt == 3:
                        Tn.mark(("xfree", i))
                add("Tn", rmm)
                def soft1(V):
                    V.reduce_max(mrow[:, 0:1], psR[:], axis=AX.X, negate=True)
                add("V", soft1)
                def soft2(S):
                    S.activation(probs[:], psR[:], AF.Exp, bias=mrow[:, 0:1],
                                 accum_out=mrow[:, 1:2])
                add("S", soft2)
                def soft3(V, i=i, j=j, tt=tt):
                    V.reciprocal(mrow[:, 1:2], mrow[:, 1:2])
                    V.tensor_scalar(probs[:], probs[:], mrow[:, 1:2], None, op0=ALU.mult)
                    V.reduce_max(mrow[:, 2:3], probs[:], axis=AX.X)
                    V.tensor_scalar(ge[:], probs[:], mrow[:, 2:3], None, op0=ALU.is_ge)
                    V.tensor_tensor(lo[:], probs[:], ge[:], op=ALU.mult)
                    V.tensor_tensor(lo[:], probs[:], lo[:], op=ALU.subtract)
                    V.reduce_max(mrow[:, 3:4], lo[:], axis=AX.X)
                    V.tensor_scalar(ge[:], probs[:], mrow[:, 3:4], None, op0=ALU.is_ge)
                    V.tensor_tensor(cw_b[:], probs[:], ge[:], op=ALU.mult)
                add("V", soft3)
                def tpcw(Tn, tt=tt):
                    Tn.inc(Tn.raw("transpose", psT[:], cw_b[:], id_sb[:]))
                add("Tn", tpcw)
                def cpcw(V, i=i, j=j, tt=tt):
                    if tt == 0:
                        V.wait_stream(SCs[j], i // 2)
                    V.tensor_copy(cwt_sb[j][:, tt*128:(tt+1)*128], psT[:])
                add("V", cpcw)
            def cstore(G, i=i, j=j):
                G.dma(cwT[:, i*W:(i+1)*W], cwt_sb[j][:], SCs[j])
            add("G", cstore)

        blk = e(nc.Block())
        @blk.sync
        def _(q_):
            Q = b.Q
            for i in range(NTILE):
                j = i % 2
                if i >= 2:
                    Q.wait(("xfree", i - 2))
                Q.dma(xnb[j][:], fm_slice(xn_b, i*W, W), LDs[j])
        @blk.gpsimd
        def _(g):
            G = b.G
            G.dma(Wr_sb[:], Wr.rearrange("(c p) f -> p c f", p=128), WTs)
            G.dma(id_sb[:], id128, WTs)
            run_chain(b, steps, G)
            for j in range(2):
                if SCs[j].n:
                    G.wait_stream(SCs[j], SCs[j].n)
        @blk.tensor
        def _(t):
            run_chain(b, steps, b.Tn)
        @blk.scalar
        def _(s):
            run_chain(b, steps, b.S)
        @blk.vector
        def _(v):
            run_chain(b, steps, b.V)
    return nc


def build_launch_E():
    return build_twice(_launch_E)



def _launch_G(b):
    nc = _mknc()
    x = _dram(nc, "x_fm", [D, NT], F32, "ExternalInput")
    s_fm = _dram(nc, "s_fm", [D, NT], F32, "ExternalInput")
    moe = _dram(nc, "moe_fm", [D, NT], F32, "ExternalInput")
    onesb = _dram(nc, "onesb", [128, 1], BF16, "ExternalInput")
    ones1 = _dram(nc, "ones1", [1, 128], BF16, "ExternalInput")
    x1_f = _dram(nc, "x1_f", [D, NT], F32, "ExternalOutput")
    xn1_f = _dram(nc, "xn1_f", [D, NT], F32, "ExternalOutput")
    xn1_b = _dram(nc, "xn1_b", [D, NT], BF16, "ExternalOutput")
    W = 512
    NTILE = NT // W
    with ExitStack() as stk:
        e = stk.enter_context
        b.make(nc, stk)
        xt = [e(_sbuf(nc, f"xt{j}", [128, 8, W], F32)) for j in range(2)]
        st_ = [e(_sbuf(nc, f"st{j}", [128, 8, W], F32)) for j in range(2)]
        mt = [e(_sbuf(nc, f"mt{j}", [128, 8, W], F32)) for j in range(2)]
        x1b = [e(_sbuf(nc, f"x1b{j}", [128, 8, W], BF16)) for j in range(2)]
        xsq = [e(_sbuf(nc, f"xsq{j}", [128, 8, W], BF16)) for j in range(2)]
        xn1bt = [e(_sbuf(nc, f"xn1b{j}", [128, 8, W], BF16)) for j in range(2)]
        x1t = xt
        xn1t = mt
        rows = [e(_sbuf(nc, f"rows{j}", [1, 2, W], F32)) for j in range(2)]
        rows_b = [e(_sbuf(nc, f"rowsb{j}", [1, 2, W], BF16)) for j in range(2)]
        ones_sb = e(_sbuf(nc, "ones_sb", [128, 1], BF16))
        ones1_sb = e(_sbuf(nc, "ones1_sb", [1, 128], BF16))
        ps_mu = e(_psum(nc, "ps_mu", [1, W], F32))
        ps_s2 = e(_psum(nc, "ps_s2", [1, W], F32))
        ps_b1 = e(_psum(nc, "ps_b1", [128, W], F32))
        ps_b2 = e(_psum(nc, "ps_b2", [128, W], F32))
        LXs = [b.stream(nc, stk, f"LX{j}") for j in range(2)]
        LSs = [b.stream(nc, stk, f"LS{j}") for j in range(2)]
        LMs = [b.stream(nc, stk, f"LM{j}") for j in range(2)]
        WTs = b.stream(nc, stk, "WT")
        S1s = [b.stream(nc, stk, f"S1{j}") for j in range(2)]
        S2s = [b.stream(nc, stk, f"S2{j}") for j in range(2)]
        S3s = [b.stream(nc, stk, f"S3{j}") for j in range(2)]

        steps = []
        def add(en, fn):
            steps.append((en, fn))
        for i in range(NTILE):
            j = i % 2
            def gsilu(S, i=i, j=j):
                S.wait_stream(LMs[j], i // 2 + 1)
                S.activation(mt[j][:].rearrange("p c t -> p (c t)"),
                             mt[j][:].rearrange("p c t -> p (c t)"), AF.Silu)
            add("S", gsilu)
            def comb(V, i=i, j=j):
                V.wait_stream(LXs[j], i // 2 + 1)
                V.wait_stream(LSs[j], i // 2 + 1)
                if i >= 2:
                    V.wait_stream(S1s[j], i // 2)
                V.tensor_tensor(mt[j][:], st_[j][:], mt[j][:], op=ALU.mult)
                V.tensor_tensor(x1t[j][:], xt[j][:], mt[j][:], op=ALU.add)
                V.mark(("x1done", i))
            add("V", comb)
            def x1cast(G, i=i, j=j):
                G.tensor_copy(x1b[j][:], x1t[j][:])
                G.dma(fm_slice(x1_f, i*W, W), x1t[j][:], S1s[j])
            add("G", x1cast)
            def sq(S, i=i, j=j):
                S.activation(xsq[j][:].rearrange("p c t -> p (c t)"),
                             x1t[j][:].rearrange("p c t -> p (c t)"), AF.Square)
            add("S", sq)
            def stats(Tn, i=i, j=j):
                if i == 0:
                    Tn.wait_stream(WTs, 2)
                for c in range(8):
                    ins = Tn.raw("matmul", ps_mu[:], ones_sb[:], x1b[j][:, c, :],
                                 start=(c == 0), stop=(c == 7))
                Tn.inc(ins)
                for c in range(8):
                    ins = Tn.raw("matmul", ps_s2[:], ones_sb[:], xsq[j][:, c, :],
                                 start=(c == 0), stop=(c == 7))
                Tn.inc(ins)
            add("Tn", stats)
            def rowsa(V, j=j):
                V.tensor_scalar(rows[j][:, 0, :], ps_mu[:], 1.0 / D, None, op0=ALU.mult)
                V.tensor_tensor(rows[j][:, 1, :], rows[j][:, 0, :], rows[j][:, 0, :], op=ALU.mult)
                V.tensor_scalar(rows[j][:, 1, :], rows[j][:, 1, :], -1.0, 1e-5, op0=ALU.mult, op1=ALU.add)
                V.scalar_tensor_tensor(rows[j][:, 1, :], ps_s2[:], 1.0 / D, rows[j][:, 1, :],
                                       op0=ALU.mult, op1=ALU.add)
            add("V", rowsa)
            def rowsb(S, j=j):
                S.activation(rows[j][:, 1, :], rows[j][:, 1, :], AF.Ln)
                S.activation(rows_b[j][:, 0, :], rows[j][:, 1, :], AF.Exp, scale=-0.5)
            add("S", rowsb)
            def roww(V, j=j):
                V.tensor_tensor(rows_b[j][:, 1, :], rows[j][:, 0, :], rows_b[j][:, 0, :], op=ALU.mult)
            add("V", roww)
            def bcast(Tn, j=j):
                Tn.inc(Tn.raw("matmul", ps_b1[:], ones1_sb[:], rows_b[j][:, 0, :], start=True, stop=True))
                Tn.inc(Tn.raw("matmul", ps_b2[:], ones1_sb[:], rows_b[j][:, 1, :], start=True, stop=True))
            add("Tn", bcast)
            def apply(V, i=i, j=j):
                pass_ = None
                V.tensor_tensor(xn1t[j][:], x1t[j][:], bc_mid(ps_b1[:], 8), op=ALU.mult)
                V.tensor_tensor(xn1t[j][:], xn1t[j][:], bc_mid(ps_b2[:], 8), op=ALU.subtract)
            add("V", apply)
            def outs(G, i=i, j=j):
                if i >= 2:
                    G.wait_stream(S3s[j], i // 2)
                G.tensor_copy(xn1bt[j][:], xn1t[j][:])
                G.dma(fm_slice(xn1_f, i*W, W), xn1t[j][:], S2s[j])
                G.dma(fm_slice(xn1_b, i*W, W), xn1bt[j][:], S3s[j])
            add("G", outs)

        blk = e(nc.Block())
        @blk.sync
        def _(q_):
            Q = b.Q
            for i in range(NTILE):
                j = i % 2
                if i >= 2:
                    Q.wait(("x1done", i - 2))
                    Q.wait_stream(S1s[j], i // 2)
                    Q.wait_stream(S2s[j], i // 2)
                    Q.wait_stream(S3s[j], i // 2)
                Q.dma(xt[j][:], fm_slice(x, i*W, W), LXs[j])
                Q.dma(st_[j][:], fm_slice(s_fm, i*W, W), LSs[j])
                Q.dma(mt[j][:], fm_slice(moe, i*W, W), LMs[j])
        @blk.scalar
        def _(s):
            run_chain(b, steps, b.S)
        @blk.gpsimd
        def _(g):
            G = b.G
            G.dma(ones_sb[:], onesb, WTs)
            G.dma(ones1_sb[:], ones1, WTs)
            run_chain(b, steps, G)
            for j in range(2):
                for s_ in (S1s[j], S2s[j], S3s[j]):
                    if s_.n:
                        G.wait_stream(s_, s_.n)
        @blk.tensor
        def _(t):
            run_chain(b, steps, b.Tn)
        @blk.vector
        def _(v):
            run_chain(b, steps, b.V)
    return nc


def build_launch_G():
    return build_twice(_launch_G)



def _launch_F(b):
    nc = _mknc()
    xn_b = _dram(nc, "xn_b", [D, NT], BF16, "ExternalInput")
    cwT = _dram(nc, "cwT", [16, NT], BF16, "ExternalInput")
    Wg = _dram(nc, "Wg", [16, D, 512], F32, "ExternalInput")
    Wu = _dram(nc, "Wu", [16, D, 512], F32, "ExternalInput")
    Wd = _dram(nc, "Wd", [16, 512, D], F32, "ExternalInput")
    Sg = _dram(nc, "Sg", [D, 1024], F32, "ExternalInput")
    Su = _dram(nc, "Su", [D, 1024], F32, "ExternalInput")
    Sd = _dram(nc, "Sd", [1024, D], F32, "ExternalInput")
    sel16 = _dram(nc, "sel16", [16, 2048], BF16, "ExternalInput")
    moe = _dram(nc, "moe_fm", [D, NT], F32, "ExternalOutput")
    WB = 2048
    NBLK = NT // WB
    NE = 18
    with ExitStack() as stk:
        e = stk.enter_context
        b.make(nc, stk)
        xnb = e(_sbuf(nc, "xnb", [128, 8, WB], BF16))
        cw_sb = e(_sbuf(nc, "cw_sb", [16, WB], BF16))
        sel_sb = e(_sbuf(nc, "sel_sb", [16, 2048], BF16))
        Wg_sb = [e(_sbuf(nc, f"Wg{j}", [128, 8, 512], BF16)) for j in range(2)]
        Wu_sb = [e(_sbuf(nc, f"Wu{j}", [128, 8, 512], BF16)) for j in range(2)]
        Wd_sb = [e(_sbuf(nc, f"Wd{j}", [128, 4, 1024], BF16)) for j in range(2)]
        acc = e(_sbuf(nc, "acc", [128, 8, WB], F32))
        hg = e(_sbuf(nc, "hg", [128, 4, 512], F32))
        h1 = e(_sbuf(nc, "h1", [128, 4, 512], F32))
        hc = e(_sbuf(nc, "hc", [128, 4, 512], BF16))
        psG = e(_psum(nc, "psG", [128, 4, 512], F32))
        psU = e(_psum(nc, "psU", [128, 4, 512], F32))
        LXs = b.stream(nc, stk, "LX")
        LCs = b.stream(nc, stk, "LC")
        WTs = b.stream(nc, stk, "WT")
        Wgs = [b.stream(nc, stk, f"WG{j}") for j in range(2)]
        SMs = b.stream(nc, stk, "SM")

        steps = []
        def add(en, fn):
            steps.append((en, fn))

        def emit_wload(G, eidx, j):
            if eidx < 16:
                G.dma(Wg_sb[j][:], Wg[eidx].rearrange("(c p) f -> p c f", p=128), Wgs[j])
                G.dma(Wu_sb[j][:], Wu[eidx].rearrange("(c p) f -> p c f", p=128), Wgs[j])
                G.dma(Wd_sb[j][:], Wd[eidx].rearrange("(c p) f -> p c f", p=128), Wgs[j])
            else:
                hh = eidx - 16
                G.dma(Wg_sb[j][:], Sg[:, hh*512:(hh+1)*512].rearrange("(c p) f -> p c f", p=128), Wgs[j])
                G.dma(Wu_sb[j][:], Su[:, hh*512:(hh+1)*512].rearrange("(c p) f -> p c f", p=128), Wgs[j])
                G.dma(Wd_sb[j][:], Sd[hh*512:(hh+1)*512, :].rearrange("(c p) f -> p c f", p=128), Wgs[j])

        ne = 0
        mk = {"wdacc0": 0, "wdacc1": 0, "hbuild": 0, "wdmm1": 0, "gsil": 0}
        def addd(en, fn, deps):
            steps.append((en, fn, tuple(deps)))
            return len(steps)
        for blk_i in range(NBLK):
            def bload(G, blk_i=blk_i):
                if blk_i > 0:
                    G.wait_stream(SMs, blk_i)
                G.dma(xnb[:], fm_slice(xn_b, blk_i*WB, WB), LXs)
                G.dma(cw_sb[:], cwT[:, blk_i*WB:(blk_i+1)*WB], LCs)
            addd("G", bload, [mk["wdacc1"]] if blk_i else [])
            for eidx in list(range(16)) + [16, 17]:
                j = ne % 2
                if ne == 0:
                    def wl0(G, eidx=eidx, j=j):
                        emit_wload(G, eidx, j)
                    addd("G", wl0, [])
                nxt = eidx + 1 if eidx < 17 else (0 if blk_i + 1 < NBLK else None)
                if nxt is not None:
                    def wln(G, nxt=nxt, j2=(ne + 1) % 2):
                        emit_wload(G, nxt, j2)
                    addd("G", wln, [mk["wdmm1"]])
                for q in range(4):
                    def gmm(Tn, j=j, q=q, ne=ne, blk_i=blk_i):
                        if q == 0:
                            Tn.wait_stream(Wgs[j], 3 * (ne // 2 + 1))
                            Tn.wait_stream(LXs, blk_i + 1)
                            Tn.wait_stream(LCs, blk_i + 1)
                            if ne == 0:
                                Tn.wait_stream(WTs, 1)
                        for fc in range(4):
                            for dc in range(8):
                                ins = Tn.raw("matmul", psG[:, fc, :],
                                             Wg_sb[j][:, dc, fc*128:(fc+1)*128],
                                             xnb[:, dc, q*512:(q+1)*512],
                                             start=(dc == 0), stop=(dc == 7))
                        Tn.inc(ins)
                    i_gmm = addd("Tn", gmm, [mk["wdacc0"]])
                    def gsil(S):
                        S.activation(hg[:].rearrange("p c t -> p (c t)"),
                                     psG[:].rearrange("p c t -> p (c t)"), AF.Silu)
                    i_gsil = addd("S", gsil, [i_gmm, mk["hbuild"]])
                    mk["gsil"] = i_gsil
                    def umm(Tn, j=j, q=q):
                        for fc in range(4):
                            for dc in range(8):
                                ins = Tn.raw("matmul", psU[:, fc, :],
                                             Wu_sb[j][:, dc, fc*128:(fc+1)*128],
                                             xnb[:, dc, q*512:(q+1)*512],
                                             start=(dc == 0), stop=(dc == 7))
                        Tn.inc(ins)
                    i_umm = addd("Tn", umm, [mk["wdacc1"]])
                    def bcs(Tn, eidx=eidx, q=q):
                        if eidx < 16:
                            Tn.inc(Tn.raw("matmul", psG[:, 0, :], sel_sb[:, eidx*128:(eidx+1)*128],
                                          cw_sb[:, q*512:(q+1)*512], start=True, stop=True))
                        else:
                            Tn.inc(Tn.raw("engine_nop" if False else "nop"))
                    i_bc = addd("Tn", bcs, [i_gsil])
                    def hbuild(V, eidx=eidx):
                        V.tensor_tensor(h1[:].rearrange("p c t -> p (c t)"),
                                        hg[:].rearrange("p c t -> p (c t)"),
                                        psU[:].rearrange("p c t -> p (c t)"), op=ALU.mult)
                        if eidx < 16:
                            V.tensor_tensor(hc[:], h1[:], bc_mid(psG[:, 0, :], 4), op=ALU.mult)
                        else:
                            V.tensor_copy(hc[:].rearrange("p c t -> p (c t)"),
                                          h1[:].rearrange("p c t -> p (c t)"))
                    i_hb = addd("V", hbuild, [i_bc, i_gsil, mk["wdmm1"]])
                    mk["hbuild"] = i_hb
                    for half in range(2):
                        ps = psG if half == 0 else psU
                        def wdmm(Tn, j=j, half=half, ps=ps):
                            for dchh in range(4):
                                dch = half*4 + dchh
                                for fsc in range(4):
                                    ins = Tn.raw("matmul", ps[:, dchh, :],
                                                 Wd_sb[j][:, fsc, dch*128:(dch+1)*128],
                                                 hc[:, fsc, :],
                                                 start=(fsc == 0), stop=(fsc == 3))
                            Tn.inc(ins)
                        i_wd = addd("Tn", wdmm, [i_hb])
                        if half == 1:
                            mk["wdmm1"] = i_wd
                        def wdacc(V, half=half, q=q, eidx=eidx, ps=ps):
                            a = acc[:, half*4:half*4+4, q*512:(q+1)*512]
                            pv = ps[:]
                            if eidx == 0:
                                V.tensor_copy(a, pv)
                            else:
                                V.tensor_tensor(a, a, pv, op=ALU.add)
                        i_wa = addd("V", wdacc, [i_wd])
                        mk["wdacc0" if half == 0 else "wdacc1"] = i_wa
                ne += 1
            def mstore(G, blk_i=blk_i):
                G.dma(fm_slice(moe, blk_i*WB, WB), acc[:], SMs)
            addd("G", mstore, [mk["wdacc1"]])
        blk = e(nc.Block())
        @blk.gpsimd
        def _(g):
            G = b.G
            G.dma(sel_sb[:], sel16, WTs)
            run_chain(b, steps, G)
            if SMs.n:
                G.wait_stream(SMs, SMs.n)
        @blk.tensor
        def _(t):
            run_chain(b, steps, b.Tn)
        @blk.scalar
        def _(s):
            run_chain(b, steps, b.S)
        @blk.vector
        def _(v):
            run_chain(b, steps, b.V)
        @blk.sync
        def _(q_):
            pass
    return nc


def build_launch_F():
    return build_twice(_launch_F)



def _launch_H(b):
    nc = _mknc()
    xn1 = _dram(nc, "xn1_b", [D, NT], BF16, "ExternalInput")
    Wq1 = _dram(nc, "Wq1", [D, 1024], F32, "ExternalInput")
    Wqr = _dram(nc, "Wqr", [D, 512], F32, "ExternalInput")
    Wkr = _dram(nc, "Wkr", [D, 32], F32, "ExternalInput")
    Wkvd = _dram(nc, "Wkvd", [D, 256], F32, "ExternalInput")
    Wku = _dram(nc, "Wku", [256, 1024], F32, "ExternalInput")
    Wvu = _dram(nc, "Wvu", [256, 1024], F32, "ExternalInput")
    ropeC = _dram(nc, "ropeC", [T, 512], F32, "ExternalInput")
    ropeS = _dram(nc, "ropeS", [T, 512], F32, "ExternalInput")
    id128 = _dram(nc, "id128", [128, 128], BF16, "ExternalInput")
    q96 = _dram(nc, "q96", [B, 16, 96, T], BF16, "ExternalOutput")
    k96 = _dram(nc, "k96", [B, 16, 96, T], BF16, "ExternalOutput")
    vaug = _dram(nc, "vaug", [B, T, 1040], BF16, "ExternalOutput")
    W = 512
    NTILE = NT // W
    with ExitStack() as stk:
        e = stk.enter_context
        b.make(nc, stk)
        xnb = [e(_sbuf(nc, f"xnb{j}", [128, 8, W], BF16)) for j in range(2)]
        Wq_sb = e(_sbuf(nc, "Wq_sb", [128, 8, 1024], BF16))
        Wqr_sb = e(_sbuf(nc, "Wqr_sb", [128, 8, 512], BF16))
        Wkr_sb = e(_sbuf(nc, "Wkr_sb", [128, 8, 32], BF16))
        Wkvd_sb = e(_sbuf(nc, "Wkvd_sb", [128, 8, 256], BF16))
        Wku_sb = e(_sbuf(nc, "Wku_sb", [128, 2, 1024], BF16))
        Wvu_sb = e(_sbuf(nc, "Wvu_sb", [128, 2, 1024], BF16))
        id_sb = e(_sbuf(nc, "id_sb", [128, 128], BF16))
        ropC = [e(_sbuf(nc, f"ropC{j}", [128, 4, 512], F32)) for j in range(2)]
        ropS = [e(_sbuf(nc, f"ropS{j}", [128, 4, 512], F32)) for j in range(2)]
        qc_t = [e(_sbuf(nc, f"qc{j}", [128, 8, W], BF16)) for j in range(2)]
        lat_b = [e(_sbuf(nc, f"lat{j}", [128, 2, W], BF16)) for j in range(2)]
        kc_t = [e(_sbuf(nc, f"kc{j}", [128, 8, W], BF16)) for j in range(2)]
        qr_tm = e(_sbuf(nc, "qr_tm", [128, 4, 512], F32))
        qr_b = e(_sbuf(nc, "qr_b", [128, 4, 512], BF16))
        kr_tm = e(_sbuf(nc, "kr_tm", [128, 4, 32], F32))
        kr_b = e(_sbuf(nc, "kr_b", [128, 4, 32], BF16))
        qr_f = [e(_sbuf(nc, f"qrf{j}", [128, 4, 128], BF16)) for j in range(2)]
        kr_f = [e(_sbuf(nc, f"krf{j}", [32, 4, 128], BF16)) for j in range(2)]
        vau_t = [e(_sbuf(nc, f"vau{j}", [128, 4, 1040], BF16)) for j in range(2)]
        tmp1 = e(_sbuf(nc, "tmp1", [128, 512], F32))
        tmp2 = e(_sbuf(nc, "tmp2", [128, 512], F32))
        psA = e(_psum(nc, "psA", [128, W], F32))
        psB = e(_psum(nc, "psB", [128, W], F32))
        psT = e(_psum(nc, "psT", [128, 128], BF16))
        psK = e(_psum(nc, "psK", [32, 128], BF16))
        LDs = [b.stream(nc, stk, f"LD{j}") for j in range(2)]
        RPs = [b.stream(nc, stk, f"RP{j}") for j in range(2)]
        WTs = b.stream(nc, stk, "WT")
        SQ = [b.stream(nc, stk, f"SQ{j}") for j in range(2)]
        SK = [b.stream(nc, stk, f"SK{j}") for j in range(2)]
        SV = [b.stream(nc, stk, f"SV{j}") for j in range(2)]

        steps = []
        def add(en, fn):
            steps.append((en, fn))
        for i in range(NTILE):
            j = i % 2
            bb = i // 4
            tl = (i % 4) * W
            for fc in range(8):
                def qcmm(Tn, i=i, j=j, fc=fc):
                    if fc == 0:
                        Tn.wait_stream(WTs, 7)
                        Tn.wait_stream(LDs[j], i // 2 + 1)
                        Tn.wait_stream(RPs[j], 2 * (i // 2 + 1))
                    ps = psA if fc % 2 == 0 else psB
                    for dc in range(8):
                        ins = Tn.raw("matmul", ps[:], Wq_sb[:, dc, fc*128:(fc+1)*128],
                                     xnb[j][:, dc, :], start=(dc == 0), stop=(dc == 7))
                    Tn.inc(ins)
                add("Tn", qcmm)
                def qcev(S, i=i, j=j, fc=fc):
                    if fc == 0:
                        S.wait_stream(SQ[j], 80 * (i // 2))
                    ps = psA if fc % 2 == 0 else psB
                    S.copy(qc_t[j][:, fc, :], ps[:])
                add("S", qcev)
            def qcstore(G, i=i, j=j, bb=bb, tl=tl):
                for fc in range(8):
                    for h2 in range(2):
                        G.dma(q96[bb, fc*2+h2, 0:64, tl:tl+W], qc_t[j][64*h2:64*h2+64, fc, :], SQ[j])
            add("G", qcstore)
            for tt in range(4):
                def qrmm(Tn, j=j, tt=tt):
                    for dc in range(8):
                        ins = Tn.raw("matmul", psA[:], xnb[j][:, dc, tt*128:(tt+1)*128],
                                     Wqr_sb[:, dc, :], start=(dc == 0), stop=(dc == 7))
                    Tn.inc(ins)
                add("Tn", qrmm)
                def qrrope(V, j=j, tt=tt):
                    def v32(ap, off):
                        return bass.AP(tensor=ap.tensor, offset=ap.offset + off,
                                       ap=[list(ap.ap[0]), [32, 16], [1, 16]])
                    V.tensor_tensor(tmp1[:], psA[:], ropC[j][:, tt, :], op=ALU.mult)
                    V.tensor_tensor(v32(tmp2[:], 0), v32(psA[:], 16),
                                    v32(ropS[j][:, tt, :], 0), op=ALU.mult)
                    V.tensor_tensor(v32(tmp2[:], 16), v32(psA[:], 0),
                                    v32(ropS[j][:, tt, :], 16), op=ALU.mult)
                    V.tensor_tensor(qr_b[:, tt, :], tmp1[:], tmp2[:], op=ALU.add)
                add("V", qrrope)
            for tt in range(4):
                for g in range(4):
                    def qrtp(Tn, j=j, tt=tt, g=g):
                        Tn.inc(Tn.raw("transpose", psT[:], qr_b[:, tt, g*128:(g+1)*128], id_sb[:]))
                    add("Tn", qrtp)
                    def qrtpc(V, j=j, tt=tt, g=g, i=i):
                        if g == 0:
                            V.wait_stream(SQ[j], 80 * (i // 2) + 16 + 16 * tt)
                        V.tensor_copy(qr_f[j][:, g, :], psT[:])
                    add("V", qrtpc)
                def qrstore(G, j=j, bb=bb, tl=tl, tt=tt):
                    for g in range(4):
                        for hh in range(4):
                            G.dma(q96[bb, 4*g+hh, 64:96, tl+tt*128:tl+(tt+1)*128],
                                  qr_f[j][32*hh:32*hh+32, g, :], SQ[j])
                add("G", qrstore)
            for tt in range(4):
                def krmm(Tn, j=j, tt=tt):
                    for dc in range(8):
                        ins = Tn.raw("matmul", psA[:, 0:32], xnb[j][:, dc, tt*128:(tt+1)*128],
                                     Wkr_sb[:, dc, :], start=(dc == 0), stop=(dc == 7))
                    Tn.inc(ins)
                add("Tn", krmm)
                def krrope(V, j=j, tt=tt):
                    V.tensor_tensor(tmp1[:, 0:32], psA[:, 0:32], ropC[j][:, tt, 0:32], op=ALU.mult)
                    V.tensor_tensor(tmp2[:, 0:16], psA[:, 16:32], ropS[j][:, tt, 0:16], op=ALU.mult)
                    V.tensor_tensor(tmp2[:, 16:32], psA[:, 0:16], ropS[j][:, tt, 16:32], op=ALU.mult)
                    V.tensor_tensor(kr_b[:, tt, :], tmp1[:, 0:32], tmp2[:, 0:32], op=ALU.add)
                add("V", krrope)
                def krtp(Tn, j=j, tt=tt):
                    Tn.inc(Tn.raw("transpose", psK[:], kr_b[:, tt, :], id_sb[:]))
                add("Tn", krtp)
                def krtpc(V, j=j, tt=tt, i=i):
                    if tt == 0:
                        V.wait_stream(SK[j], 80 * (i // 2))
                    V.tensor_copy(kr_f[j][0:32, tt, :], psK[:])
                add("V", krtpc)
            def krstore(G, j=j, bb=bb, tl=tl):
                for tt in range(4):
                    for h in range(16):
                        G.dma(k96[bb, h, 64:96, tl+tt*128:tl+(tt+1)*128],
                              kr_f[j][0:32, tt, :], SK[j])
            add("G", krstore)
            for fc in range(2):
                def latmm(Tn, j=j, fc=fc):
                    for dc in range(8):
                        ins = Tn.raw("matmul", psA[:], Wkvd_sb[:, dc, fc*128:(fc+1)*128],
                                     xnb[j][:, dc, :], start=(dc == 0), stop=(dc == 7))
                    Tn.inc(ins)
                add("Tn", latmm)
                def latev(S, j=j, fc=fc):
                    S.copy(lat_b[j][:, fc, :], psA[:])
                add("S", latev)
            for fc in range(8):
                def kcmm(Tn, j=j, fc=fc):
                    ps = psA if fc % 2 == 0 else psB
                    for dc in range(2):
                        ins = Tn.raw("matmul", ps[:], Wku_sb[:, dc, fc*128:(fc+1)*128],
                                     lat_b[j][:, dc, :], start=(dc == 0), stop=(dc == 1))
                    Tn.inc(ins)
                add("Tn", kcmm)
                def kcev(S, j=j, fc=fc, i=i):
                    if fc == 0:
                        S.wait_stream(SK[j], 80 * (i // 2))
                    ps = psA if fc % 2 == 0 else psB
                    S.copy(kc_t[j][:, fc, :], ps[:])
                add("S", kcev)
            def kcstore(G, i=i, j=j, bb=bb, tl=tl):
                for fc in range(8):
                    for h2 in range(2):
                        G.dma(k96[bb, fc*2+h2, 0:64, tl:tl+W], kc_t[j][64*h2:64*h2+64, fc, :], SK[j])
            add("G", kcstore)
            def vinit(G, j=j, i=i):
                G.wait_stream(SV[j], i // 2)
                ap = bass.AP(tensor=vau_t[j][:].tensor, offset=vau_t[j][:].offset + 64,
                             ap=[list(vau_t[j][:].ap[0]), [1040, 4], [65, 16], [1, 1]])
                G.memset(ap, 1.0)
            add("G", vinit)
            for tt in range(4):
                for vh in range(2):
                    def vmm(Tn, j=j, tt=tt, vh=vh):
                        ps = psA if vh == 0 else psB
                        for dc in range(2):
                            ins = Tn.raw("matmul", ps[:], lat_b[j][:, dc, tt*128:(tt+1)*128],
                                         Wvu_sb[:, dc, vh*512:(vh+1)*512], start=(dc == 0), stop=(dc == 1))
                        Tn.inc(ins)
                    add("Tn", vmm)
                    def vev(V, j=j, tt=tt, vh=vh):
                        ps = psA if vh == 0 else psB
                        dst = vau_t[j][:, tt, :]
                        dstap = bass.AP(tensor=dst.tensor, offset=dst.offset + vh*8*65,
                                        ap=[list(dst.ap[0]), [65, 8], [1, 64]])
                        V.tensor_copy(dstap, ps[:].rearrange("p (h x) -> p h x", h=8))
                    add("V", vev)
            def vstore(G, i=i, j=j, bb=bb, tl=tl):
                G.dma(vaug[bb, tl:tl+W, :].rearrange("(tt p) f -> p tt f", p=128),
                      vau_t[j][:], SV[j])
                G.inc(G.raw("engine_nop"))
                G.mark(("tiledone", i))
            add("G", vstore)

        blk = e(nc.Block())
        @blk.sync
        def _(q_):
            Q = b.Q
            for i in range(NTILE):
                j = i % 2
                bb = i // 4
                tl = (i % 4) * W
                if i >= 2:
                    Q.wait(("tiledone", i - 2))
                Q.dma(xnb[j][:], fm_slice(xn1, i*W, W), LDs[j])
                Q.dma(ropC[j][:], ropeC[tl:tl+W, :].rearrange("(tt p) f -> p tt f", p=128), RPs[j])
                Q.dma(ropS[j][:], ropeS[tl:tl+W, :].rearrange("(tt p) f -> p tt f", p=128), RPs[j])
        @blk.gpsimd
        def _(g):
            G = b.G
            G.dma(Wq_sb[:], Wq1.rearrange("(c p) f -> p c f", p=128), WTs)
            G.dma(Wqr_sb[:], Wqr.rearrange("(c p) f -> p c f", p=128), WTs)
            G.dma(Wkr_sb[:], Wkr.rearrange("(c p) f -> p c f", p=128), WTs)
            G.dma(Wkvd_sb[:], Wkvd.rearrange("(c p) f -> p c f", p=128), WTs)
            G.dma(Wku_sb[:], Wku.rearrange("(c p) f -> p c f", p=128), WTs)
            G.dma(Wvu_sb[:], Wvu.rearrange("(c p) f -> p c f", p=128), WTs)
            G.dma(id_sb[:], id128, WTs)
            run_chain(b, steps, G)
            for j in range(2):
                for s_ in (SQ[j], SK[j], SV[j]):
                    if s_.n:
                        G.wait_stream(s_, s_.n)
        @blk.tensor
        def _(t):
            run_chain(b, steps, b.Tn)
        @blk.scalar
        def _(s):
            run_chain(b, steps, b.S)
        @blk.vector
        def _(v):
            run_chain(b, steps, b.V)
    return nc


def build_launch_H():
    return build_twice(_launch_H)



def _launch_I(b):
    nc = _mknc()
    q96 = _dram(nc, "q96", [B, 16, 96, T], BF16, "ExternalInput")
    k96 = _dram(nc, "k96", [B, 16, 96, T], BF16, "ExternalInput")
    vaug = _dram(nc, "vaug", [B, T, 1040], BF16, "ExternalInput")
    lobo = _dram(nc, "lobo", [1, 16], F32, "ExternalInput")
    ones1 = _dram(nc, "ones1", [1, 128], BF16, "ExternalInput")
    id128 = _dram(nc, "id128", [128, 128], BF16, "ExternalInput")
    maskdiag = _dram(nc, "maskdiag", [4, 128, 512], BF16, "ExternalInput")
    attn = _dram(nc, "attn", [B, 1024, T], BF16, "ExternalOutput")
    mscr = _dram(nc, "mscr", [64, T], F32, "Internal")
    SCALE = float(1.0 / np.sqrt(96.0))
    with ExitStack() as stk:
        e = stk.enter_context
        b.make(nc, stk)
        qt = [e(_sbuf(nc, f"qt{j}", [96, T], BF16)) for j in range(2)]
        kt_ = [e(_sbuf(nc, f"kt{j}", [96, T], BF16)) for j in range(2)]
        vt_ = [e(_sbuf(nc, f"vt{j}", [128, 16, 65], BF16)) for j in range(2)]
        emax = e(_sbuf(nc, "emax", [128, 16, 128], BF16))
        e_sb = e(_sbuf(nc, "e_sb", [128, 512], BF16))
        emc = e(_sbuf(nc, "emc", [128, 16], F32))
        mrow = e(_sbuf(nc, "mrow", [1, 2048], F32))
        drow = e(_sbuf(nc, "drow", [1, 2048], F32))
        rrow = e(_sbuf(nc, "rrow", [1, 2048], BF16))
        u_sb = e(_sbuf(nc, "u_sb", [64, 512], F32))
        ao = [e(_sbuf(nc, f"ao{j}", [64, 4, 512], BF16)) for j in range(2)]
        Ch = e(_sbuf(nc, "Ch", [1, 16], F32))
        ones1_sb = e(_sbuf(nc, "ones1_sb", [1, 128], BF16))
        id_sb = e(_sbuf(nc, "id_sb", [128, 128], BF16))
        md_sb = e(_sbuf(nc, "md_sb", [128, 4, 512], BF16))
        psSc = e(_psum(nc, "psSc", [128, 512], F32))
        psO = [e(_psum(nc, f"psO{qb}", [65, 512], F32)) for qb in range(4)]
        psT = e(_psum(nc, "psT", [128, 128], BF16))
        psBc = e(_psum(nc, "psBc", [128, 512], F32))
        LDs = [b.stream(nc, stk, f"LD{j}") for j in range(2)]
        WTs = b.stream(nc, stk, "WT")
        MWs = b.stream(nc, stk, "MW")
        MRs = b.stream(nc, stk, "MR")
        AOs = [b.stream(nc, stk, f"AO{j}") for j in range(2)]

        steps = []
        def add(en, fn):
            steps.append((en, fn))
        def chstart(S):
            S.activation(Ch[:], Ch[:], AF.Exp)
        def chld(G):
            G.dma(Ch[:], lobo, WTs)
            G.dma(ones1_sb[:], ones1, WTs)
            G.dma(id_sb[:], id128, WTs)
            G.dma(md_sb[:], maskdiag.rearrange("m p t -> p m t"), WTs)
        add("G", chld)
        def chexp(S):
            S.wait_stream(WTs, 4)
            S.activation(Ch[:], Ch[:], AF.Exp)
        add("S", chexp)

        bh = -1
        for bb_ in range(B):
            for h in range(16):
                bh += 1
                j = bh % 2
                for tkc in range(16):
                    qb0 = tkc // 4
                    for qb in range(qb0, 4):
                        def scmm(Tn, j=j, tkc=tkc, qb=qb, bh=bh):
                            if tkc == 0 and qb == 0:
                                Tn.wait_stream(LDs[j], 3 * (bh // 2 + 1))
                            Tn.inc(Tn.raw("matmul", psSc[:], kt_[j][:, tkc*128:(tkc+1)*128],
                                          qt[j][:, qb*512:(qb+1)*512], start=True, stop=True))
                        add("Tn", scmm)
                        def scexp(S, tkc=tkc, qb=qb):
                            S.activation(e_sb[:], psSc[:], AF.Exp, scale=SCALE)
                        add("S", scexp)
                        if qb == qb0:
                            def scmask(V, tkc=tkc):
                                V.tensor_tensor(e_sb[:], e_sb[:], md_sb[:, tkc % 4, :], op=ALU.mult)
                            add("V", scmask)
                        def emaxup(V, tkc=tkc, qb=qb):
                            dst = emax[:, qb*4:(qb+1)*4, :]
                            if tkc == 0:
                                V.tensor_copy(dst, e_sb[:].rearrange("p (c t) -> p c t", c=4))
                            else:
                                V.tensor_max(dst, dst, e_sb[:].rearrange("p (c t) -> p c t", c=4))
                        add("V", emaxup)
                        def pvmm(Tn, j=j, tkc=tkc, qb=qb):
                            Tn.inc(Tn.raw("matmul", psO[qb][:], vt_[j][:, tkc, :], e_sb[:],
                                          start=(tkc == 0), stop=(tkc == qb*4 + 3)))
                        add("Tn", pvmm)
                for tc in range(16):
                    def mtp(Tn, tc=tc):
                        Tn.inc(Tn.raw("transpose", psT[:], emax[:, tc, :], id_sb[:]))
                    add("Tn", mtp)
                    def mred(V, tc=tc):
                        V.reduce_max(emc[:, tc:tc+1], psT[:], axis=AX.X)
                    add("V", mred)
                def mwrite(G, bh=bh):
                    for tc in range(16):
                        G.dma(mscr[bh, tc*128:(tc+1)*128], emc[:, tc:tc+1], MWs)
                add("G", mwrite)
                def mread(G, bh=bh):
                    G.wait_stream(MWs, 16 * (bh + 1))
                    G.dma(mrow[:], mscr[bh:bh+1, :], MRs)
                add("G", mread)
                def dcol(V, bh=bh, h=h):
                    V.wait_stream(MRs, bh + 1)
                    for qb in range(4):
                        V.tensor_copy(drow[:, qb*512:(qb+1)*512], psO[qb][64:65, :])
                    V.tensor_scalar(mrow[:], mrow[:], Ch[:, h:h+1], None, op0=ALU.mult)
                    V.tensor_tensor(drow[:], drow[:], mrow[:], op=ALU.add)
                    V.reciprocal(drow[:], drow[:])
                    V.tensor_copy(rrow[:], drow[:])
                add("V", dcol)
                for qb in range(4):
                    def rbc(Tn, qb=qb):
                        Tn.inc(Tn.raw("matmul", psBc[:], ones1_sb[:], rrow[:, qb*512:(qb+1)*512],
                                      start=True, stop=True))
                    add("Tn", rbc)
                    def uev(S, qb=qb):
                        S.copy(u_sb[:], psO[qb][0:64, :])
                    add("S", uev)
                    def oak(V, qb=qb, bh=bh, j=j):
                        if qb == 0:
                            V.wait_stream(AOs[j], 4 * (bh // 2))
                        V.tensor_tensor(ao[j][:, qb, :], u_sb[:], psBc[0:64, :], op=ALU.mult)
                    add("V", oak)
                def astore(G, bh=bh, j=j, bb_=bb_, h=h):
                    for qb in range(4):
                        G.dma(attn[bb_, h*64:(h+1)*64, qb*512:(qb+1)*512], ao[j][:, qb, :], AOs[j])
                    G.inc(G.raw("engine_nop"))
                    G.mark(("bhdone", bh))
                add("G", astore)

        blk = e(nc.Block())
        @blk.sync
        def _(q_):
            Q = b.Q
            bh = -1
            for bb_ in range(B):
                for h in range(16):
                    bh += 1
                    j = bh % 2
                    if bh >= 2:
                        Q.wait(("bhdone", bh - 2))
                    Q.dma(qt[j][:], q96[bb_, h], LDs[j])
                    Q.dma(kt_[j][:], k96[bb_, h], LDs[j])
                    Q.dma(vt_[j][:], vaug[bb_][:, h*65:(h+1)*65].rearrange("(c p) f -> p c f", p=128), LDs[j])
        @blk.gpsimd
        def _(g):
            run_chain(b, steps, b.G)
            for j in range(2):
                if AOs[j].n:
                    b.G.wait_stream(AOs[j], AOs[j].n)
        @blk.tensor
        def _(t):
            run_chain(b, steps, b.Tn)
        @blk.scalar
        def _(s):
            run_chain(b, steps, b.S)
        @blk.vector
        def _(v):
            run_chain(b, steps, b.V)
    return nc


def build_launch_I():
    return build_twice(_launch_I)



def _launch_J(b):
    nc = _mknc()
    attn = _dram(nc, "attn", [B, 1024, T], BF16, "ExternalInput")
    Wo1 = _dram(nc, "Wo1", [1024, 1024], F32, "ExternalInput")
    s1 = _dram(nc, "s1_fm", [D, NT], F32, "ExternalOutput")
    W = 512
    NTILE = NT // W
    with ExitStack() as stk:
        e = stk.enter_context
        b.make(nc, stk)
        at = [e(_sbuf(nc, f"at{j}", [128, 8, W], BF16)) for j in range(2)]
        st_ = [e(_sbuf(nc, f"st{j}", [128, 8, W], F32)) for j in range(2)]
        Wo_sb = e(_sbuf(nc, "Wo_sb", [128, 8, 1024], BF16))
        psA = e(_psum(nc, "psA", [128, W], F32))
        psB = e(_psum(nc, "psB", [128, W], F32))
        LDs = [b.stream(nc, stk, f"LD{j}") for j in range(2)]
        WTs = b.stream(nc, stk, "WT")
        SSs = [b.stream(nc, stk, f"SS{j}") for j in range(2)]
        steps = []
        def add(en, fn):
            steps.append((en, fn))
        for i in range(NTILE):
            j = i % 2
            for fc in range(8):
                def mmst(Tn, i=i, j=j, fc=fc):
                    if fc == 0:
                        Tn.wait_stream(WTs, 1)
                        Tn.wait_stream(LDs[j], i // 2 + 1)
                    ps = psA if fc % 2 == 0 else psB
                    for hv in range(8):
                        ins = Tn.raw("matmul", ps[:], Wo_sb[:, hv, fc*128:(fc+1)*128],
                                     at[j][:, hv, :], start=(hv == 0), stop=(hv == 7))
                    Tn.inc(ins)
                    if fc == 7:
                        Tn.mark(("atfree", i))
                add("Tn", mmst)
                def evst(S, i=i, j=j, fc=fc):
                    if fc == 0:
                        S.wait_stream(SSs[j], i // 2)
                    ps = psA if fc % 2 == 0 else psB
                    S.copy(st_[j][:, fc, :], ps[:])
                add("S", evst)
            def sstore(G, i=i, j=j):
                G.dma(fm_slice(s1, i*W, W), st_[j][:], SSs[j])
            add("G", sstore)
        blk = e(nc.Block())
        @blk.sync
        def _(q_):
            Q = b.Q
            for i in range(NTILE):
                j = i % 2
                bb = i // 4
                tl = (i % 4) * W
                if i >= 2:
                    Q.wait(("atfree", i - 2))
                Q.dma(at[j][:], attn[bb][:, tl:tl+W].rearrange("(c p) t -> p c t", p=128), LDs[j])
        @blk.gpsimd
        def _(g):
            G = b.G
            G.dma(Wo_sb[:], Wo1.rearrange("(c p) f -> p c f", p=128), WTs)
            run_chain(b, steps, G)
            for j in range(2):
                if SSs[j].n:
                    G.wait_stream(SSs[j], SSs[j].n)
        @blk.tensor
        def _(t):
            run_chain(b, steps, b.Tn)
        @blk.scalar
        def _(s):
            run_chain(b, steps, b.S)
        @blk.vector
        def _(v):
            pass
    return nc


def build_launch_J():
    return build_twice(_launch_J)



def _launch_K(b):
    nc = _mknc()
    xn1 = _dram(nc, "xn1_b", [D, NT], BF16, "ExternalInput")
    Wfr = _dram(nc, "Wfr", [D, 1024], F32, "ExternalInput")
    Wfi = _dram(nc, "Wfi", [D, 1024], F32, "ExternalInput")
    bias = _dram(nc, "bias", [128, 8], F32, "ExternalInput")
    dFr = _dram(nc, "dftFr", [D, 1024], BF16, "ExternalInput")
    dFi = _dram(nc, "dftFi", [D, 1024], BF16, "ExternalInput")
    dFiN = _dram(nc, "dftFiNeg", [D, 1024], BF16, "ExternalInput")
    dGr = _dram(nc, "dftFrN", [D, 1024], BF16, "ExternalInput")
    dGi = _dram(nc, "dftFiN", [D, 1024], BF16, "ExternalInput")
    ffn1 = _dram(nc, "ffn1_fm", [D, NT], F32, "ExternalOutput")
    W = 512
    NTILE = NT // W
    with ExitStack() as stk:
        e = stk.enter_context
        b.make(nc, stk)
        Wr_sb = e(_sbuf(nc, "Wr_sb", [128, 8, 1024], BF16))
        Wi_sb = e(_sbuf(nc, "Wi_sb", [128, 8, 1024], BF16))
        Fr_sb = e(_sbuf(nc, "Fr_sb", [128, 8, 1024], BF16))
        Fi_sb = e(_sbuf(nc, "Fi_sb", [128, 8, 1024], BF16))
        FiN_sb = e(_sbuf(nc, "FiN_sb", [128, 8, 1024], BF16))
        T1r = e(_sbuf(nc, "T1r", [128, 8, 1024], BF16))
        T1i = e(_sbuf(nc, "T1i", [128, 8, 1024], BF16))
        M_sb = e(_sbuf(nc, "M_sb", [128, 8, 1024], BF16))
        xnb = [e(_sbuf(nc, f"xnb{j}", [128, 8, W], BF16)) for j in range(2)]
        ft = [e(_sbuf(nc, f"ft{j}", [128, 8, W], F32)) for j in range(2)]
        bias_sb = e(_sbuf(nc, "bias_sb", [128, 8], F32))
        psA = e(_psum(nc, "psA", [128, W], F32))
        psB = e(_psum(nc, "psB", [128, W], F32))
        LDs = [b.stream(nc, stk, f"LD{j}") for j in range(2)]
        WTs = b.stream(nc, stk, "WT")
        SFs = [b.stream(nc, stk, f"SF{j}") for j in range(2)]
        steps = []
        def add(en, fn):
            steps.append((en, fn))
        for (dst, t1, t2) in ((T1r, (Wr_sb, Fr_sb), (Wi_sb, FiN_sb)),
                              (T1i, (Wr_sb, Fi_sb), (Wi_sb, Fr_sb))):
            for oc in range(8):
                for nb2 in range(2):
                    def t1mm(Tn, dst=dst, t1=t1, t2=t2, oc=oc, nb2=nb2):
                        if oc == 0 and nb2 == 0 and dst is T1r:
                            Tn.wait_stream(WTs, 6)
                        ps = psA if nb2 == 0 else psB
                        for dc in range(8):
                            Tn.raw("matmul", ps[:], t1[0][:, dc, oc*128:(oc+1)*128],
                                   t1[1][:, dc, nb2*512:(nb2+1)*512], start=(dc == 0), stop=False)
                        for dc in range(8):
                            ins = Tn.raw("matmul", ps[:], t2[0][:, dc, oc*128:(oc+1)*128],
                                         t2[1][:, dc, nb2*512:(nb2+1)*512], start=False, stop=(dc == 7))
                        Tn.inc(ins)
                    add("Tn", t1mm)
                    def t1ev(S, dst=dst, oc=oc, nb2=nb2):
                        ps = psA if nb2 == 0 else psB
                        S.copy(dst[:, oc, nb2*512:(nb2+1)*512], ps[:])
                    add("S", t1ev)
        def reloadG(G):
            G.dma(Fr_sb[:], dGr.rearrange("(c p) f -> p c f", p=128), WTs)
            G.dma(Fi_sb[:], dGi.rearrange("(c p) f -> p c f", p=128), WTs)
        add("G", reloadG)
        for ncc in range(8):
            for nb2 in range(2):
                def mmm(Tn, ncc=ncc, nb2=nb2):
                    if ncc == 0 and nb2 == 0:
                        Tn.wait_stream(WTs, 8)
                    ps = psA if nb2 == 0 else psB
                    for oc in range(8):
                        Tn.raw("matmul", ps[:], T1r[:, oc, ncc*128:(ncc+1)*128],
                               dGr_sb[:, oc, nb2*512:(nb2+1)*512], start=(oc == 0), stop=False)
                    for oc in range(8):
                        ins = Tn.raw("matmul", ps[:], T1i[:, oc, ncc*128:(ncc+1)*128],
                                     dGi_sb[:, oc, nb2*512:(nb2+1)*512], start=False, stop=(oc == 7))
                    Tn.inc(ins)
                add("Tn", mmm)
                def mev(S, ncc=ncc, nb2=nb2):
                    S.copy(M_sb[:, ncc, nb2*512:(nb2+1)*512], ps_sel(nb2))
                add("S", mev)
        for i in range(NTILE):
            j = i % 2
            for fc in range(8):
                def fmm(Tn, i=i, j=j, fc=fc):
                    if fc == 0:
                        Tn.wait_stream(LDs[j], i // 2 + 1)
                    ps = psA if fc % 2 == 0 else psB
                    for dc in range(8):
                        ins = Tn.raw("matmul", ps[:], M_sb[:, dc, fc*128:(fc+1)*128],
                                     xnb[j][:, dc, :], start=(dc == 0), stop=(dc == 7))
                    Tn.inc(ins)
                    if fc == 7:
                        Tn.mark(("xnfree", i))
                add("Tn", fmm)
                def fev(V, i=i, j=j, fc=fc):
                    if fc == 0:
                        V.wait_stream(SFs[j], i // 2)
                    ps = psA if fc % 2 == 0 else psB
                    V.tensor_scalar(ft[j][:, fc, :], ps[:], bias_sb[:, fc:fc+1], None, op0=ALU.add)
                add("V", fev)
            def fstore(G, i=i, j=j):
                G.dma(fm_slice(ffn1, i*W, W), ft[j][:], SFs[j])
            add("G", fstore)

        dGr_sb = Fr_sb
        dGi_sb = Fi_sb
        def ps_sel(nb2):
            return (psA if nb2 == 0 else psB)[:]

        blk = e(nc.Block())
        @blk.sync
        def _(q_):
            Q = b.Q
            for i in range(NTILE):
                j = i % 2
                if i >= 2:
                    Q.wait(("xnfree", i - 2))
                Q.dma(xnb[j][:], fm_slice(xn1, i*W, W), LDs[j])
        @blk.gpsimd
        def _(g):
            G = b.G
            G.dma(Wr_sb[:], Wfr.rearrange("(c p) f -> p c f", p=128), WTs)
            G.dma(Wi_sb[:], Wfi.rearrange("(c p) f -> p c f", p=128), WTs)
            G.dma(Fr_sb[:], dFr.rearrange("(c p) f -> p c f", p=128), WTs)
            G.dma(Fi_sb[:], dFi.rearrange("(c p) f -> p c f", p=128), WTs)
            G.dma(FiN_sb[:], dFiN.rearrange("(c p) f -> p c f", p=128), WTs)
            G.dma(bias_sb[:], bias, WTs)
            run_chain(b, steps, G)
            for j in range(2):
                if SFs[j].n:
                    G.wait_stream(SFs[j], SFs[j].n)
        @blk.tensor
        def _(t):
            run_chain(b, steps, b.Tn)
        @blk.scalar
        def _(s):
            run_chain(b, steps, b.S)
        @blk.vector
        def _(v):
            run_chain(b, steps, b.V)
    return nc


def build_launch_K():
    return build_twice(_launch_K)



def _launch_L(b):
    nc = _mknc()
    x1 = _dram(nc, "x1_f", [D, NT], F32, "ExternalInput")
    s1 = _dram(nc, "s1_fm", [D, NT], F32, "ExternalInput")
    ffn1 = _dram(nc, "ffn1_fm", [D, NT], F32, "ExternalInput")
    out = _dram(nc, "out_fm", [D, NT], F32, "ExternalOutput")
    W = 512
    NTILE = NT // W
    with ExitStack() as stk:
        e = stk.enter_context
        b.make(nc, stk)
        xt = [e(_sbuf(nc, f"xt{j}", [128, 8, W], F32)) for j in range(2)]
        st_ = [e(_sbuf(nc, f"st{j}", [128, 8, W], F32)) for j in range(2)]
        mt = [e(_sbuf(nc, f"mt{j}", [128, 8, W], F32)) for j in range(2)]
        LXs = [b.stream(nc, stk, f"LX{j}") for j in range(2)]
        LSs = [b.stream(nc, stk, f"LS{j}") for j in range(2)]
        LMs = [b.stream(nc, stk, f"LM{j}") for j in range(2)]
        SOs = [b.stream(nc, stk, f"SO{j}") for j in range(2)]
        steps = []
        def add(en, fn):
            steps.append((en, fn))
        for i in range(NTILE):
            j = i % 2
            def gsilu(S, i=i, j=j):
                S.wait_stream(LMs[j], i // 2 + 1)
                S.activation(mt[j][:].rearrange("p c t -> p (c t)"),
                             mt[j][:].rearrange("p c t -> p (c t)"), AF.Silu)
            add("S", gsilu)
            def comb(V, i=i, j=j):
                V.wait_stream(LXs[j], i // 2 + 1)
                V.wait_stream(LSs[j], i // 2 + 1)
                if i >= 2:
                    V.wait_stream(SOs[j], i // 2)
                V.tensor_tensor(mt[j][:], st_[j][:], mt[j][:], op=ALU.mult)
                V.tensor_tensor(xt[j][:], xt[j][:], mt[j][:], op=ALU.add)
                V.mark(("odone", i))
            add("V", comb)
            def ostore(G, i=i, j=j):
                G.dma(fm_slice(out, i*W, W), xt[j][:], SOs[j])
            add("G", ostore)
        blk = e(nc.Block())
        @blk.sync
        def _(q_):
            Q = b.Q
            for i in range(NTILE):
                j = i % 2
                if i >= 2:
                    Q.wait(("odone", i - 2))
                    Q.wait_stream(SOs[j], i // 2)
                Q.dma(xt[j][:], fm_slice(x1, i*W, W), LXs[j])
                Q.dma(st_[j][:], fm_slice(s1, i*W, W), LSs[j])
                Q.dma(mt[j][:], fm_slice(ffn1, i*W, W), LMs[j])
        @blk.gpsimd
        def _(g):
            G = b.G
            run_chain(b, steps, G)
            for j in range(2):
                if SOs[j].n:
                    G.wait_stream(SOs[j], SOs[j].n)
        @blk.scalar
        def _(s):
            run_chain(b, steps, b.S)
        @blk.vector
        def _(v):
            run_chain(b, steps, b.V)
        @blk.tensor
        def _(t):
            pass
    return nc


def build_launch_L():
    return build_twice(_launch_L)



def _launch_ALL(b):
    global _FUSED_NC, _DRAM_CACHE
    nc = bass.Bass("TRN2", target_bir_lowering=False)
    _FUSED_NC = nc
    _DRAM_CACHE = {}
    try:
        for nm, fn in (("A", _launch_A), ("B", _launch_B), ("C", _launch_C),
                       ("D", _launch_D), ("E", _launch_E), ("F", _launch_F),
                       ("G", _launch_G), ("H", _launch_H), ("I", _launch_I),
                       ("J", _launch_J), ("K", _launch_K), ("L", _launch_L)):
            _PHASE[0] = nm + "_"
            b.prefix = nm
            fn(b)
            _barrier(nc, b)
    finally:
        _FUSED_NC = None
        _PHASE[0] = ""
        b.prefix = ""
    return nc


def build_launch_ALL():
    return build_twice(_launch_ALL)



def shard_inputs(inputs):
    c = host_consts()
    per_core = []
    x = np.asarray(inputs["L0_attn_x"])
    for core in range(NCORES):
        xs = x[4*core:4*core+4]
        x_fm = np.ascontiguousarray(xs.reshape(NT, D).T)
        per_core.append({"x_fm": x_fm})
    return per_core


def chunk_vec(v):
    return np.ascontiguousarray(np.asarray(v, np.float32).reshape(-1, 128).T)


def _bf(a):
    return np.asarray(a).astype(BF) if a.dtype != BF else a


def kernel(**inputs):
    c = host_consts()
    f32 = lambda k: np.ascontiguousarray(np.asarray(inputs[k], np.float32))
    shared = {
        "A": {"lnw": chunk_vec(inputs["L0_attn_ln_w"]), "lnb": chunk_vec(inputs["L0_attn_ln_b"]),
              "onesb": c["onesb"], "ones1": c["ones1"]},
        "B": {"Wq": f32("L0_attn_Wq"), "Wk": f32("L0_attn_Wk"), "Wv": f32("L0_attn_Wv"),
              "Wb": f32("L0_attn_Wb"), "onesb": c["onesb"], "ones1": c["ones1"], "id128": c["id128"]},
        "C": {"onw": chunk_vec(inputs["L0_attn_o_norm_w"]), "onesb": c["onesb"], "ones1": c["ones1"],
              "id128": c["id128"], "id128f": c["id128f"], "maskLd": c["maskLd"],
              "maskLoff": c["maskLoff"], "maskWuf": c["maskWuf"], "bdmaskf": c["bdmaskf"]},
        "D": {"Wo": f32("L0_attn_Wo")},
        "E": {"Wr": f32("L0_ffn_Wr"), "id128": c["id128"]},
        "F": {"Wg": f32("L0_ffn_Wg"), "Wu": f32("L0_ffn_Wu"), "Wd": f32("L0_ffn_Wd"),
              "Sg": f32("L0_ffn_Sg"), "Su": f32("L0_ffn_Su"), "Sd": f32("L0_ffn_Sd"),
              "sel16": c["sel16"]},
        "G": {"onesb": c["onesb"], "ones1": c["ones1"]},
        "H": {"Wq1": f32("L1_attn_Wq"), "Wqr": f32("L1_attn_Wqr"), "Wkr": f32("L1_attn_Wkr"),
              "Wkvd": f32("L1_attn_Wkvd"), "Wku": f32("L1_attn_Wku"), "Wvu": f32("L1_attn_Wvu"),
              "ropeC": c["ropeC"], "ropeS": c["ropeS"], "id128": c["id128"]},
        "I": {"lobo": np.ascontiguousarray(np.asarray(inputs["L1_attn_lobo_log"], np.float32).reshape(1, 16)),
              "ones1": c["ones1"], "id128": c["id128"], "maskdiag": c["maskdiag"]},
        "J": {"Wo1": f32("L1_attn_Wo")},
        "K": {"Wfr": f32("L1_ffn_w_freq_real"), "Wfi": f32("L1_ffn_w_freq_imag"),
              "bias": chunk_vec(inputs["L1_ffn_bias"]),
              "dftFr": c["dftFr"], "dftFi": c["dftFi"], "dftFiNeg": c["dftFiNeg"],
              "dftFrN": c["dftFrN"], "dftFiN": c["dftFiN"]},
        "L": {},
    }
    x = np.asarray(inputs["L0_attn_x"], np.float32)
    state = []
    for core in range(NCORES):
        xs = x[4*core:4*core+4]
        state.append({"x_fm": np.ascontiguousarray(xs.reshape(NT, D).T)})

    merged = {}
    for d_ in shared.values():
        merged.update(d_)
    nc = build_launch_ALL()
    in_maps = [dict(merged, x_fm=state[core]["x_fm"]) for core in range(NCORES)]
    res = run_bass_kernel_spmd(nc, in_maps, core_ids=list(range(NCORES)))
    for core in range(NCORES):
        state[core]["out_fm"] = np.asarray(res.results[core]["out_fm"])

    out = np.empty((32, T, D), np.float32)
    for core in range(NCORES):
        out[4*core:4*core+4] = state[core]["out_fm"].T.reshape(4, T, D)
    return out
